# version 03
# baseline (speedup 1.0000x reference)
import contextlib
import numpy as np
import ml_dtypes
import concourse.bass as bass
import concourse.mybir as mybir
from concourse.bass_utils import run_bass_kernel_spmd

F32 = mybir.dt.float32
BF16 = mybir.dt.bfloat16
AF = mybir.ActivationFunctionType
ALU = mybir.AluOpType
AX = mybir.AxisListType

T = 2048
D = 1024
NT = T // 128
NC = D // 128
H = 16
NEG = -1.0e9
LN_EPS = 1e-5
bfd = ml_dtypes.bfloat16
FVALS = [-0.5, 0.125, 1e-5, -1.0, 1.0]
FIDX = {v: i for i, v in enumerate(FVALS)}


class Sem:
    def __init__(self, nc, es, name):
        self.h = es.enter_context(nc.semaphore(name))
        self.n = 0

    def inc(self, instr, k=1):
        instr.then_inc(self.h, k)
        self.n += k
        return self.n


def on(blk, eng):
    def deco(fn):
        getattr(blk, eng)(fn)
    return deco


def host_prep(inputs):
    I = inputs
    w = {}
    ct = lambda a: np.ascontiguousarray(np.asarray(a, dtype=np.float32).T)
    cc = lambda a: np.ascontiguousarray(np.asarray(a), dtype=np.float32)
    w["l0_wqkvT"] = ct(I["L0_attn_in_proj_w"])
    w["l0_bqkv"] = cc(I["L0_attn_in_proj_b"])
    w["l0_woT"] = ct(I["L0_attn_out_proj_w"])
    w["l0_bo"] = cc(I["L0_attn_out_proj_b"])
    w["l0_basewT"] = ct(I["L0_ffn_base_w"])
    w["l0_baseb"] = cc(I["L0_ffn_base_b"])
    w["l0_loraA"] = cc(I["L0_ffn_lora_A"])
    w["l0_loraB"] = cc(np.reshape(np.asarray(I["L0_ffn_lora_B"]), (64, 1024)))
    w["l0_routerT"] = ct(I["L0_ffn_router_w"])
    w["l0_routerb"] = cc(I["L0_ffn_router_b"])
    for k in ("WQ", "WK", "WV", "WO"):
        w[f"l1_{k}T"] = ct(I[f"L1_attn_{k}"])
    for k in ("bQ", "bK", "bV", "bO"):
        w[f"l1_{k}"] = cc(I[f"L1_attn_{k}"])
    w["l1_tableT"] = ct(I["L1_attn_rel_pos_table"])
    w["l1_ffw"] = cc(I["L1_ffn_weight"])
    w["l1_ffa1"] = cc(I["L1_ffn_aux_weight1"])
    w["l1_ffa2"] = cc(I["L1_ffn_aux_weight2"])
    w["l1_ffb"] = cc(I["L1_ffn_bias"])
    for k in ("Wq", "Wk", "Wv", "Wo"):
        w[f"l2_{k}T"] = ct(I[f"L2_attn_{k}"])
    w["l2_w1T"] = ct(I["L2_ffn_w1_w"])
    w["l2_w2T"] = ct(I["L2_ffn_w2_w"])
    w["l2_h1T"] = ct(I["L2_ffn_h1_w"])
    w["l2_h2T"] = ct(I["L2_ffn_h2_w"])
    w["l2_w3T"] = ct(I["L2_ffn_w3_w"])
    for k in ("w1_s", "w2_s", "h1_s", "h2_s", "w3_s"):
        w[f"l2_{k}"] = cc(I[f"L2_ffn_{k}"])
    for k in ("Wq", "Wk", "Wv", "Wo", "bq", "bk", "bv", "bo"):
        w[f"l3_{k}"] = cc(I[f"L3_attn_{k}"])
    w["l3_routerT"] = ct(I["L3_ffn_w_router"])
    w["l3_gup"] = cc(I["L3_ffn_gate_up_proj"])
    w["l3_down"] = cc(I["L3_ffn_down_proj"])
    w["l3_wgT"] = ct(I["L3_ffn_w_gate"])
    w["l3_wuT"] = ct(I["L3_ffn_w_up"])
    w["l3_wdT"] = ct(I["L3_ffn_w_down"])
    w["identbf"] = np.eye(128, dtype=bfd)
    r = np.arange(128)
    w["tmaskbf"] = np.where(r[:, None] <= r[None, :], 0.0, NEG).astype(bfd)
    w["jexbf"] = np.ascontiguousarray(np.eye(128, dtype=np.float32)[:, ::-1]).astype(bfd)
    w["onesrowbf"] = np.ones((1, 128), dtype=bfd)
    w["onescolbf"] = np.ones((128, 1), dtype=bfd)
    w["onesrowf"] = np.ones((1, 128), dtype=np.float32)
    o64 = np.zeros((65, 128), dtype=np.float32)
    o64[64, :] = 1.0
    w["ones64f"] = o64
    w["neg1e9row"] = np.full((1, 1024), NEG, dtype=bfd)
    w["iota16"] = np.tile(np.arange(16, dtype=np.float32)[None, :], (128, 1))
    w["iota64m"] = np.tile(np.arange(16, dtype=np.float32)[None, :] - 64.0, (128, 1))
    w["eselbf"] = np.ascontiguousarray(
        np.eye(16, dtype=bfd)[:, :, None] * np.ones((1, 1, 128), dtype=bfd)
    ).reshape(16, 16 * 128).astype(bfd)
    w["fvals"] = np.tile(np.array(FVALS, dtype=np.float32)[None, :], (128, 1))
    x = np.asarray(inputs["L0_attn_x"], dtype=np.float32)
    xs = [np.ascontiguousarray(x[i]) for i in range(8)]
    return w, xs


CONSTS = [("c_ident", "identbf", [128, 128], BF16),
          ("c_tmask", "tmaskbf", [128, 128], BF16),
          ("c_jex", "jexbf", [128, 128], BF16),
          ("c_or", "onesrowbf", [1, 128], BF16),
          ("c_oc", "onescolbf", [128, 1], BF16),
          ("c_orf", "onesrowf", [1, 128], F32),
          ("c_o64", "ones64f", [65, 128], F32),
          ("c_n9", "neg1e9row", [1, 1024], BF16),
          ("c_iota", "iota16", [128, 16], F32),
          ("c_iota64m", "iota64m", [128, 16], F32),
          ("c_esel", "eselbf", [16, 16 * 128], BF16),
          ("c_fv", "fvals", [128, len(FVALS)], F32)]

LAYER_KEYS = {
    0: ["l0_wqkvT", "l0_bqkv", "l0_woT", "l0_bo", "l0_basewT", "l0_baseb",
        "l0_loraA", "l0_loraB", "l0_routerT", "l0_routerb"],
    1: ["l1_WQT", "l1_WKT", "l1_WVT", "l1_WOT", "l1_bQ", "l1_bK", "l1_bV",
        "l1_bO", "l1_tableT", "l1_ffw", "l1_ffa1", "l1_ffa2", "l1_ffb"],
    2: ["l2_WqT", "l2_WkT", "l2_WvT", "l2_WoT", "l2_w1T", "l2_w2T", "l2_h1T",
        "l2_h2T", "l2_w3T", "l2_w1_s", "l2_w2_s", "l2_h1_s", "l2_h2_s", "l2_w3_s"],
    3: ["l3_Wq", "l3_Wk", "l3_Wv", "l3_Wo", "l3_bq", "l3_bk", "l3_bv", "l3_bo",
        "l3_routerT", "l3_gup", "l3_down", "l3_wgT", "l3_wuT", "l3_wdT"],
}


CONST_NEED = {
    0: ["c_ident", "c_tmask", "c_or", "c_oc", "c_orf", "c_o64", "c_n9", "c_fv"],
    1: ["c_ident", "c_tmask", "c_jex", "c_or", "c_oc", "c_orf", "c_o64",
        "c_n9", "c_fv"],
    2: ["c_ident", "c_or", "c_oc", "c_orf", "c_o64", "c_n9", "c_fv"],
    3: ["c_ident", "c_or", "c_oc", "c_orf", "c_o64", "c_iota", "c_iota64m",
        "c_esel", "c_fv"],
}


class LayerProg:
    def __init__(self, W, wkeys, no_g=False, need=None):
        nc = bass.Bass("TRN2", target_bir_lowering=False)
        self.nc = nc
        _sb, _ps = nc.sbuf_tensor, nc.psum_tensor
        self._uid = [0]

        def _mk(f):
            def wrap(name, *a, **kw):
                self._uid[0] += 1
                return f(f"{name}_u{self._uid[0]}", *a, **kw)
            return wrap
        nc.sbuf_tensor = _mk(_sb)
        nc.psum_tensor = _mk(_ps)
        self.x = nc.dram_tensor("x", [T, D], F32, kind="ExternalInput").ap()
        self.y = nc.dram_tensor("y", [T, D], F32, kind="ExternalOutput").ap()
        self.win = {}
        for k in wkeys + [c[1] for c in CONSTS]:
            a = W[k]
            dt = BF16 if a.dtype == bfd else F32
            self.win[k] = nc.dram_tensor(k, list(a.shape), dt,
                                         kind="ExternalInput").ap()
        self.es = contextlib.ExitStack()
        self.es.__enter__()
        es = self.es
        self.blk = es.enter_context(nc.Block())
        self.xq = Sem(nc, es, "xq")
        self.wq = Sem(nc, es, "wq")
        self.pe = Sem(nc, es, "pe")
        self.dv = Sem(nc, es, "dv")
        self.ac = Sem(nc, es, "ac")
        self.sq = Sem(nc, es, "sq")
        self.sv = Sem(nc, es, "sv")
        self.se = Sem(nc, es, "se")
        self.yq = Sem(nc, es, "yq")
        self.xnf = es.enter_context(nc.sbuf_tensor("xnf", [128, NC * T], BF16))
        self.g = None if no_g else es.enter_context(
            nc.sbuf_tensor("g", [128, NT * D], F32))
        self.consts = {}
        for nm, k, sh, dt in CONSTS:
            if need is not None and nm not in need:
                setattr(self, nm, None)
                continue
            tile = es.enter_context(nc.sbuf_tensor(nm, sh, dt))
            self.consts[nm] = tile
            setattr(self, nm, tile)

    def ensure_consts(self, need, es):
        C = self
        missing = [c for c in CONSTS if c[0] in need and c[0] not in self.consts]
        if not missing:
            return
        for nm, k, sh, dt in missing:
            tile = es.enter_context(self.nc.sbuf_tensor(nm, sh, dt))
            self.consts[nm] = tile
            setattr(self, nm, tile)

        def _drop():
            for nm, k, sh, dt in missing:
                self.consts.pop(nm, None)
                setattr(self, nm, None)
        es.callback(_drop)

        @on(C.blk, 'sync')
        def _(q):
            for nm, k, sh, dt in missing:
                C.xq.inc(q.dma_start(out=C.consts[nm][:], in_=C.win[k]), 16)
        C.barrier()

    def fv(self, val):
        return self.c_fv[:, FIDX[val]:FIDX[val] + 1]

    def xnf3(self):
        return self.xnf[:].rearrange("p (c t) -> p c t", c=NC)

    def emit_const_loads(self):
        C = self

        @on(C.blk, 'sync')
        def _(q):
            for nm, k, sh, dt in CONSTS:
                if nm in C.consts:
                    C.xq.inc(q.dma_start(out=C.consts[nm][:], in_=C.win[k]), 16)
        C.barrier()

    def barrier(self, engs=("sync", "gpsimd", "vector", "scalar", "tensor")):
        C = self
        vals = [(s.h, s.n) for s in
                (C.xq, C.wq, C.pe, C.dv, C.ac, C.sq, C.sv, C.se, C.yq)
                if s.n > 0]
        for eng in engs:
            @on(C.blk, eng)
            def _(q):
                for h, n in vals:
                    q.wait_ge(h, n)

    def cast_load(self, q, dst_ap, src_ap):
        self.wq.inc(q.dma_start(out=dst_ap, in_=src_ap), 16)

    def close(self):
        self.es.__exit__(None, None, None)


def emit_ln_transpose(C, lo_dst=None):
    if lo_dst is not None:
        return emit_ln_transpose_lo(C, lo_dst)
    nc, blk = C.nc, C.blk
    es2 = contextlib.ExitStack()
    xfull = es2.enter_context(nc.sbuf_tensor("ln_xfull", [128, NT * 1024], F32))
    xntok = es2.enter_context(nc.sbuf_tensor("ln_xntok", [128, NT * 1024], BF16))
    scrF = es2.enter_context(nc.sbuf_tensor("ln_scr", [128, 1024], F32))
    st = es2.enter_context(nc.sbuf_tensor("ln_stats", [128, 6 * NT], F32))
    pts = [es2.enter_context(nc.psum_tensor(f"ln_pt{j}", [128, 1024], BF16))
           for j in range(2)]
    NS = NT
    ac0, dv0, xq0 = C.ac.n, C.dv.n, C.xq.n

    @on(blk, 'sync')
    def _(q):
        C.xq.inc(q.dma_start(
            out=xfull[:].rearrange("p (n d) -> p n d", d=1024),
            in_=C.x.rearrange("(n p) d -> p n d", p=128)), 16)

    @on(blk, 'scalar')
    def _(q):
        q.wait_ge(C.xq.h, xq0 + 16)
        for i in range(NT):
            xt = xfull[:, i * 1024:(i + 1) * 1024]
            C.ac.inc(q.activation(out=scrF[:], in_=xt, func=AF.Square,
                                  accum_out=st[:, 2 * NS + i:2 * NS + i + 1]))
            q.wait_ge(C.dv.h, dv0 + 5 * i + 4)
            q.activation(out=st[:, 3 * NS + i:3 * NS + i + 1],
                         in_=st[:, 3 * NS + i:3 * NS + i + 1],
                         func=AF.Ln, bias=C.fv(1e-5))
            C.ac.inc(q.activation(out=st[:, 4 * NS + i:4 * NS + i + 1],
                                  in_=st[:, 3 * NS + i:3 * NS + i + 1],
                                  func=AF.Exp, scale=C.fv(-0.5)))
            q.wait_ge(C.dv.h, dv0 + 5 * i + 5)
            C.ac.inc(q.activation(out=xntok[:, i * 1024:(i + 1) * 1024], in_=xt,
                                  func=AF.Identity,
                                  bias=st[:, 5 * NS + i:5 * NS + i + 1],
                                  scale=st[:, 4 * NS + i:4 * NS + i + 1]))

    @on(blk, 'vector')
    def _(q):
        q.wait_ge(C.xq.h, xq0 + 16)
        for i in range(NT):
            xt = xfull[:, i * 1024:(i + 1) * 1024]
            q.wait_ge(C.dv.h, C.dv.inc(
                q.reduce_sum(st[:, i:i + 1], xt, axis=AX.X, negate=True)))
            q.wait_ge(C.dv.h, C.dv.inc(
                q.tensor_scalar_mul(st[:, NS + i:NS + i + 1],
                                    st[:, i:i + 1], 1.0 / D)))
            q.wait_ge(C.ac.h, ac0 + 3 * i + 1)
            q.tensor_mul(st[:, i:i + 1], st[:, NS + i:NS + i + 1],
                         st[:, NS + i:NS + i + 1])
            q.wait_ge(C.dv.h, C.dv.inc(
                q.tensor_scalar_mul(st[:, 3 * NS + i:3 * NS + i + 1],
                                    st[:, 2 * NS + i:2 * NS + i + 1], 1.0 / D)))
            C.dv.inc(q.tensor_sub(st[:, 3 * NS + i:3 * NS + i + 1],
                                  st[:, 3 * NS + i:3 * NS + i + 1],
                                  st[:, i:i + 1]))
            q.wait_ge(C.ac.h, ac0 + 3 * i + 2)
            C.dv.inc(q.tensor_mul(st[:, 5 * NS + i:5 * NS + i + 1],
                                  st[:, NS + i:NS + i + 1],
                                  st[:, 4 * NS + i:4 * NS + i + 1]))

    sq0, sv0 = C.sq.n, C.sv.n

    @on(blk, 'tensor')
    def _(q):
        for i in range(NT):
            q.wait_ge(C.ac.h, ac0 + 3 * i + 3)
            if i >= 2:
                q.wait_ge(C.sv.h, sv0 + i - 1)
            pt = pts[i % 2]
            last = None
            for c in range(NC):
                last = q.transpose(
                    pt[:, c * 128:(c + 1) * 128],
                    xntok[:, i * 1024 + c * 128:i * 1024 + (c + 1) * 128],
                    C.c_ident[:])
            C.sq.inc(last)

    @on(blk, 'vector')
    def _(q):
        for i in range(NT):
            q.wait_ge(C.sq.h, sq0 + i + 1)
            C.sv.inc(q.tensor_copy(
                C.xnf3()[:, :, i * 128:(i + 1) * 128],
                pts[i % 2][:].rearrange("p (c t) -> p c t", c=NC)))

    C.barrier()
    es2.close()


def emit_ln_transpose_lo(C, lo_dst):
    nc, blk = C.nc, C.blk
    es2 = contextlib.ExitStack()
    xfull = es2.enter_context(nc.sbuf_tensor("ln_xfull", [128, NT * 1024], F32))
    xntok = es2.enter_context(nc.sbuf_tensor("ln_xntok", [128, NT * 1024], BF16))
    xlotok = es2.enter_context(nc.sbuf_tensor("ln_xlotok", [128, 1024], BF16))
    scrF = es2.enter_context(nc.sbuf_tensor("ln_scr", [128, 1024], F32))
    scr2 = es2.enter_context(nc.sbuf_tensor("ln_scr2", [128, 1024], F32))
    st = es2.enter_context(nc.sbuf_tensor("ln_stats", [128, 6 * NT], F32))
    pts = [es2.enter_context(nc.psum_tensor(f"ln_pt{j}", [128, 1024], BF16))
           for j in range(2)]
    NS = NT
    ac0, dv0, xq0 = C.ac.n, C.dv.n, C.xq.n
    sq0, sv0 = C.sq.n, C.sv.n

    @on(blk, 'sync')
    def _(q):
        C.xq.inc(q.dma_start(
            out=xfull[:].rearrange("p (n d) -> p n d", d=1024),
            in_=C.x.rearrange("(n p) d -> p n d", p=128)), 16)

    @on(blk, 'scalar')
    def _(q):
        q.wait_ge(C.xq.h, xq0 + 16)
        for i in range(NT):
            xt = xfull[:, i * 1024:(i + 1) * 1024]
            C.ac.inc(q.activation(out=scrF[:], in_=xt, func=AF.Square,
                                  accum_out=st[:, 2 * NS + i:2 * NS + i + 1]))
            q.wait_ge(C.dv.h, dv0 + 6 * i + 4)
            q.activation(out=st[:, 3 * NS + i:3 * NS + i + 1],
                         in_=st[:, 3 * NS + i:3 * NS + i + 1],
                         func=AF.Ln, bias=C.fv(1e-5))
            C.ac.inc(q.activation(out=st[:, 4 * NS + i:4 * NS + i + 1],
                                  in_=st[:, 3 * NS + i:3 * NS + i + 1],
                                  func=AF.Exp, scale=C.fv(-0.5)))
            q.wait_ge(C.dv.h, dv0 + 6 * i + 5)
            C.ac.inc(q.activation(out=xntok[:, i * 1024:(i + 1) * 1024], in_=xt,
                                  func=AF.Identity,
                                  bias=st[:, 5 * NS + i:5 * NS + i + 1],
                                  scale=st[:, 4 * NS + i:4 * NS + i + 1]))
            if i >= 1:
                q.wait_ge(C.dv.h, dv0 + 6 * i)
            C.ac.inc(q.activation(out=scr2[:], in_=xt, func=AF.Identity,
                                  bias=st[:, 5 * NS + i:5 * NS + i + 1],
                                  scale=st[:, 4 * NS + i:4 * NS + i + 1]))

    def emit_evac(q, j):
        q.wait_ge(C.sq.h, sq0 + 2 * j + 1)
        C.sv.inc(q.tensor_copy(
            C.xnf3()[:, :, j * 128:(j + 1) * 128],
            pts[0][:].rearrange("p (c t) -> p c t", c=NC)))
        q.wait_ge(C.sq.h, sq0 + 2 * j + 2)
        C.sv.inc(q.tensor_copy(
            lo_dst[:].rearrange("p (c t) -> p c t", c=NC)
            [:, :, j * 128:(j + 1) * 128],
            pts[1][:].rearrange("p (c t) -> p c t", c=NC)))

    @on(blk, 'vector')
    def _(q):
        q.wait_ge(C.xq.h, xq0 + 16)
        for i in range(NT):
            xt = xfull[:, i * 1024:(i + 1) * 1024]
            q.wait_ge(C.dv.h, C.dv.inc(
                q.reduce_sum(st[:, i:i + 1], xt, axis=AX.X, negate=True)))
            q.wait_ge(C.dv.h, C.dv.inc(
                q.tensor_scalar_mul(st[:, NS + i:NS + i + 1],
                                    st[:, i:i + 1], 1.0 / D)))
            q.wait_ge(C.ac.h, ac0 + 4 * i + 1)
            q.tensor_mul(st[:, i:i + 1], st[:, NS + i:NS + i + 1],
                         st[:, NS + i:NS + i + 1])
            q.wait_ge(C.dv.h, C.dv.inc(
                q.tensor_scalar_mul(st[:, 3 * NS + i:3 * NS + i + 1],
                                    st[:, 2 * NS + i:2 * NS + i + 1], 1.0 / D)))
            C.dv.inc(q.tensor_sub(st[:, 3 * NS + i:3 * NS + i + 1],
                                  st[:, 3 * NS + i:3 * NS + i + 1],
                                  st[:, i:i + 1]))
            q.wait_ge(C.ac.h, ac0 + 4 * i + 2)
            C.dv.inc(q.tensor_mul(st[:, 5 * NS + i:5 * NS + i + 1],
                                  st[:, NS + i:NS + i + 1],
                                  st[:, 4 * NS + i:4 * NS + i + 1]))
            if i >= 1:
                emit_evac(q, i - 1)
            q.wait_ge(C.ac.h, ac0 + 4 * i + 4)
            C.dv.inc(q.tensor_sub(xlotok[:], scr2[:],
                                  xntok[:, i * 1024:(i + 1) * 1024]))
        emit_evac(q, NT - 1)

    @on(blk, 'tensor')
    def _(q):
        for i in range(NT):
            q.wait_ge(C.ac.h, ac0 + 4 * i + 3)
            gidx = 2 * i
            if gidx >= 2:
                q.wait_ge(C.sv.h, sv0 + gidx - 1)
            pt = pts[0]
            last = None
            for c in range(NC):
                last = q.transpose(
                    pt[:, c * 128:(c + 1) * 128],
                    xntok[:, i * 1024 + c * 128:i * 1024 + (c + 1) * 128],
                    C.c_ident[:])
            C.sq.inc(last)
            q.wait_ge(C.dv.h, dv0 + 6 * i + 6)
            gidx = 2 * i + 1
            if gidx >= 2:
                q.wait_ge(C.sv.h, sv0 + gidx - 1)
            pt = pts[1]
            last = None
            for c in range(NC):
                last = q.transpose(
                    pt[:, c * 128:(c + 1) * 128],
                    xlotok[:, c * 128:(c + 1) * 128],
                    C.c_ident[:])
            C.sq.inc(last)

    C.barrier()
    es2.close()


def emit_l0_ffn(C):
    nc, blk = C.nc, C.blk
    es2 = contextlib.ExitStack()
    wbase = es2.enter_context(nc.sbuf_tensor("wbase", [128, NC * 1024], BF16))
    wrout = es2.enter_context(nc.sbuf_tensor("wrout", [128, NC * 8], BF16))
    wlA = es2.enter_context(nc.sbuf_tensor("wlA", [128, NC * 8], BF16))
    wlB2 = es2.enter_context(nc.sbuf_tensor("wlB2", [64, 1024], BF16))
    r_baseb = es2.enter_context(nc.sbuf_tensor("r_baseb", [1, 1024], BF16))
    r_routb = es2.enter_context(nc.sbuf_tensor("r_routb", [1, 8], BF16))
    gate = es2.enter_context(nc.sbuf_tensor("gate", [128, 16], BF16))
    tsb = es2.enter_context(nc.sbuf_tensor("tsb", [128, 64], BF16))
    tTall = es2.enter_context(nc.sbuf_tensor("tTall", [64, NT * 128], BF16))
    fst = es2.enter_context(nc.sbuf_tensor("fst", [128, 4], F32))
    gatef = es2.enter_context(nc.sbuf_tensor("gatef", [128, 8], F32))
    pff = [es2.enter_context(nc.psum_tensor(f"pff{i}", [128, 512], F32))
           for i in range(4)]
    pr8 = es2.enter_context(nc.psum_tensor("pr8", [128, 8], F32))
    pm8 = es2.enter_context(nc.psum_tensor("pm8", [128, 8], F32))
    ptT = es2.enter_context(nc.psum_tensor("ptT", [64, 128], BF16))

    @on(blk, 'gpsimd')
    def _(q):
        C.cast_load(q, wbase[:].rearrange("p (c o) -> p c o", c=NC),
                    C.win["l0_basewT"].rearrange("(c p) o -> p c o", p=128))
        C.cast_load(q, wrout[:].rearrange("p (c o) -> p c o", c=NC),
                    C.win["l0_routerT"].rearrange("(c p) o -> p c o", p=128))
        C.cast_load(q, wlA[:].rearrange("p (c o) -> p c o", c=NC),
                    C.win["l0_loraA"].rearrange("(c p) o -> p c o", p=128))
        C.cast_load(q, wlB2[:], C.win["l0_loraB"])
        C.cast_load(q, r_baseb[:], C.win["l0_baseb"].unsqueeze(0))
        C.cast_load(q, r_routb[:], C.win["l0_routerb"].unsqueeze(0))
    wq_ff = C.wq.n

    @on(blk, 'vector')
    def _(q):
        q.wait_ge(C.wq.h, wq_ff)
        C.dv.inc(q.tensor_scalar_mul(wlB2[:], wlB2[:], 2.0))
    dv_lb2 = C.dv.n

    pe0, dv0, ac0 = C.pe.n, C.dv.n, C.ac.n

    @on(blk, 'tensor')
    def _(q):
        q.wait_ge(C.wq.h, wq_ff)
        q.wait_ge(C.dv.h, dv_lb2)
        for i in range(NT):
            if i >= 1:
                q.wait_ge(C.ac.h, ac0 + i)
                q.wait_ge(C.dv.h, dv0 + 6 * (i - 1) + 6)
            for c in range(NC):
                lhs = C.xnf3()[:, c, i * 128:(i + 1) * 128]
                q.matmul(pr8[:], lhs, wrout[:, c * 8:(c + 1) * 8],
                         start=(c == 0), stop=False)
                q.matmul(pm8[:], lhs, wlA[:, c * 8:(c + 1) * 8],
                         start=(c == 0), stop=(c == NC - 1))
            C.pe.inc(q.matmul(pr8[:], C.c_or[:1, :], r_routb[:1, :],
                              start=False, stop=True))
            q.wait_ge(C.dv.h, dv0 + 6 * i + 5)
            C.pe.inc(q.transpose(ptT[:], tsb[:], C.c_ident[:]))

    @on(blk, 'vector')
    def _(q):
        for i in range(NT):
            q.wait_ge(C.pe.h, pe0 + 2 * i + 1)
            C.dv.inc(q.reduce_max(fst[:, 0:1], pr8[:], axis=AX.X, negate=True))
            q.wait_ge(C.ac.h, ac0 + i + 1)
            q.wait_ge(C.dv.h, C.dv.inc(q.reciprocal(fst[:, 2:3], fst[:, 1:2])))
            q.wait_ge(C.dv.h, C.dv.inc(
                q.tensor_scalar_mul(gatef[:], gatef[:], fst[:, 2:3])))
            q.tensor_copy(gate[:, 0:8], gatef[:])
            q.wait_ge(C.dv.h, C.dv.inc(q.tensor_copy(gate[:, 8:16], pm8[:])))
            u = gate[:, 0:8].unsqueeze(2).broadcast_to([128, 8, 8])
            v = gate[:, 8:16].unsqueeze(1).broadcast_to([128, 8, 8])
            C.dv.inc(q.tensor_tensor(tsb[:].rearrange("p (e r) -> p e r", e=8),
                                     u, v, ALU.mult))
            q.wait_ge(C.pe.h, pe0 + 2 * i + 2)
            C.dv.inc(q.tensor_copy(tTall[:, i * 128:(i + 1) * 128], ptT[:]))

    @on(blk, 'scalar')
    def _(q):
        for i in range(NT):
            q.wait_ge(C.dv.h, dv0 + 6 * i + 1)
            C.ac.inc(q.activation(out=gatef[:], in_=pr8[:], func=AF.Exp,
                                  bias=fst[:, 0:1], accum_out=fst[:, 1:2]))

    C.barrier(("tensor", "scalar"))

    pe0, ac0 = C.pe.n, C.ac.n

    @on(blk, 'tensor')
    def _(q):
        for i in range(NT):
            if i >= 2:
                q.wait_ge(C.ac.h, ac0 + (i - 1))
            pa, pb = pff[2 * (i % 2)], pff[2 * (i % 2) + 1]
            for c in range(NC):
                lhs = C.xnf3()[:, c, i * 128:(i + 1) * 128]
                stf = dict(start=(c == 0), stop=False)
                q.matmul(pa[:], lhs, wbase[:, c * 1024:c * 1024 + 512], **stf)
                q.matmul(pb[:], lhs, wbase[:, c * 1024 + 512:(c + 1) * 1024], **stf)
            tt = tTall[:, i * 128:(i + 1) * 128]
            q.matmul(pa[:], tt, wlB2[:, 0:512], start=False, stop=False)
            q.matmul(pb[:], tt, wlB2[:, 512:1024], start=False, stop=False)
            q.matmul(pa[:], C.c_or[:1, :], r_baseb[:1, 0:512], start=False, stop=True)
            C.pe.inc(q.matmul(pb[:], C.c_or[:1, :], r_baseb[:1, 512:1024],
                              start=False, stop=True))

    @on(blk, 'scalar')
    def _(q):
        for i in range(NT):
            q.wait_ge(C.pe.h, pe0 + i + 1)
            q.activation(out=C.g[:, i * 1024:i * 1024 + 512],
                         in_=pff[2 * (i % 2)][:], func=AF.Silu)
            C.ac.inc(q.activation(out=C.g[:, i * 1024 + 512:(i + 1) * 1024],
                                  in_=pff[2 * (i % 2) + 1][:], func=AF.Silu))

    C.barrier()
    es2.close()


def emit_l1_ffn(C):
    nc, blk = C.nc, C.blk
    es2 = contextlib.ExitStack()
    wbuf = es2.enter_context(nc.sbuf_tensor("l1w", [128, NC * 1024], BF16))
    h1 = es2.enter_context(nc.sbuf_tensor("l1h1", [128, NC * T], BF16))
    h2 = es2.enter_context(nc.sbuf_tensor("l1h2", [128, NC * T], BF16))
    r_b = es2.enter_context(nc.sbuf_tensor("l1rb", [1, 1024], BF16))
    tmp = es2.enter_context(nc.sbuf_tensor("l1tmp", [128, 512], F32))
    pf = [es2.enter_context(nc.psum_tensor(f"l1p{j}", [128, 512], F32))
          for j in range(4)]

    def load_w(k, first):
        pe_now = C.pe.n

        @on(blk, 'gpsimd')
        def _(q):
            if not first:
                q.wait_ge(C.pe.h, pe_now)
            C.cast_load(q, wbuf[:].rearrange("p (c o) -> p c o", c=NC),
                        C.win[k].rearrange("(c p) o -> p c o", p=128))
            if first:
                C.cast_load(q, r_b[:], C.win["l1_ffb"].unsqueeze(0))
        return C.wq.n

    def feat_stage(wsb, src3, dst, pe_base, dv_base, wq0):
        @on(blk, 'tensor')
        def _(q):
            q.wait_ge(C.wq.h, wq0)
            u = 0
            for dtile in range(NC):
                for tch in range(4):
                    if u >= 4:
                        q.wait_ge(C.dv.h, dv_base + u - 3)
                    last = None
                    for c in range(NC):
                        last = q.matmul(
                            pf[u % 4][:], wsb[:, c * 1024 + dtile * 128:
                                              c * 1024 + (dtile + 1) * 128],
                            src3[:, c, tch * 512:(tch + 1) * 512],
                            start=(c == 0), stop=(c == NC - 1))
                    C.pe.inc(last)
                    u += 1

        @on(blk, 'vector')
        def _(q):
            dst3 = dst[:].rearrange("p (c t) -> p c t", c=NC)
            u = 0
            for dtile in range(NC):
                for tch in range(4):
                    q.wait_ge(C.pe.h, pe_base + u + 1)
                    C.dv.inc(q.tensor_copy(
                        dst3[:, dtile, tch * 512:(tch + 1) * 512], pf[u % 4][:]))
                    u += 1

    wq0 = load_w("l1_ffw", True)
    feat_stage(wbuf, C.xnf3(), h1, C.pe.n, C.dv.n, wq0)
    wq0 = load_w("l1_ffa1", False)
    feat_stage(wbuf, h1[:].rearrange("p (c t) -> p c t", c=NC), h2,
               C.pe.n, C.dv.n, wq0)
    wq0 = load_w("l1_ffa2", False)

    pe0, ac0 = C.pe.n, C.ac.n
    h23 = h2[:].rearrange("p (c t) -> p c t", c=NC)

    @on(blk, 'tensor')
    def _(q):
        q.wait_ge(C.wq.h, wq0)
        u = 0
        for i in range(NT):
            for oc in range(2):
                if u >= 4:
                    q.wait_ge(C.ac.h, ac0 + u - 3)
                last = None
                for c in range(NC):
                    last = q.matmul(pf[u % 4][:], h23[:, c, i * 128:(i + 1) * 128],
                                    wbuf[:, c * 1024 + oc * 512:
                                         c * 1024 + (oc + 1) * 512],
                                    start=(c == 0), stop=False)
                last = q.matmul(pf[u % 4][:], C.c_or[:1, :],
                                r_b[:1, oc * 512:(oc + 1) * 512],
                                start=False, stop=True)
                C.pe.inc(last)
                u += 1

    @on(blk, 'scalar')
    def _(q):
        u = 0
        for i in range(NT):
            for oc in range(2):
                q.wait_ge(C.pe.h, pe0 + u + 1)
                q.activation(out=tmp[:], in_=pf[u % 4][:], func=AF.Tanh)
                C.ac.inc(q.activation(
                    out=C.g[:, i * 1024 + oc * 512:i * 1024 + (oc + 1) * 512],
                    in_=tmp[:], func=AF.Silu))
                u += 1

    C.barrier()
    es2.close()


def body_layer1(C):
    esl = contextlib.ExitStack()
    C.ensure_consts(CONST_NEED[1], esl)
    C.g = esl.enter_context(C.nc.sbuf_tensor("g1", [128, NT * D], F32))
    emit_ln_transpose(C)
    emit_l1_ffn(C)
    aof = esl.enter_context(C.nc.sbuf_tensor("aof1", [128, NC * T], BF16))
    emit_st_attention(C, 1, aof)
    emit_combine(C, aof, "l1_WOT", "l1_bO")
    C.barrier()
    esl.close()


def build_layer1(W, dbg=False, stop_after=None, npairs=H // 2, substage=None):
    C = LayerProg(W, LAYER_KEYS[1], no_g=True, need=CONST_NEED[1])
    C._keep_attn_scope = dbg
    C.emit_const_loads()
    body_layer1(C)
    C.close()
    return C


def emit_l2_ffn(C, g_dram):
    nc, blk = C.nc, C.blk
    es2 = contextlib.ExitStack()
    FH = 4096
    NFC = FH // 128
    hA = es2.enter_context(nc.sbuf_tensor("l2hA", [128, NFC * T], BF16))
    stg = es2.enter_context(nc.sbuf_tensor("l2stg", [128, 1024], BF16))
    hB_dram = nc.dram_tensor("l2_hscr", [FH, T], BF16).ap()
    scol = es2.enter_context(nc.sbuf_tensor("l2scol", [128, 8], F32))
    s3row = es2.enter_context(nc.sbuf_tensor("l2s3r", [1, 1024], F32))
    s3b = es2.enter_context(nc.sbuf_tensor("l2s3b", [128, 1024], F32))
    tmp2 = es2.enter_context(nc.sbuf_tensor("l2tmp2", [128, 1024], F32))
    pp = [es2.enter_context(nc.psum_tensor(f"l2pp{j}", [128, 512], F32))
          for j in range(4)]
    es3 = contextlib.ExitStack()
    w12 = [es3.enter_context(nc.sbuf_tensor(f"l2w12_{j}", [128, NC * 2 * 128], BF16))
           for j in range(2)]
    whs = [es3.enter_context(nc.sbuf_tensor(f"l2whs{j}", [128, NFC * 128], BF16))
           for j in range(2)]
    tmp = es3.enter_context(nc.sbuf_tensor("l2tmp", [128, 1024], F32))

    hA3 = hA[:].rearrange("p (c t) -> p c t", c=NFC)

    pe0, ac0, dv0, wq0 = C.pe.n, C.ac.n, C.dv.n, C.wq.n

    @on(blk, 'gpsimd')
    def _(q):
        for m in range(NFC):
            if m >= 2:
                q.wait_ge(C.pe.h, pe0 + 4 * (m - 1))
            wv = w12[m % 2][:].rearrange("p (c j o) -> p c j o", c=NC, j=2)
            C.cast_load(q, wv[:, :, 0, :],
                        C.win["l2_w1T"][:, m * 128:(m + 1) * 128]
                        .rearrange("(c p) o -> p c o", p=128))
            C.cast_load(q, wv[:, :, 1, :],
                        C.win["l2_w2T"][:, m * 128:(m + 1) * 128]
                        .rearrange("(c p) o -> p c o", p=128))

    @on(blk, 'sync')
    def _(q):
        for m in range(NFC):
            if m >= 2:
                q.wait_ge(C.ac.h, ac0 + 4 * (m - 1))
                q.wait_ge(C.dv.h, dv0 + 4 * (m - 1))
            C.xq.inc(q.dma_start(
                out=scol[:, 2 * (m % 2):2 * (m % 2) + 1],
                in_=C.win["l2_w1_s"][m * 128:(m + 1) * 128].unsqueeze(1)), 16)
            C.xq.inc(q.dma_start(
                out=scol[:, 2 * (m % 2) + 1:2 * (m % 2) + 2],
                in_=C.win["l2_w2_s"][m * 128:(m + 1) * 128].unsqueeze(1)), 16)
    xq0 = C.xq.n

    @on(blk, 'tensor')
    def _(q):
        for m in range(NFC):
            q.wait_ge(C.wq.h, wq0 + 32 * (m + 1))
            wv = w12[m % 2][:].rearrange("p (c j o) -> p c j o", c=NC, j=2)
            for tc in range(4):
                u = 4 * m + tc
                if u >= 2:
                    q.wait_ge(C.dv.h, dv0 + u - 1)
                for c in range(NC):
                    q.matmul(pp[2 * (u % 2)][:], wv[:, c, 0, :],
                             C.xnf3()[:, c, tc * 512:(tc + 1) * 512],
                             start=(c == 0), stop=(c == NC - 1))
                for c in range(NC):
                    last = q.matmul(pp[2 * (u % 2) + 1][:], wv[:, c, 1, :],
                                    C.xnf3()[:, c, tc * 512:(tc + 1) * 512],
                                    start=(c == 0), stop=(c == NC - 1))
                C.pe.inc(last)

    @on(blk, 'scalar')
    def _(q):
        for m in range(NFC):
            for tc in range(4):
                u = 4 * m + tc
                q.wait_ge(C.pe.h, pe0 + u + 1)
                if tc == 0:
                    q.wait_ge(C.xq.h, xq0 - 16 * 2 * (NFC - 1 - m))
                if u >= 2:
                    q.wait_ge(C.dv.h, dv0 + u - 1)
                C.ac.inc(q.activation(out=tmp[:, (u % 2) * 512:(u % 2) * 512 + 512],
                                      in_=pp[2 * (u % 2)][:],
                                      func=AF.Silu,
                                      scale=scol[:, 2 * (m % 2):2 * (m % 2) + 1]))

    @on(blk, 'vector')
    def _(q):
        for m in range(NFC):
            for tc in range(4):
                u = 4 * m + tc
                q.wait_ge(C.ac.h, ac0 + u + 1)
                C.dv.inc(q.scalar_tensor_tensor(
                    hA3[:, m, tc * 512:(tc + 1) * 512],
                    pp[2 * (u % 2) + 1][:],
                    scol[:, 2 * (m % 2) + 1:2 * (m % 2) + 2],
                    tmp[:, (u % 2) * 512:(u % 2) * 512 + 512],
                    ALU.mult, ALU.mult))

    C.barrier()

    def mid_stage(wkey, skey):
        pe0, ac0, wq0, yq0 = C.pe.n, C.ac.n, C.wq.n, C.yq.n

        @on(blk, 'gpsimd')
        def _(q):
            for m in range(NFC):
                if m >= 2:
                    q.wait_ge(C.pe.h, pe0 + 4 * (m - 1))
                C.cast_load(q, whs[m % 2][:].rearrange("p (c o) -> p c o", c=NFC),
                            C.win[wkey][:, m * 128:(m + 1) * 128]
                            .rearrange("(c p) o -> p c o", p=128))

        scol_xq = []

        def store_u(q, u):
            m2, tc2 = u // 4, u % 4
            q.wait_ge(C.ac.h, ac0 + u + 1)
            C.yq.inc(q.dma_start(
                out=hB_dram[m2 * 128:(m2 + 1) * 128,
                            tc2 * 512:(tc2 + 1) * 512],
                in_=stg[:, (u % 2) * 512:(u % 2) * 512 + 512]), 16)

        @on(blk, 'sync')
        def _(q):
            for u in range(4 * NFC):
                if u % 4 == 0:
                    m = u // 4
                    if m >= 2:
                        q.wait_ge(C.ac.h, ac0 + 4 * (m - 1))
                    C.xq.inc(q.dma_start(
                        out=scol[:, 4 + (m % 2):5 + (m % 2)],
                        in_=C.win[skey][m * 128:(m + 1) * 128].unsqueeze(1)), 16)
                    scol_xq.append(C.xq.n)
                if u >= 4:
                    store_u(q, u - 4)
            for u in range(4 * NFC - 4, 4 * NFC):
                store_u(q, u)
            q.wait_ge(C.yq.h, C.yq.n)

        @on(blk, 'tensor')
        def _(q):
            for m in range(NFC):
                q.wait_ge(C.wq.h, wq0 + 16 * (m + 1))
                ws = whs[m % 2][:].rearrange("p (c o) -> p c o", c=NFC)
                for tc in range(4):
                    u = 4 * m + tc
                    if u >= 4:
                        q.wait_ge(C.ac.h, ac0 + u - 3)
                    last = None
                    for c in range(NFC):
                        last = q.matmul(pp[u % 4][:], ws[:, c, :],
                                        hA3[:, c, tc * 512:(tc + 1) * 512],
                                        start=(c == 0), stop=(c == NFC - 1))
                    C.pe.inc(last)

        @on(blk, 'scalar')
        def _(q):
            for m in range(NFC):
                for tc in range(4):
                    u = 4 * m + tc
                    q.wait_ge(C.pe.h, pe0 + u + 1)
                    if tc == 0:
                        q.wait_ge(C.xq.h, scol_xq[m])
                    if u >= 2:
                        q.wait_ge(C.yq.h, yq0 + 16 * (u - 1))
                    C.ac.inc(q.activation(
                        out=stg[:, (u % 2) * 512:(u % 2) * 512 + 512],
                        in_=pp[u % 4][:], func=AF.Gelu,
                        scale=scol[:, 4 + (m % 2):5 + (m % 2)]))

        @on(blk, 'sync')
        def _(q):
            for m in range(NFC):
                for tc in range(4):
                    u = 4 * m + tc
                    q.wait_ge(C.ac.h, ac0 + u + 1)
                    C.yq.inc(q.dma_start(
                        out=hB_dram[m * 128:(m + 1) * 128,
                                    tc * 512:(tc + 1) * 512],
                        in_=stg[:, (u % 2) * 512:(u % 2) * 512 + 512]), 16)
            q.wait_ge(C.yq.h, C.yq.n)
        C.barrier()

        @on(blk, 'sync')
        def _(q):
            C.xq.inc(q.dma_start(
                out=hA3, in_=hB_dram.rearrange("(c p) t -> p c t", p=128)), 16)
        C.barrier()

    mid_stage("l2_h1T", "l2_h1_s")
    mid_stage("l2_h2T", "l2_h2_s")
    es3.close()

    pe0, ac0, dv0, wq0 = C.pe.n, C.ac.n, C.dv.n, C.wq.n
    sq0 = C.sq.n

    @on(blk, 'sync')
    def _(q):
        C.xq.inc(q.dma_start(out=s3row[:],
                             in_=C.win["l2_w3_s"].unsqueeze(0)), 16)
    xq_s3 = C.xq.n

    @on(blk, 'tensor')
    def _(q):
        q.wait_ge(C.xq.h, xq_s3)
        for half in range(2):
            C.sq.inc(q.matmul(pp[3][:], C.c_orf[0:1, :],
                              s3row[0:1, half * 512:(half + 1) * 512],
                              start=True, stop=True))
            q.wait_ge(C.dv.h, dv0 + half + 1)

    @on(blk, 'vector')
    def _(q):
        for half in range(2):
            q.wait_ge(C.sq.h, sq0 + half + 1)
            C.dv.inc(q.tensor_copy(s3b[:, half * 512:(half + 1) * 512], pp[3][:]))
    dv_s3 = C.dv.n

    NOC = 4
    OW = 1024 // NOC
    w3h = es2.enter_context(nc.sbuf_tensor("l2w3h", [128, NFC * OW], BF16))
    gsb = es2.enter_context(nc.sbuf_tensor("l2gsb", [128, 2 * OW], F32))
    w3h3 = w3h[:].rearrange("p (c o) -> p c o", c=NFC)
    pe4, ac4, dv4, yq0 = C.pe.n, C.ac.n, C.dv.n, C.yq.n

    @on(blk, 'gpsimd')
    def _(q):
        for oc in range(NOC):
            if oc >= 1:
                q.wait_ge(C.pe.h, pe4 + oc * NT)
            C.cast_load(q, w3h3,
                        C.win["l2_w3T"][:, oc * OW:(oc + 1) * OW]
                        .rearrange("(c p) o -> p c o", p=128))
    wq_w3 = C.wq.n

    @on(blk, 'tensor')
    def _(q):
        for u in range(NOC * NT):
            oc, i = u // NT, u % NT
            q.wait_ge(C.wq.h, wq_w3 - 16 * (NOC - 1 - oc))
            if u >= 2:
                q.wait_ge(C.dv.h, dv_s3 + u - 1)
            last = None
            for c in range(NFC):
                last = q.matmul(pp[u % 2][:, 0:OW],
                                hA3[:, c, i * 128:(i + 1) * 128],
                                w3h3[:, c, :],
                                start=(c == 0), stop=(c == NFC - 1))
            C.pe.inc(last)

    @on(blk, 'vector')
    def _(q):
        for u in range(NOC * NT):
            oc, i = u // NT, u % NT
            q.wait_ge(C.pe.h, pe4 + u + 1)
            if u >= 2:
                q.wait_ge(C.ac.h, ac4 + u - 1)
            C.dv.inc(q.tensor_mul(tmp2[:, (u % 2) * OW:(u % 2) * OW + OW],
                                  pp[u % 2][:, 0:OW],
                                  s3b[:, oc * OW:(oc + 1) * OW]))

    @on(blk, 'scalar')
    def _(q):
        for u in range(NOC * NT):
            oc, i = u // NT, u % NT
            q.wait_ge(C.dv.h, dv_s3 + u + 1)
            if u >= 2:
                q.wait_ge(C.yq.h, yq0 + 16 * (u - 1))
            C.ac.inc(q.activation(
                out=gsb[:, (u % 2) * OW:(u % 2) * OW + OW],
                in_=tmp2[:, (u % 2) * OW:(u % 2) * OW + OW], func=AF.Silu))

    @on(blk, 'sync')
    def _(q):
        for u in range(NOC * NT):
            oc, i = u // NT, u % NT
            q.wait_ge(C.ac.h, ac4 + u + 1)
            C.yq.inc(q.dma_start(
                out=g_dram[i * 128:(i + 1) * 128, oc * OW:(oc + 1) * OW],
                in_=gsb[:, (u % 2) * OW:(u % 2) * OW + OW]), 16)
        q.wait_ge(C.yq.h, C.yq.n)

    C.barrier()
    es2.close()


def emit_l3_ffn(C, xnlo=None, es_lo=None, ohf=None, g_es=None):
    nc, blk = C.nc, C.blk
    es2 = contextlib.ExitStack()
    FE = 2048
    NEF = FE // 128
    TH = 1024

    if ohf is None:
        ohf = es2.enter_context(nc.sbuf_tensor("l3ohf", [16, T], BF16))
    es3 = contextlib.ExitStack()
    wr32 = es3.enter_context(nc.sbuf_tensor("l3wr32", [128, NC * 16], F32))
    wr = es3.enter_context(nc.sbuf_tensor("l3wr", [128, NC * 16], BF16))
    wrl = es3.enter_context(nc.sbuf_tensor("l3wrl", [128, NC * 16], BF16))
    rst = es3.enter_context(nc.sbuf_tensor("l3rst", [128, 8], F32))
    oh1 = es3.enter_context(nc.sbuf_tensor("l3oh1", [128, 16], F32))
    ohb = es3.enter_context(nc.sbuf_tensor("l3ohb", [128, 16], BF16))
    pr = es3.enter_context(nc.psum_tensor("l3pr", [128, 16], F32))
    ptr = es3.enter_context(nc.psum_tensor("l3ptr", [16, 128], BF16))

    @on(blk, 'sync')
    def _(q):
        C.xq.inc(q.dma_start(
            out=wr32[:].rearrange("p (c o) -> p c o", c=NC),
            in_=C.win["l3_routerT"].rearrange("(c p) o -> p c o", p=128)), 16)
    xq_r = C.xq.n

    @on(blk, 'vector')
    def _(q):
        q.wait_ge(C.xq.h, xq_r)
        q.wait_ge(C.dv.h, C.dv.inc(q.tensor_copy(wr[:], wr32[:])))
        C.dv.inc(q.tensor_sub(wrl[:], wr32[:], wr[:]))
    wq0 = C.wq.n
    dv_wr = C.dv.n
    pe0, dv0, ac0, sq0, sv0 = C.pe.n, C.dv.n, C.ac.n, C.sq.n, C.sv.n

    @on(blk, 'tensor')
    def _(q):
        q.wait_ge(C.dv.h, dv_wr)
        for i in range(NT):
            if i >= 1:
                q.wait_ge(C.sv.h, sv0 + i)
                q.wait_ge(C.dv.h, dv0 + 6 * i)
            last = None
            for c in range(NC):
                q.matmul(pr[:], C.xnf3()[:, c, i * 128:(i + 1) * 128],
                         wr[:, c * 16:(c + 1) * 16],
                         start=(c == 0), stop=False)
            for c in range(NC):
                q.matmul(pr[:], C.xnf3()[:, c, i * 128:(i + 1) * 128],
                         wrl[:, c * 16:(c + 1) * 16],
                         start=False, stop=False)
            for c in range(NC):
                last = q.matmul(
                    pr[:],
                    xnlo[:].rearrange("p (c t) -> p c t", c=NC)
                    [:, c, i * 128:(i + 1) * 128],
                    wr[:, c * 16:(c + 1) * 16],
                    start=False, stop=(c == NC - 1))
            C.pe.inc(last)
            q.wait_ge(C.dv.h, dv0 + 6 * i + 6)
            C.sq.inc(q.transpose(ptr[:], ohb[:], C.c_ident[:]))

    @on(blk, 'vector')
    def _(q):
        for i in range(NT):
            q.wait_ge(C.pe.h, pe0 + i + 1)
            q.wait_ge(C.dv.h, C.dv.inc(
                q.reduce_max(rst[:, 0:1], pr[:], axis=AX.X)))
            q.wait_ge(C.dv.h, C.dv.inc(
                q.tensor_scalar(oh1[:], pr[:], rst[:, 0:1], None, ALU.is_ge)))
            q.wait_ge(C.dv.h, C.dv.inc(
                q.tensor_mul(oh1[:], oh1[:], C.c_iota64m[:])))
            q.wait_ge(C.dv.h, C.dv.inc(
                q.tensor_reduce(rst[:, 1:2], oh1[:], AX.X, ALU.min)))
            q.wait_ge(C.dv.h, C.dv.inc(
                q.tensor_scalar_add(rst[:, 1:2], rst[:, 1:2], 64.0)))
            q.wait_ge(C.ac.h, ac0 + i + 1)
            C.dv.inc(q.tensor_scalar(ohb[:], C.c_iota[:], rst[:, 1:2],
                                     rst[:, 2:3], ALU.is_equal, ALU.mult))
            q.wait_ge(C.sq.h, sq0 + i + 1)
            C.sv.inc(q.tensor_copy(ohf[:, i * 128:(i + 1) * 128], ptr[:]))

    @on(blk, 'scalar')
    def _(q):
        for i in range(NT):
            q.wait_ge(C.dv.h, dv0 + 6 * i + 1)
            C.ac.inc(q.activation(out=rst[:, 2:3], in_=rst[:, 0:1],
                                  func=AF.Sigmoid))
    C.barrier()
    es3.close()
    if es_lo is not None:
        es_lo.close()
    if C.g is None:
        C.g = (g_es or C.es).enter_context(
            nc.sbuf_tensor("g3", [128, NT * D], F32))

    es4 = contextlib.ExitStack()
    hsh = es4.enter_context(nc.sbuf_tensor("l3hsh", [128, NEF * T], BF16))
    wsl = [es4.enter_context(nc.sbuf_tensor(f"l3wsl{j}", [128, NC * 128], BF16))
           for j in range(2)]
    pg = [es4.enter_context(nc.psum_tensor(f"l3pg{j}", [128, 512], F32))
          for j in range(4)]
    hsh3 = hsh[:].rearrange("p (c t) -> p c t", c=NEF)

    def sh_pass(wkey, second):
        pe1, ac1, dv1, wq1 = C.pe.n, C.ac.n, C.dv.n, C.wq.n

        @on(blk, 'gpsimd')
        def _(q):
            for m in range(NEF):
                if m >= 2:
                    q.wait_ge(C.pe.h, pe1 + 4 * (m - 1))
                C.cast_load(q, wsl[m % 2][:].rearrange("p (c o) -> p c o", c=NC),
                            C.win[wkey][:, m * 128:(m + 1) * 128]
                            .rearrange("(c p) o -> p c o", p=128))

        @on(blk, 'tensor')
        def _(q):
            for m in range(NEF):
                q.wait_ge(C.wq.h, wq1 + 16 * (m + 1))
                ws = wsl[m % 2][:].rearrange("p (c o) -> p c o", c=NC)
                for tc in range(4):
                    u = 4 * m + tc
                    if u >= 4:
                        if second:
                            q.wait_ge(C.dv.h, dv1 + u - 3)
                        else:
                            q.wait_ge(C.ac.h, ac1 + u - 3)
                    last = None
                    for c in range(NC):
                        last = q.matmul(pg[u % 4][:], ws[:, c, :],
                                        C.xnf3()[:, c, tc * 512:(tc + 1) * 512],
                                        start=(c == 0), stop=(c == NC - 1))
                    C.pe.inc(last)

        if not second:
            @on(blk, 'scalar')
            def _(q):
                for u in range(4 * NEF):
                    q.wait_ge(C.pe.h, pe1 + u + 1)
                    m, tc = u // 4, u % 4
                    C.ac.inc(q.activation(
                        out=hsh3[:, m, tc * 512:(tc + 1) * 512],
                        in_=pg[u % 4][:], func=AF.Silu))
        else:
            @on(blk, 'vector')
            def _(q):
                for u in range(4 * NEF):
                    q.wait_ge(C.pe.h, pe1 + u + 1)
                    m, tc = u // 4, u % 4
                    C.dv.inc(q.scalar_tensor_tensor(
                        hsh3[:, m, tc * 512:(tc + 1) * 512],
                        pg[u % 4][:], 1.0,
                        hsh3[:, m, tc * 512:(tc + 1) * 512],
                        ALU.mult, ALU.mult))
        C.barrier()

    sh_pass("l3_wgT", False)
    sh_pass("l3_wuT", True)

    wdl = es4.enter_context(nc.sbuf_tensor("l3wdl", [128, NEF * 512], BF16))
    pe2, dv2, wq2 = C.pe.n, C.dv.n, C.wq.n

    @on(blk, 'gpsimd')
    def _(q):
        for oc in range(2):
            if oc == 1:
                q.wait_ge(C.pe.h, pe2 + NT)
            C.cast_load(q, wdl[:].rearrange("p (c o) -> p c o", c=NEF),
                        C.win["l3_wdT"][:, oc * 512:(oc + 1) * 512]
                        .rearrange("(c p) o -> p c o", p=128))
    wq3 = C.wq.n
    wdl3 = wdl[:].rearrange("p (c o) -> p c o", c=NEF)

    @on(blk, 'tensor')
    def _(q):
        for u in range(2 * NT):
            oc, i = u // NT, u % NT
            q.wait_ge(C.wq.h, wq3 - 16 * (1 - oc))
            if u >= 2:
                q.wait_ge(C.dv.h, dv2 + u - 1)
            last = None
            for c in range(NEF):
                last = q.matmul(pg[u % 2][:], hsh3[:, c, i * 128:(i + 1) * 128],
                                wdl3[:, c, :], start=(c == 0), stop=(c == NEF - 1))
            C.pe.inc(last)

    @on(blk, 'vector')
    def _(q):
        for u in range(2 * NT):
            oc, i = u // NT, u % NT
            q.wait_ge(C.pe.h, pe2 + u + 1)
            C.dv.inc(q.tensor_copy(
                C.g[:, i * 1024 + oc * 512:i * 1024 + (oc + 1) * 512],
                pg[u % 2][:]))
    C.barrier()
    es4.close()

    es5 = contextlib.ExitStack()
    selb = es5.enter_context(nc.sbuf_tensor("l3selb", [128, TH], F32))
    mxn = es5.enter_context(nc.sbuf_tensor("l3mxn", [128, NC * TH], BF16))
    hge = es5.enter_context(nc.sbuf_tensor("l3hge", [128, NEF * TH], BF16))
    wge = [es5.enter_context(nc.sbuf_tensor(f"l3wge{j}", [128, NC * 512], BF16))
           for j in range(2)]
    wde = es5.enter_context(nc.sbuf_tensor("l3wde", [128, NEF * 512], BF16))
    pm = [es5.enter_context(nc.psum_tensor(f"l3pm{j}", [128, 512], F32))
          for j in range(4)]
    psel = es5.enter_context(nc.psum_tensor("l3psel", [128, 512], F32))
    mxn3 = mxn[:].rearrange("p (c t) -> p c t", c=NC)
    hge3 = hge[:].rearrange("p (c t) -> p c t", c=NEF)
    hue3 = hge3

    NTH = TH // 128
    for th in range(2):
        for e in range(16):
            pe5, dv5, sq5 = C.pe.n, C.dv.n, C.sq.n

            @on(blk, 'tensor')
            def _(q, th=th, e=e, dvn=C.dv.n):
                q.wait_ge(C.dv.h, dvn)
                for hf in range(2):
                    if hf == 1:
                        q.wait_ge(C.dv.h, dv5 + 1)
                    C.sq.inc(q.matmul(
                        psel[:], C.c_esel[:, e * 128:(e + 1) * 128],
                        ohf[:, th * TH + hf * 512:th * TH + (hf + 1) * 512],
                        start=True, stop=True))

            @on(blk, 'vector')
            def _(q, th=th, e=e):
                for hf in range(2):
                    q.wait_ge(C.sq.h, sq5 + hf + 1)
                    C.dv.inc(q.tensor_copy(
                        selb[:, hf * 512:(hf + 1) * 512], psel[:]))
                q.wait_ge(C.dv.h, C.dv.n)
                for c in range(NC):
                    C.dv.inc(q.tensor_mul(
                        mxn3[:, c, :], C.xnf3()[:, c, th * TH:(th + 1) * TH],
                        selb[:]))
            dv6 = C.dv.n

            pe6, ac6, dv7, wq6 = C.pe.n, C.ac.n, C.dv.n, C.wq.n

            @on(blk, 'gpsimd')
            def _(q, th=th, e=e):
                for w8 in range(8):
                    if w8 >= 2:
                        q.wait_ge(C.pe.h, pe6 + 8 * (w8 - 1))
                    C.cast_load(
                        q, wge[w8 % 2][:].rearrange("p (c o) -> p c o", c=NC),
                        C.win["l3_gup"][e][:, w8 * 512:(w8 + 1) * 512]
                        .rearrange("(c p) o -> p c o", p=128))

            @on(blk, 'tensor')
            def _(q, th=th, e=e):
                q.wait_ge(C.dv.h, dv6)
                for w8 in range(8):
                    q.wait_ge(C.wq.h, wq6 + 16 * (w8 + 1))
                    wg3 = wge[w8 % 2][:].rearrange("p (c o) -> p c o", c=NC)
                    for sub in range(4):
                        for tch in range(2):
                            u = 8 * w8 + 2 * sub + tch
                            if u >= 4:
                                if w8 < 4:
                                    q.wait_ge(C.ac.h, ac6 + u - 3)
                                else:
                                    q.wait_ge(C.ac.h, ac6 + 32)
                                    q.wait_ge(C.dv.h, dv7 + max(0, u - 32 - 3))
                            last = None
                            for c in range(NC):
                                last = q.matmul(
                                    pm[u % 4][:],
                                    wg3[:, c, sub * 128:(sub + 1) * 128],
                                    mxn3[:, c, tch * 512:(tch + 1) * 512],
                                    start=(c == 0), stop=(c == NC - 1))
                            C.pe.inc(last)

            @on(blk, 'scalar')
            def _(q, th=th, e=e):
                for w8 in range(4):
                    for sub in range(4):
                        for tch in range(2):
                            u = 8 * w8 + 2 * sub + tch
                            fg = 4 * w8 + sub
                            q.wait_ge(C.pe.h, pe6 + u + 1)
                            C.ac.inc(q.activation(
                                out=hge3[:, fg, tch * 512:(tch + 1) * 512],
                                in_=pm[u % 4][:], func=AF.Silu))

            @on(blk, 'vector')
            def _(q, th=th, e=e):
                for w8 in range(4, 8):
                    for sub in range(4):
                        for tch in range(2):
                            u = 8 * w8 + 2 * sub + tch
                            fu = 4 * (w8 - 4) + sub
                            q.wait_ge(C.pe.h, pe6 + u + 1)
                            C.dv.inc(q.scalar_tensor_tensor(
                                hue3[:, fu, tch * 512:(tch + 1) * 512],
                                pm[u % 4][:], 1.0,
                                hge3[:, fu, tch * 512:(tch + 1) * 512],
                                ALU.mult, ALU.mult))
            dv8 = C.dv.n

            pe7, dv9, wq7 = C.pe.n, C.dv.n, C.wq.n

            @on(blk, 'gpsimd')
            def _(q, th=th, e=e):
                for oc in range(2):
                    if oc == 1:
                        q.wait_ge(C.pe.h, pe7 + NTH)
                    C.cast_load(
                        q, wde[:].rearrange("p (c o) -> p c o", c=NEF),
                        C.win["l3_down"][e][:, oc * 512:(oc + 1) * 512]
                        .rearrange("(c p) o -> p c o", p=128))
            wq8 = C.wq.n
            wde3 = wde[:].rearrange("p (c o) -> p c o", c=NEF)

            @on(blk, 'tensor')
            def _(q, th=th, e=e):
                q.wait_ge(C.dv.h, dv8)
                for u in range(2 * NTH):
                    oc, i = u // NTH, u % NTH
                    q.wait_ge(C.wq.h, wq8 - 16 * (1 - oc))
                    if u >= 2:
                        q.wait_ge(C.dv.h, dv9 + u - 1)
                    it = th * NTH + i
                    last = None
                    for c in range(NEF):
                        last = q.matmul(
                            pm[u % 2][:], hue3[:, c, i * 128:(i + 1) * 128],
                            wde3[:, c, :], start=(c == 0), stop=(c == NEF - 1))
                    C.pe.inc(last)

            @on(blk, 'vector')
            def _(q, th=th, e=e):
                for u in range(2 * NTH):
                    oc, i = u // NTH, u % NTH
                    q.wait_ge(C.pe.h, pe7 + u + 1)
                    it = th * NTH + i
                    gsl = C.g[:, it * 1024 + oc * 512:it * 1024 + (oc + 1) * 512]
                    C.dv.inc(q.tensor_add(gsl, gsl, pm[u % 2][:]))

    C.barrier()
    es5.close()

    @on(blk, 'scalar')
    def _(q):
        for i in range(NT):
            C.ac.inc(q.activation(out=C.g[:, i * 1024:(i + 1) * 1024],
                                  in_=C.g[:, i * 1024:(i + 1) * 1024],
                                  func=AF.Silu))
    C.barrier()
    es2.close()


def emit_l3_attn(C, aof):
    nc, blk = C.nc, C.blk
    es2 = contextlib.ExitStack()
    EPS = 1e-6
    aot = es2.enter_context(nc.sbuf_tensor("l3aot", [128, NT * D], BF16))
    wp = es2.enter_context(nc.sbuf_tensor("l3wp", [128, NC * 128], BF16))
    qf = es2.enter_context(nc.sbuf_tensor("l3qf", [128, T], BF16))
    kf = es2.enter_context(nc.sbuf_tensor("l3kf", [128, T], BF16))
    vf = es2.enter_context(nc.sbuf_tensor("l3vf", [128, T], BF16))
    qt = es2.enter_context(nc.sbuf_tensor("l3qt", [128, NT * 128], BF16))
    kt = es2.enter_context(nc.sbuf_tensor("l3kt", [128, NT * 128], BF16))
    vt = es2.enter_context(nc.sbuf_tensor("l3vt", [128, NT * 128], BF16))
    bcol = es2.enter_context(nc.sbuf_tensor("l3bcol", [128, 4], F32))
    sums = es2.enter_context(nc.sbuf_tensor("l3sums", [128, 4], F32))
    sumb = es2.enter_context(nc.sbuf_tensor("l3sumb", [128, 4], BF16))
    tcol = es2.enter_context(nc.sbuf_tensor("l3tcol", [128, 4], F32))
    tcolb = es2.enter_context(nc.sbuf_tensor("l3tcolb", [128, 4], BF16))
    esc = es2.enter_context(nc.sbuf_tensor("l3esc", [1, 8], F32))
    escb = es2.enter_context(nc.sbuf_tensor("l3escb", [1, 8], BF16))
    stat = es2.enter_context(nc.sbuf_tensor("l3stat", [128, 4 * NT], F32))
    stb = es2.enter_context(nc.sbuf_tensor("l3stb", [128, 2 * NT], BF16))
    csr = es2.enter_context(nc.sbuf_tensor("l3csr", [1, T], F32))
    csb = vf
    rr = es2.enter_context(nc.sbuf_tensor("l3rr", [1, 4], F32))
    sct = es2.enter_context(nc.sbuf_tensor("l3sct", [128, NT], F32))
    vsb = es2.enter_context(nc.sbuf_tensor("l3vsb", [128, 2 * 64], BF16))
    kvb = es2.enter_context(nc.sbuf_tensor("l3kvb", [128, 64], BF16))
    pA = [es2.enter_context(nc.psum_tensor(f"l3pA{j}", [128, 1024], F32))
          for j in range(2)]
    pB = es2.enter_context(nc.psum_tensor("l3pB", [128, 512], F32))
    pS = es2.enter_context(nc.psum_tensor("l3pS", [128, 512], BF16))
    pD = es2.enter_context(nc.psum_tensor("l3pD", [128, 512], F32))
    pE = es2.enter_context(nc.psum_tensor("l3pE", [128, 512], F32))

    wnames = ("l3_Wq", "l3_Wk", "l3_Wv")
    bnames = ("l3_bq", "l3_bk", "l3_bv")

    for p in range(H // 2):
        wv3 = wp[:].rearrange("p (c o) -> p c o", c=NC)
        sqld = C.sq.n

        @on(blk, 'gpsimd')
        def _(q, p=p, sqn=C.sq.n):
            if p >= 1:
                q.wait_ge(C.sq.h, sqn)
            for j in range(3):
                if j >= 1:
                    q.wait_ge(C.sq.h, sqld + 2 * j)
                C.cast_load(q, wv3,
                            C.win[wnames[j]][:, p * 128:(p + 1) * 128]
                            .rearrange("(c q) o -> q c o", q=128))

        @on(blk, 'sync')
        def _(q, p=p, acn=C.ac.n):
            if p >= 1:
                q.wait_ge(C.ac.h, acn)
            for j in range(3):
                C.xq.inc(q.dma_start(out=bcol[:, j:j + 1],
                                     in_=C.win[bnames[j]][p * 128:(p + 1) * 128]
                                     .unsqueeze(1)), 16)
        wq_p, xq_p = C.wq.n, C.xq.n
        sq0, ac0 = C.sq.n, C.ac.n

        @on(blk, 'tensor')
        def _(q, p=p, sen=C.se.n, svn=C.sv.n, dvn=C.dv.n):
            if p >= 1:
                q.wait_ge(C.sv.h, svn)
                q.wait_ge(C.dv.h, dvn)
            for u in range(6):
                j, hv = u // 2, u % 2
                q.wait_ge(C.wq.h, wq_p - 16 * (2 - j))
                if u >= 2:
                    q.wait_ge(C.ac.h, ac0 + u - 1)
                dst = pA[u % 2]
                last = None
                for half in range(2):
                    for c in range(NC):
                        last = q.matmul(
                            dst[:, half * 512:(half + 1) * 512],
                            wv3[:, c, :],
                            C.xnf3()[:, c, hv * 1024 + half * 512:
                                     hv * 1024 + (half + 1) * 512],
                            start=(c == 0), stop=(c == NC - 1))
                C.sq.inc(last)

        @on(blk, 'scalar')
        def _(q, p=p):
            q.wait_ge(C.xq.h, xq_p)
            for u in range(6):
                j, hv = u // 2, u % 2
                tgt = (qf, kf, vf)[j]
                q.wait_ge(C.sq.h, sq0 + u + 1)
                fn = AF.Sigmoid if j < 2 else AF.Identity
                C.ac.inc(q.activation(out=tgt[:, hv * 1024:(hv + 1) * 1024],
                                      in_=pA[u % 2][:], func=fn,
                                      bias=bcol[:, j:j + 1]))
        ac_pr = C.ac.n

        sq1, sv1 = C.sq.n, C.sv.n

        @on(blk, 'tensor')
        def _(q, p=p):
            q.wait_ge(C.ac.h, ac_pr)
            g = 0
            for (srcT, dstT) in ((qf, qt), (kf, kt), (vf, vt)):
                for grp in range(4):
                    if g >= 1:
                        q.wait_ge(C.sv.h, sv1 + g)
                    last = None
                    for s in range(4):
                        i = grp * 4 + s
                        last = q.transpose(pS[:, s * 128:(s + 1) * 128],
                                           srcT[:, i * 128:(i + 1) * 128],
                                           C.c_ident[:])
                    C.sq.inc(last)
                    g += 1

        @on(blk, 'vector')
        def _(q, p=p):
            g = 0
            for (srcT, dstT) in ((qf, qt), (kf, kt), (vf, vt)):
                for grp in range(4):
                    q.wait_ge(C.sq.h, sq1 + g + 1)
                    C.sv.inc(q.tensor_copy(
                        dstT[:, grp * 512:(grp + 1) * 512], pS[:]))
                    g += 1
        sv_tr = C.sv.n

        @on(blk, 'vector')
        def _(q, p=p):
            q.wait_ge(C.ac.h, ac_pr)
            q.wait_ge(C.dv.h, C.dv.inc(
                q.reduce_sum(sums[:, 0:1], kf[:], axis=AX.X)))
            q.wait_ge(C.dv.h, C.dv.inc(
                q.tensor_scalar_add(sums[:, 0:1], sums[:, 0:1], EPS)))
            q.wait_ge(C.dv.h, C.dv.inc(
                q.reduce_sum(sums[:, 1:2], qf[:], axis=AX.X)))
            q.wait_ge(C.dv.h, C.dv.inc(
                q.tensor_scalar_add(sums[:, 1:2], sums[:, 1:2], EPS)))
            C.dv.inc(q.tensor_copy(sumb[:, 0:2], sums[:, 0:2]))
        dv_sums = C.dv.n

        for hh in range(2):
            h = 2 * p + hh
            base = hh * 64

            sq2 = C.sq.n

            @on(blk, 'tensor')
            def _(q, hh=hh, base=base):
                q.wait_ge(C.dv.h, dv_sums)
                q.wait_ge(C.sv.h, sv_tr)
                C.sq.inc(q.matmul(pD[0:1, 20:21], sumb[base:base + 64, 0:1],
                                  C.c_oc[base:base + 64, 0:1],
                                  start=True, stop=True))
                C.sq.inc(q.matmul(pD[0:1, 21:22], sumb[base:base + 64, 1:2],
                                  C.c_oc[base:base + 64, 0:1],
                                  start=True, stop=True))

            @on(blk, 'vector')
            def _(q, hh=hh):
                q.wait_ge(C.sq.h, sq2 + 2)
                q.wait_ge(C.dv.h, C.dv.inc(
                    q.tensor_scalar_mul(esc[0:1, 0:2], pD[0:1, 20:22], EPS)))
                C.dv.inc(q.tensor_copy(escb[0:1, 0:2], esc[0:1, 0:2]))
            dv_es = C.dv.n

            sq3 = C.sq.n

            @on(blk, 'tensor')
            def _(q, hh=hh, base=base):
                q.wait_ge(C.dv.h, dv_es)
                last = None
                for i in range(NT):
                    q.matmul(pD[:, i:i + 1], qf[base:base + 64, i * 128:(i + 1) * 128],
                             sumb[base:base + 64, 0:1], start=True, stop=False)
                    last = q.matmul(pD[:, i:i + 1], C.c_or[:1, :],
                                    escb[0:1, 0:1], start=False, stop=True)
                C.sq.inc(last)
                last = None
                for i in range(NT):
                    q.matmul(pE[:, i:i + 1], kf[base:base + 64, i * 128:(i + 1) * 128],
                             sumb[base:base + 64, 1:2], start=True, stop=False)
                    last = q.matmul(pE[:, i:i + 1], C.c_or[:1, :],
                                    escb[0:1, 1:2], start=False, stop=True)
                C.sq.inc(last)

            @on(blk, 'vector')
            def _(q, hh=hh):
                q.wait_ge(C.sq.h, sq3 + 1)
                q.wait_ge(C.dv.h, C.dv.inc(
                    q.reciprocal(stat[:, 0:NT], pD[:, 0:NT])))
                q.wait_ge(C.dv.h, C.dv.inc(
                    q.tensor_copy(stb[:, 0:NT], stat[:, 0:NT])))
                q.wait_ge(C.sq.h, sq3 + 2)
                q.wait_ge(C.dv.h, C.dv.inc(
                    q.reciprocal(stat[:, 3 * NT:4 * NT], pE[:, 0:NT])))
                C.dv.inc(q.tensor_copy(stb[:, NT:2 * NT], stat[:, 3 * NT:4 * NT]))
            dv_sio = C.dv.n

            sq4 = C.sq.n

            @on(blk, 'tensor')
            def _(q, hh=hh, base=base):
                q.wait_ge(C.dv.h, dv_sio)
                last = None
                for i in range(NT):
                    last = q.matmul(pD[base:base + 64, 16:17],
                                    kt[:, i * 128 + base:i * 128 + base + 64],
                                    stb[:, NT + i:NT + i + 1],
                                    start=(i == 0), stop=(i == NT - 1))
                C.sq.inc(last)
                last = None
                for i in range(NT):
                    last = q.matmul(pD[base:base + 64, 17:18],
                                    qt[:, i * 128 + base:i * 128 + base + 64],
                                    stb[:, i:i + 1],
                                    start=(i == 0), stop=(i == NT - 1))
                C.sq.inc(last)

            @on(blk, 'vector')
            def _(q, hh=hh, base=base):
                q.wait_ge(C.sq.h, sq4 + 2)
                q.wait_ge(C.dv.h, C.dv.inc(
                    q.tensor_scalar_add(tcol[base:base + 64, 0:2],
                                        pD[base:base + 64, 16:18], EPS)))
                C.dv.inc(q.tensor_copy(tcolb[base:base + 64, 0:2],
                                       tcol[base:base + 64, 0:2]))
            dv_t = C.dv.n
            sq5 = C.sq.n

            @on(blk, 'tensor')
            def _(q, hh=hh, base=base):
                q.wait_ge(C.dv.h, dv_t)
                C.sq.inc(q.matmul(pE[0:1, 20:21], tcolb[base:base + 64, 0:1],
                                  C.c_oc[base:base + 64, 0:1],
                                  start=True, stop=True))
                C.sq.inc(q.matmul(pE[0:1, 21:22], tcolb[base:base + 64, 1:2],
                                  C.c_oc[base:base + 64, 0:1],
                                  start=True, stop=True))

            @on(blk, 'vector')
            def _(q, hh=hh):
                q.wait_ge(C.sq.h, sq5 + 2)
                q.wait_ge(C.dv.h, C.dv.inc(
                    q.tensor_scalar_mul(esc[0:1, 2:4], pE[0:1, 20:22], EPS)))
                C.dv.inc(q.tensor_copy(escb[0:1, 2:4], esc[0:1, 2:4]))
            dv_es2 = C.dv.n

            sq6 = C.sq.n

            @on(blk, 'tensor')
            def _(q, hh=hh, base=base):
                q.wait_ge(C.dv.h, dv_es2)
                last = None
                for i in range(NT):
                    q.matmul(pB[:, i:i + 1], qf[base:base + 64, i * 128:(i + 1) * 128],
                             tcolb[base:base + 64, 0:1], start=True, stop=False)
                    last = q.matmul(pB[:, i:i + 1], C.c_or[:1, :],
                                    escb[0:1, 2:3], start=False, stop=True)
                C.sq.inc(last)
            ac1 = C.ac.n

            @on(blk, 'scalar')
            def _(q, hh=hh):
                q.wait_ge(C.sq.h, sq6 + 1)
                C.ac.inc(q.activation(out=stat[:, NT:2 * NT], in_=pB[:, 0:NT],
                                      func=AF.Exp, scale=C.fv(-1.0)))

            @on(blk, 'vector')
            def _(q, hh=hh):
                q.wait_ge(C.ac.h, ac1 + 1)
                q.wait_ge(C.dv.h, C.dv.inc(
                    q.tensor_scalar_add(stat[:, NT:2 * NT],
                                        stat[:, NT:2 * NT], 1.0)))
                q.wait_ge(C.dv.h, C.dv.inc(
                    q.reciprocal(stat[:, NT:2 * NT], stat[:, NT:2 * NT])))
                C.dv.inc(q.tensor_mul(stat[:, 2 * NT:3 * NT], stat[:, 0:NT],
                                      stat[:, NT:2 * NT]))
            dv_sa = C.dv.n

            sq7 = C.sq.n

            @on(blk, 'tensor')
            def _(q, hh=hh, base=base, dvn=C.dv.n):
                q.wait_ge(C.dv.h, dvn)
                for ch in range(4):
                    if ch >= 2:
                        q.wait_ge(C.dv.h, dvn + ch - 1)
                    last = None
                    q.matmul(pA[ch % 2][0:1, 0:512],
                             tcolb[base:base + 64, 1:2],
                             kf[base:base + 64, ch * 512:(ch + 1) * 512],
                             start=True, stop=False)
                    for r4 in range(4):
                        last = q.matmul(pA[ch % 2][0:1, r4 * 128:(r4 + 1) * 128],
                                        escb[0:1, 3:4], C.c_or[0:1, :],
                                        start=False, stop=(r4 == 3))
                    C.sq.inc(last)

            @on(blk, 'vector')
            def _(q, hh=hh, dvn=C.dv.n):
                for ch in range(4):
                    q.wait_ge(C.sq.h, sq7 + ch + 1)
                    C.dv.inc(q.tensor_copy(csr[0:1, ch * 512:(ch + 1) * 512],
                                           pA[ch % 2][0:1, 0:512]))
                q.wait_ge(C.dv.h, C.dv.inc(
                    q.tensor_scalar(csr[0:1, :], csr[0:1, :], 1.0, -1.0,
                                    ALU.min, ALU.max)))
                C.dv.inc(q.reduce_max(rr[0:1, 0:1], csr[0:1, :], axis=AX.X,
                                      negate=True))
            dv_cs = C.dv.n
            ac2 = C.ac.n

            @on(blk, 'scalar')
            def _(q, hh=hh):
                q.wait_ge(C.dv.h, dv_cs)
                C.ac.inc(q.activation(out=csr[0:1, :], in_=csr[0:1, :],
                                      func=AF.Exp, bias=rr[0:1, 0:1],
                                      accum_out=rr[0:1, 1:2]))

            @on(blk, 'vector')
            def _(q, hh=hh):
                q.wait_ge(C.ac.h, ac2 + 1)
                q.wait_ge(C.dv.h, C.dv.inc(
                    q.reciprocal(rr[0:1, 2:3], rr[0:1, 1:2])))
                C.dv.inc(q.tensor_scalar(csb[0:1, :], csr[0:1, :],
                                         rr[0:1, 2:3], float(T),
                                         ALU.mult, ALU.mult))
            dv_sc = C.dv.n
            sq8 = C.sq.n

            @on(blk, 'tensor')
            def _(q, hh=hh):
                q.wait_ge(C.dv.h, dv_sc)
                last = None
                for i in range(NT):
                    last = q.transpose(pS[:, 2 * i:2 * i + 1],
                                       csb[0:1, i * 128:(i + 1) * 128],
                                       C.c_ident[0:1, 0:1])
                C.sq.inc(last)

            @on(blk, 'vector')
            def _(q, hh=hh):
                q.wait_ge(C.sq.h, sq8 + 1)
                C.dv.inc(q.tensor_copy(
                    sct[:, 0:NT],
                    bass.AP(pS.tensor if hasattr(pS, "tensor") else pS, 0,
                            [[512, 128], [2, NT]]) if False else
                    pS[:, 0:2 * NT].rearrange("p (t two) -> p t two", two=2)[:, :, 0]))
            dv_sct = C.dv.n

            sq9, dv_v0 = C.sq.n, C.dv.n

            @on(blk, 'vector')
            def _(q, hh=hh, base=base):
                q.wait_ge(C.dv.h, dv_sct)
                for i in range(NT):
                    if i >= 2:
                        q.wait_ge(C.sq.h, sq9 + i - 1)
                    C.dv.inc(q.tensor_scalar_mul(
                        vsb[:, (i % 2) * 64:(i % 2) * 64 + 64],
                        vt[:, i * 128 + base:i * 128 + base + 64],
                        sct[:, i:i + 1]))

            @on(blk, 'tensor')
            def _(q, hh=hh, base=base):
                for i in range(NT):
                    q.wait_ge(C.dv.h, dv_v0 + i + 1)
                    C.sq.inc(q.matmul(pE[base:base + 64, 32:96],
                                      kt[:, i * 128 + base:i * 128 + base + 64],
                                      vsb[:, (i % 2) * 64:(i % 2) * 64 + 64],
                                      start=(i == 0), stop=(i == NT - 1)))

            @on(blk, 'vector')
            def _(q, hh=hh, base=base):
                q.wait_ge(C.sq.h, C.sq.n)
                C.dv.inc(q.tensor_copy(kvb[base:base + 64, :],
                                       pE[base:base + 64, 32:96]))
            dv_kv = C.dv.n

            sqA = C.sq.n

            @on(blk, 'tensor')
            def _(q, hh=hh, base=base, dvn=C.dv.n):
                q.wait_ge(C.dv.h, dv_kv)
                for i in range(NT):
                    if i >= 2:
                        q.wait_ge(C.dv.h, dvn + i - 1)
                    C.sq.inc(q.matmul(pA[i % 2][:, 0:64],
                                      qf[base:base + 64, i * 128:(i + 1) * 128],
                                      kvb[base:base + 64, :],
                                      start=True, stop=True))

            @on(blk, 'vector')
            def _(q, hh=hh, h=h):
                for i in range(NT):
                    q.wait_ge(C.sq.h, sqA + i + 1)
                    C.dv.inc(q.tensor_scalar_mul(
                        aot[:, i * 1024 + h * 64:i * 1024 + h * 64 + 64],
                        pA[i % 2][:, 0:64],
                        stat[:, 2 * NT + i:2 * NT + i + 1]))

    C.barrier()
    sqT, svT = C.sq.n, C.sv.n

    @on(blk, 'tensor')
    def _(q):
        for i in range(NT):
            for gg in range(2):
                gidx = 2 * i + gg
                if gidx >= 1:
                    q.wait_ge(C.sv.h, svT + gidx)
                last = None
                for s in range(4):
                    c = gg * 4 + s
                    last = q.transpose(
                        pS[:, s * 128:(s + 1) * 128],
                        aot[:, i * 1024 + c * 128:i * 1024 + (c + 1) * 128],
                        C.c_ident[:])
                C.sq.inc(last)

    @on(blk, 'vector')
    def _(q):
        for i in range(NT):
            for gg in range(2):
                q.wait_ge(C.sq.h, sqT + 2 * i + gg + 1)
                C.sv.inc(q.tensor_copy(
                    aof[:].rearrange("p (c t) -> p c t", c=NC)
                    [:, gg * 4:(gg + 1) * 4, i * 128:(i + 1) * 128],
                    pS[:].rearrange("p (s t) -> p s t", s=4)))

    C.barrier()
    es2.close()


def emit_st_attention(C, layer, aof, npairs=H // 2, substage=None):
    nc, blk = C.nc, C.blk
    es2 = contextlib.ExitStack()
    wpair = es2.enter_context(nc.sbuf_tensor("wpair", [128, NC * 3 * 128], BF16))
    qp = es2.enter_context(nc.sbuf_tensor("qpair", [128, T], BF16))
    kp = es2.enter_context(nc.sbuf_tensor("kpair", [128, T], BF16))
    vf = es2.enter_context(nc.sbuf_tensor("vfeat", [128, T], BF16))
    va = [es2.enter_context(nc.sbuf_tensor(f"vaug{j}", [128, NT * 80], BF16))
          for j in range(2)]
    ptb = [es2.enter_context(nc.sbuf_tensor(f"ptb{j}", [128, 1024], BF16))
           for j in range(2)]
    scrb = es2.enter_context(nc.sbuf_tensor("scrb", [128, T], BF16))
    nrow = es2.enter_context(nc.sbuf_tensor("nrow", [1, T], F32))
    nbrow = es2.enter_context(nc.sbuf_tensor("nbrow", [1, T], BF16))
    zrow = es2.enter_context(nc.sbuf_tensor("zrow", [65, 1024], F32))
    zbsb = es2.enter_context(nc.sbuf_tensor("zbsb", [64, 512], F32))
    mvsb = es2.enter_context(nc.sbuf_tensor("mvsb", [64, 512], BF16))
    kst = es2.enter_context(nc.sbuf_tensor("kst", [1, 8], F32))
    kcol = es2.enter_context(nc.sbuf_tensor("kcol", [128, 1], BF16))
    bcol = es2.enter_context(nc.sbuf_tensor("bcol", [128, 4], F32))
    ps = [es2.enter_context(nc.psum_tensor(f"psc{j}", [128, 1024], F32))
          for j in range(2)]
    ppv = es2.enter_context(nc.psum_tensor("ppv", [65, 1024], F32))
    pzb = es2.enter_context(nc.psum_tensor("pzb", [64, 512], F32))
    ptv = es2.enter_context(nc.psum_tensor("ptv", [128, 512], BF16))

    QCH = 1024
    NQC = T // QCH
    scale = 0.125

    if layer == 1:
        bdall = es2.enter_context(nc.sbuf_tensor("bdall", [128, 16 * 128], BF16))
        hkb = es2.enter_context(nc.sbuf_tensor("hkb", [128, 16 * 128], BF16))
        tblf = es2.enter_context(nc.sbuf_tensor("tblf", [64, 1024], F32))
        tcolb = es2.enter_context(nc.sbuf_tensor("tcolb", [64, 1], BF16))

        @on(blk, 'vector')
        def _(q):
            C.dv.inc(q.memset(tblf[:], 0.0))
        dv_ms = C.dv.n

        @on(blk, 'sync')
        def _(q):
            q.wait_ge(C.dv.h, dv_ms)
            th = C.win["l1_tableT"].tensor
            C.xq.inc(q.dma_start(out=tblf[0:63, :],
                                 in_=bass.AP(th, 0, [[1024, 63], [1, 1024]])), 16)
            C.xq.inc(q.dma_start(out=tblf[63:64, 0:1008],
                                 in_=bass.AP(th, 63 * 1024, [[1008, 1], [1, 1008]])), 16)
        xq_t = C.xq.n

        @on(blk, 'vector')
        def _(q):
            q.wait_ge(C.xq.h, xq_t)
            C.dv.inc(q.reduce_max(tcolb[:], tblf[:], axis=AX.X,
                                  apply_absolute_value=True))
        dv_t = C.dv.n

        @on(blk, 'tensor')
        def _(q):
            q.wait_ge(C.dv.h, dv_t)
            C.pe.inc(q.transpose(ptv[0:1, 0:64], tcolb[:], C.c_ident[0:64, 0:64]))
        pe_t = C.pe.n

        @on(blk, 'vector')
        def _(q):
            q.wait_ge(C.pe.h, pe_t)
            q.wait_ge(C.dv.h, C.dv.inc(
                q.reduce_max(kst[0:1, 4:5], ptv[0:1, 0:64], axis=AX.X)))
            C.dv.inc(q.tensor_scalar_add(kst[0:1, 4:5], kst[0:1, 4:5], 1.0))
    else:
        @on(blk, 'vector')
        def _(q):
            C.dv.inc(q.memset(kst[0:1, 4:5], 1.0))

    @on(blk, 'vector')
    def _(q):
        q.memset(kst[:], 0.0)
        q.memset(nbrow[:], 0.0)
        q.memset(nrow[:], 0.0)
        for j in range(2):
            q.memset(va[j][:], 0.0)
            C.dv.inc(q.memset(
                va[j][:].rearrange("p (n e) -> p n e", e=80)[:, :, 64:65], 1.0))

    C.barrier()

    wsl = {0: ("l0_wqkvT",) * 3, 1: ("l1_WQT", "l1_WKT", "l1_WVT"),
           2: ("l2_WqT", "l2_WkT", "l2_WvT")}[layer]
    woff = (0, 1024, 2048) if layer == 0 else (0, 0, 0)

    def kranges(qc):
        if layer == 0:
            return list(range(0, min(NT, qc * 8 + 10)))
        if layer == 1:
            return list(range(0, qc * 8 + 8))
        return list(range(NT))

    for p in range(npairs):
        wv3 = wpair[:].rearrange("p (c j o) -> p c j o", c=NC, j=3)
        pe_prev, sv_prev, se_prev = C.pe.n, C.sv.n, C.se.n

        @on(blk, 'gpsimd')
        def _(q, p=p):
            if p >= 1:
                q.wait_ge(C.sq.h, C.sq.n)
            for j in range(3):
                C.cast_load(q, wv3[:, :, j, :],
                            C.win[wsl[j]][:, woff[j] + p * 128:woff[j] + (p + 1) * 128]
                            .rearrange("(c q) o -> q c o", q=128))

        if layer in (0, 1):
            @on(blk, 'sync')
            def _(q, p=p, acn=C.ac.n):
                if p >= 1:
                    q.wait_ge(C.ac.h, acn)
                if layer == 0:
                    b = C.win["l0_bqkv"]
                    srcs = [b[p * 128:(p + 1) * 128],
                            b[1024 + p * 128:1024 + (p + 1) * 128],
                            b[2048 + p * 128:2048 + (p + 1) * 128]]
                else:
                    srcs = [C.win[nm][p * 128:(p + 1) * 128]
                            for nm in ("l1_bQ", "l1_bK", "l1_bV")]
                for j in range(3):
                    C.xq.inc(q.dma_start(out=bcol[:, j:j + 1],
                                         in_=srcs[j].unsqueeze(1)), 16)

            @on(blk, 'vector')
            def _(q):
                q.wait_ge(C.xq.h, C.xq.n)
                C.dv.inc(q.tensor_scalar_mul(bcol[:, 0:1], bcol[:, 0:1], scale))

        if substage == "load":
            continue
        wq_p, dv_p, xq_p = C.wq.n, C.dv.n, C.xq.n
        sq0, ac0 = C.sq.n, C.ac.n

        NU = 6 if not (substage or "").startswith("proj") or substage == "proj" \
            else int(substage[4:])
        @on(blk, 'tensor')
        def _(q, p=p):
            q.wait_ge(C.wq.h, wq_p)
            q.wait_ge(C.dv.h, dv_p)
            if p >= 1:
                q.wait_ge(C.se.h, se_prev)
                q.wait_ge(C.sv.h, sv_prev)
            for u in range(NU):
                j, hv = u // 2, u % 2
                if u >= 2:
                    q.wait_ge(C.ac.h, ac0 + u - 1)
                dst = ps[u % 2]
                for c in range(NC):
                    q.matmul(dst[:, 0:512], wv3[:, c, j, :],
                             C.xnf3()[:, c, hv * 1024:hv * 1024 + 512],
                             start=(c == 0), stop=(c == NC - 1))
                for c in range(NC):
                    last = q.matmul(dst[:, 512:1024], wv3[:, c, j, :],
                                    C.xnf3()[:, c, hv * 1024 + 512:hv * 1024 + 1024],
                                    start=(c == 0), stop=(c == NC - 1))
                C.sq.inc(last)

        @on(blk, 'scalar')
        def _(q, p=p):
            for u in range(NU):
                j, hv = u // 2, u % 2
                q.wait_ge(C.sq.h, sq0 + u + 1)
                tgt = (qp, kp, vf)[j]
                kw = {}
                if j == 0:
                    kw["scale"] = C.fv(0.125)
                if layer in (0, 1):
                    kw["bias"] = bcol[:, j:j + 1]
                C.ac.inc(q.activation(out=tgt[:, hv * 1024:(hv + 1) * 1024],
                                      in_=ps[u % 2][:], func=AF.Identity, **kw))

        if substage == "proj":
            continue
        sq1, sv1 = C.sq.n, C.sv.n

        @on(blk, 'tensor')
        def _(q, p=p):
            q.wait_ge(C.ac.h, ac0 + 6)
            for grp in range(4):
                if grp >= 1:
                    q.wait_ge(C.sv.h, sv1 + grp)
                last = None
                for t4 in range(4):
                    i = grp * 4 + t4
                    last = q.transpose(ptv[:, t4 * 128:(t4 + 1) * 128],
                                       vf[:, i * 128:(i + 1) * 128], C.c_ident[:])
                C.sq.inc(last)

        @on(blk, 'vector')
        def _(q, p=p):
            for grp in range(4):
                q.wait_ge(C.sq.h, sq1 + grp + 1)
                p3 = ptv[:].rearrange("p (n e) -> p n e", e=128)
                q.tensor_copy(
                    va[0][:].rearrange("p (n e) -> p n e", e=80)
                    [:, grp * 4:(grp + 1) * 4, 0:64], p3[:, :, 0:64])
                C.sv.inc(q.tensor_copy(
                    va[1][:].rearrange("p (n e) -> p n e", e=80)
                    [:, grp * 4:(grp + 1) * 4, 0:64], p3[:, :, 64:128]))
        sv_after_v = C.sv.n

        if substage == "vtr":
            continue
        if layer in (0, 1):
            ac1 = C.ac.n

            @on(blk, 'scalar')
            def _(q, p=p):
                if p >= 1:
                    q.wait_ge(C.sq.h, C.sq.n)
                C.ac.inc(q.activation(out=scrb[:], in_=qp[:], func=AF.Abs))
            dv1 = C.dv.n

            @on(blk, 'vector')
            def _(q, p=p):
                q.wait_ge(C.ac.h, ac1)
                C.dv.inc(q.reduce_max(kcol[:], kp[:], axis=AX.X,
                                      apply_absolute_value=True))
            sq2 = C.sq.n

            @on(blk, 'tensor')
            def _(q, p=p):
                q.wait_ge(C.dv.h, dv1 + 1)
                q.wait_ge(C.sv.h, sv_after_v)
                C.sq.inc(q.transpose(ptv[0:1, 0:128], kcol[:], C.c_ident[:]))

            @on(blk, 'vector')
            def _(q, p=p):
                q.wait_ge(C.sq.h, sq2 + 1)
                for hh in range(2):
                    q.wait_ge(C.dv.h, C.dv.inc(
                        q.reduce_max(kst[0:1, hh:hh + 1],
                                     ptv[0:1, hh * 64:(hh + 1) * 64], axis=AX.X)))
                    C.dv.inc(q.tensor_scalar_mul(kst[0:1, hh:hh + 1],
                                                 kst[0:1, hh:hh + 1], -1.02))

        if substage == "bound":
            continue
        for hh in range(2):
            h = 2 * p + hh
            base = hh * 64

            if layer == 1:
                wqh = C.wq.n

                @on(blk, 'gpsimd')
                def _(q, h=h):
                    if True:
                        q.wait_ge(C.sq.h, C.sq.n)
                    src = bass.AP(C.win["l1_tableT"].tensor, h * 4095 + 1920,
                                  [[1, 128], [128, 16], [1, 128]])
                    C.cast_load(q, hkb[:].rearrange("r (d c) -> r d c", d=16), src)
                sq3, sv3 = C.sq.n, C.sv.n

                @on(blk, 'tensor')
                def _(q, h=h):
                    q.wait_ge(C.wq.h, C.wq.n)
                    q.wait_ge(C.dv.h, C.dv.n)
                    q.wait_ge(C.sv.h, sv3)
                    for g in range(4):
                        if g >= 1:
                            q.wait_ge(C.sv.h, sv3 + g)
                        last = None
                        for s in range(4):
                            d = g * 4 + s
                            last = q.transpose(ptv[:, s * 128:(s + 1) * 128],
                                               hkb[:, d * 128:(d + 1) * 128],
                                               C.c_jex[:])
                        C.sq.inc(last)

                @on(blk, 'vector')
                def _(q, h=h):
                    for g in range(4):
                        q.wait_ge(C.sq.h, sq3 + g + 1)
                        C.sv.inc(q.tensor_copy(
                            bdall[:, g * 512:(g + 1) * 512], ptv[:]))

            if layer in (0, 1):
                sq4, sv4 = C.sq.n, C.sv.n

                @on(blk, 'tensor')
                def _(q, base=base):
                    q.wait_ge(C.ac.h, C.ac.n)
                    for qc4 in range(4):
                        if qc4 >= 1:
                            q.wait_ge(C.sv.h, sv4 + qc4)
                        C.sq.inc(q.matmul(pzb[0:1, 0:512],
                                          C.c_oc[base:base + 64, 0:1],
                                          scrb[base:base + 64,
                                               qc4 * 512:(qc4 + 1) * 512],
                                          start=True, stop=True))

                @on(blk, 'vector')
                def _(q, base=base, hh=hh, sqn=C.sq.n):
                    for qc4 in range(4):
                        q.wait_ge(C.sq.h, sq4 + qc4 + 1)
                        C.sv.inc(q.tensor_copy(nrow[0:1, qc4 * 512:(qc4 + 1) * 512],
                                               pzb[0:1, 0:512]))
                    q.wait_ge(C.sq.h, sqn)
                    q.wait_ge(C.sv.h, C.sv.n)
                    q.wait_ge(C.dv.h, C.dv.n)
                    C.dv.inc(q.tensor_scalar(nbrow[0:1, :], nrow[0:1, :],
                                             kst[0:1, hh:hh + 1],
                                             kst[0:1, 4:5],
                                             ALU.mult, ALU.subtract))

            for qc in range(NQC):
                ks = kranges(qc)
                s0, e0, v0 = C.sq.n, C.se.n, C.sv.n
                dvq = C.dv.n

                @on(blk, 'tensor')
                def _(q, h=h, base=base, qc=qc, ks=ks, s0=s0, e0=e0, v0=v0):
                    nk = len(ks)

                    def emit_pv(idx):
                        tjp = ks[idx]
                        last = None
                        for half in range(2):
                            last = q.matmul(
                                ppv[:, half * 512:(half + 1) * 512],
                                va[hh][:, tjp * 80:tjp * 80 + 65],
                                ptb[idx % 2][:, half * 512:(half + 1) * 512],
                                start=(idx == 0), stop=(idx == nk - 1))
                        C.sv.inc(last)

                    for idx, tj in enumerate(ks):
                        if idx >= 2:
                            q.wait_ge(C.se.h, e0 + idx - 1)
                        elif idx == 0:
                            q.wait_ge(C.se.h, e0)
                            q.wait_ge(C.dv.h, dvq)
                            q.wait_ge(C.sv.h, v0)
                        dst = ps[idx % 2]
                        mms = []
                        for half in range(2):
                            mms.append((dst[:, half * 512:(half + 1) * 512],
                                        kp[base:base + 64, tj * 128:(tj + 1) * 128],
                                        qp[base:base + 64,
                                           qc * QCH + half * 512:
                                           qc * QCH + (half + 1) * 512]))
                        if layer in (0, 1):
                            for half in range(2):
                                mms.append((dst[:, half * 512:(half + 1) * 512],
                                            C.c_or[:1, :],
                                            nbrow[0:1, qc * QCH + half * 512:
                                                  qc * QCH + (half + 1) * 512]))
                        if layer == 0:
                            pw = (tj - 2) * 128 - qc * QCH
                        elif layer == 1:
                            pw = tj * 128 - qc * QCH
                        else:
                            pw = -1
                        if layer == 1:
                            lo = max(qc * 8, tj)
                            hi = qc * 8 + 8
                            for ti in range(lo, hi):
                                d = ti - tj
                                off = (ti - qc * 8) * 128
                                mms.append((dst[:, off:off + 128],
                                            bdall[:, d * 128:(d + 1) * 128],
                                            C.c_ident[:]))
                        if layer in (0, 1) and pw > 0:
                            wv = min(pw, QCH)
                            off = 0
                            while off < wv:
                                ws = min(512, wv - off)
                                mms.append((dst[:, off:off + ws],
                                            C.c_or[:1, :], C.c_n9[0:1, 0:ws]))
                                off += ws
                        if layer in (0, 1) and 0 <= pw < QCH:
                            mms.append((dst[:, pw:pw + 128],
                                        C.c_ident[:], C.c_tmask[:]))
                        bank_ids = [int(o_.offset) // 512 for (o_, l_, r_) in mms]
                        first_of, last_of = {}, {}
                        for mi, b in enumerate(bank_ids):
                            first_of.setdefault(b, mi)
                            last_of[b] = mi
                        for mi, (o_, l_, r_) in enumerate(mms):
                            b = bank_ids[mi]
                            last = q.matmul(o_, l_, r_,
                                            start=(first_of[b] == mi),
                                            stop=(last_of[b] == mi))
                        C.sq.inc(last)
                        if idx >= 1:
                            q.wait_ge(C.se.h, e0 + idx)
                            emit_pv(idx - 1)
                    q.wait_ge(C.se.h, e0 + nk)
                    emit_pv(nk - 1)

                @on(blk, 'scalar')
                def _(q, qc=qc, ks=ks, s0=s0, e0=e0, v0=v0):
                    func = AF.Sigmoid if layer == 2 else AF.Exp
                    for idx in range(len(ks)):
                        q.wait_ge(C.sq.h, s0 + idx + 1)
                        if idx >= 2:
                            q.wait_ge(C.sv.h, v0 + idx - 1)
                        elif idx == 0:
                            q.wait_ge(C.sv.h, v0)
                        C.se.inc(q.activation(out=ptb[idx % 2][:],
                                              in_=ps[idx % 2][:], func=func))

                sq5 = C.sq.n

                @on(blk, 'vector')
                def _(q, qc=qc, ks=ks, v0=v0):
                    q.wait_ge(C.sv.h, v0 + len(ks))
                    if layer == 2:
                        q.wait_ge(C.dv.h, C.dv.inc(
                            q.tensor_scalar_add(zrow[64:65, 0:QCH],
                                                ppv[64:65, 0:QCH], 1e-6)))
                        C.dv.inc(q.reciprocal(zrow[64:65, 0:QCH],
                                              zrow[64:65, 0:QCH]))
                    else:
                        C.dv.inc(q.reciprocal(zrow[64:65, 0:QCH],
                                              ppv[64:65, 0:QCH]))
                dv2 = C.dv.n

                for half in range(2):
                    sq6 = C.sq.n

                    @on(blk, 'tensor')
                    def _(q, half=half, dv2=dv2):
                        q.wait_ge(C.dv.h, dv2)
                        if half == 1:
                            q.wait_ge(C.dv.h, C.dv.n)
                        C.sq.inc(q.matmul(
                            pzb[:, 0:512], C.c_o64[64:65, 0:64],
                            zrow[64:65, half * 512:half * 512 + 512],
                            start=True, stop=True))

                    @on(blk, 'vector')
                    def _(q, h=h, hh=hh, p=p, qc=qc, half=half, sq6=sq6):
                        q.wait_ge(C.sq.h, sq6 + 1)
                        q.wait_ge(C.dv.h, C.dv.inc(
                            q.tensor_copy(zbsb[:], pzb[:, 0:512])))
                        col = p * T + qc * QCH + half * 512
                        if hh == 0:
                            C.dv.inc(q.tensor_mul(
                                aof[0:64, col:col + 512],
                                ppv[0:64, half * 512:(half + 1) * 512], zbsb[:]))
                        else:
                            C.dv.inc(q.tensor_mul(
                                mvsb[:],
                                ppv[0:64, half * 512:(half + 1) * 512], zbsb[:]))
                    if hh == 1:
                        dv3, sq7 = C.dv.n, C.sq.n

                        @on(blk, 'tensor')
                        def _(q, dv3=dv3):
                            q.wait_ge(C.dv.h, dv3)
                            C.sq.inc(q.matmul(ps[0][64:128, 0:512],
                                              C.c_ident[0:64, 0:64], mvsb[:],
                                              start=True, stop=True))

                        @on(blk, 'vector')
                        def _(q, p=p, qc=qc, half=half, sq7=sq7):
                            q.wait_ge(C.sq.h, sq7 + 1)
                            col = p * T + qc * QCH + half * 512
                            C.dv.inc(q.tensor_copy(aof[64:128, col:col + 512],
                                                   ps[0][64:128, 0:512]))
                dv4 = C.dv.n

                @on(blk, 'tensor')
                def _(q, dv4=dv4):
                    q.wait_ge(C.dv.h, dv4)

    C.barrier()
    C._attn_dbg = dict(qp=qp, kp=kp, vf=vf, va=va, nbrow=nbrow, nrow=nrow,
                       kst=kst, scrb=scrb, ptb=ptb)
    if getattr(C, "_keep_attn_scope", False):
        C._attn_es = es2
    else:
        es2.close()


def emit_combine(C, aof, wkey, bkey, g_dram=None):
    nc, blk = C.nc, C.blk
    es2 = contextlib.ExitStack()
    wout = es2.enter_context(nc.sbuf_tensor("wout", [128, NC * 1024], BF16))
    r_bo = es2.enter_context(nc.sbuf_tensor("r_bo", [1, 1024], BF16))
    xrb = [es2.enter_context(nc.sbuf_tensor(f"xrb{j}", [128, 1024], F32))
           for j in range(2)]
    ysb = [es2.enter_context(nc.sbuf_tensor(f"ysb{j}", [128, 1024], F32))
           for j in range(2)]
    grb = [es2.enter_context(nc.sbuf_tensor(f"grb{j}", [128, 1024], F32))
           for j in range(2)] if g_dram is not None else None
    tmp = es2.enter_context(nc.sbuf_tensor("cmb_tmp", [128, 1024], F32))
    pcmb = [es2.enter_context(nc.psum_tensor(f"pcmb{j}", [128, 512], F32))
            for j in range(4)]

    @on(blk, 'gpsimd')
    def _(q):
        C.cast_load(q, wout[:].rearrange("p (c o) -> p c o", c=NC),
                    C.win[wkey].rearrange("(c p) o -> p c o", p=128))
        if bkey is not None:
            C.cast_load(q, r_bo[:], C.win[bkey].unsqueeze(0))
    wq_c = C.wq.n

    xq0, pe0, dv0, yq0 = C.xq.n, C.pe.n, C.dv.n, C.yq.n
    aof3 = aof[:].rearrange("p (c t) -> p c t", c=NC)

    @on(blk, 'sync')
    def _(q):
        for i in range(NT):
            if i >= 1:
                q.wait_ge(C.dv.h, dv0 + 2 * (i - 1) + 2)
                C.yq.inc(q.dma_start(out=C.y[(i - 1) * 128:i * 128, :],
                                     in_=ysb[(i - 1) % 2][:]), 16)
            C.xq.inc(q.dma_start(out=xrb[i % 2][:],
                                 in_=C.x[i * 128:(i + 1) * 128, :]), 16)
            if g_dram is not None:
                C.xq.inc(q.dma_start(out=grb[i % 2][:],
                                     in_=g_dram[i * 128:(i + 1) * 128, :]), 16)
        q.wait_ge(C.dv.h, dv0 + 2 * (NT - 1) + 2)
        C.yq.inc(q.dma_start(out=C.y[(NT - 1) * 128:NT * 128, :],
                             in_=ysb[(NT - 1) % 2][:]), 16)
        q.wait_ge(C.yq.h, C.yq.n)

    @on(blk, 'tensor')
    def _(q):
        q.wait_ge(C.wq.h, wq_c)
        for i in range(NT):
            if i >= 2:
                q.wait_ge(C.dv.h, dv0 + 2 * (i - 1))
            last = None
            for oc in range(2):
                dst = pcmb[2 * (i % 2) + oc]
                for c in range(NC):
                    last = q.matmul(dst[:], aof3[:, c, i * 128:(i + 1) * 128],
                                    wout[:, c * 1024 + oc * 512:c * 1024 + (oc + 1) * 512],
                                    start=(c == 0), stop=(bkey is None and c == NC - 1))
                if bkey is not None:
                    last = q.matmul(dst[:], C.c_or[:1, :],
                                    r_bo[:1, oc * 512:(oc + 1) * 512],
                                    start=False, stop=True)
            C.pe.inc(last)

    @on(blk, 'vector')
    def _(q):
        NL = 16 if g_dram is None else 32
        for i in range(NT):
            q.wait_ge(C.pe.h, pe0 + i + 1)
            q.wait_ge(C.xq.h, xq0 + NL * (i + 1))
            if i >= 2:
                q.wait_ge(C.yq.h, yq0 + 16 * (i - 1))
            for oc in range(2):
                gsl = (C.g[:, i * 1024 + oc * 512:i * 1024 + (oc + 1) * 512]
                       if g_dram is None else
                       grb[i % 2][:, oc * 512:(oc + 1) * 512])
                q.tensor_mul(tmp[:, oc * 512:(oc + 1) * 512],
                             pcmb[2 * (i % 2) + oc][:], gsl)
            C.dv.inc(q.tensor_add(ysb[i % 2][:], tmp[:], xrb[i % 2][:]), 2)

    es2.close()


def emit_debug_dumps(C, aof):
    nc, blk = C.nc, C.blk
    dxnf = nc.dram_tensor("dbg_xnf", [128, NC * T], BF16, kind="ExternalOutput").ap()
    dg = nc.dram_tensor("dbg_g", [128, NT * D], F32, kind="ExternalOutput").ap()
    daof = nc.dram_tensor("dbg_aof", [128, NC * T], BF16, kind="ExternalOutput").ap()
    C.barrier()

    A = getattr(C, "_attn_dbg", None)
    dqp = nc.dram_tensor("dbg_qp", [128, T], BF16, kind="ExternalOutput").ap()
    dkp = nc.dram_tensor("dbg_kp", [128, T], BF16, kind="ExternalOutput").ap()
    dnb = nc.dram_tensor("dbg_nb", [1, T], BF16, kind="ExternalOutput").ap()
    dva0 = nc.dram_tensor("dbg_va0", [128, NT * 80], BF16, kind="ExternalOutput").ap()
    dkst = nc.dram_tensor("dbg_kst", [1, 5], F32, kind="ExternalOutput").ap()

    @on(blk, 'sync')
    def _(q):
        C.yq.inc(q.dma_start(out=dxnf, in_=C.xnf[:]), 16)
        if C.g is not None:
            C.yq.inc(q.dma_start(out=dg, in_=C.g[:]), 16)
        C.yq.inc(q.dma_start(out=daof, in_=aof[:]), 16)
        if A is not None:
            C.yq.inc(q.dma_start(out=dqp, in_=A["qp"][:]), 16)
            C.yq.inc(q.dma_start(out=dkp, in_=A["kp"][:]), 16)
            C.yq.inc(q.dma_start(out=dnb, in_=A["nbrow"][:]), 16)
            C.yq.inc(q.dma_start(out=dva0, in_=A["va"][0][:]), 16)
            C.yq.inc(q.dma_start(out=dkst, in_=A["kst"][0:1, 0:5]), 16)
        q.wait_ge(C.yq.h, C.yq.n)


def emit_trivial_y(C):
    nc, blk = C.nc, C.blk
    es2 = contextlib.ExitStack()
    tb = es2.enter_context(nc.sbuf_tensor("triv", [128, 1024], F32))
    C.barrier()

    @on(blk, 'sync')
    def _(q):
        for i in range(NT):
            C.xq.inc(q.dma_start(out=tb[:], in_=C.x[i * 128:(i + 1) * 128, :]), 16)
            q.wait_ge(C.xq.h, C.xq.n)
            C.yq.inc(q.dma_start(out=C.y[i * 128:(i + 1) * 128, :], in_=tb[:]), 16)
            q.wait_ge(C.yq.h, C.yq.n)
    es2.close()


def body_layer0(C):
    esl = contextlib.ExitStack()
    C.ensure_consts(CONST_NEED[0], esl)
    C.g = esl.enter_context(C.nc.sbuf_tensor("g0", [128, NT * D], F32))
    emit_ln_transpose(C)
    emit_l0_ffn(C)
    aof = esl.enter_context(C.nc.sbuf_tensor("aof0", [128, NC * T], BF16))
    emit_st_attention(C, 0, aof)
    emit_combine(C, aof, "l0_woT", "l0_bo")
    C.barrier()
    esl.close()


def build_layer0(W, dbg=False, stop_after=None, npairs=H // 2, substage=None):
    C = LayerProg(W, LAYER_KEYS[0], no_g=True, need=CONST_NEED[0])
    C._keep_attn_scope = dbg
    C.emit_const_loads()
    body_layer0(C)
    C.close()
    return C


def body_layer2(C):
    esl = contextlib.ExitStack()
    C.ensure_consts(CONST_NEED[2], esl)
    C.g = None
    g_dram = C.nc.dram_tensor("g_scratch", [T, D], F32).ap()
    emit_ln_transpose(C)
    emit_l2_ffn(C, g_dram)
    aof = esl.enter_context(C.nc.sbuf_tensor("aof2", [128, NC * T], BF16))
    emit_st_attention(C, 2, aof)
    emit_combine(C, aof, "l2_WoT", None, g_dram=g_dram)
    C.barrier()
    esl.close()


def build_layer2(W, dbg=False, stop_after=None, npairs=H // 2, substage=None):
    C = LayerProg(W, LAYER_KEYS[2], no_g=True, need=CONST_NEED[2])
    C._keep_attn_scope = dbg
    C.emit_const_loads()
    body_layer2(C)
    C.close()
    return C


def body_layer3(C):
    esl = contextlib.ExitStack()
    C.ensure_consts(CONST_NEED[3], esl)
    C.g = None
    ohf = esl.enter_context(C.nc.sbuf_tensor("l3ohf", [16, T], BF16))
    es_lo = contextlib.ExitStack()
    xnlo = es_lo.enter_context(C.nc.sbuf_tensor("xnlo", [128, NC * T], BF16))
    emit_ln_transpose(C, lo_dst=xnlo)
    emit_l3_ffn(C, xnlo=xnlo, es_lo=es_lo, ohf=ohf, g_es=esl)
    aof = esl.enter_context(C.nc.sbuf_tensor("aof3", [128, NC * T], BF16))
    emit_l3_attn(C, aof)
    emit_combine(C, aof, "l3_Wo", "l3_bo")
    C.barrier()
    esl.close()


def build_layer3(W, dbg=False, stop_after=None, npairs=H // 2, substage=None):
    C = LayerProg(W, LAYER_KEYS[3], no_g=True, need=CONST_NEED[3])
    C._keep_attn_scope = False
    C.emit_const_loads()
    body_layer3(C)
    C.close()
    return C


def build_layer(l, W, dbg=False, stop_after=None, npairs=H // 2, substage=None):
    if l == 0:
        return build_layer0(W, dbg, stop_after, npairs, substage)
    if l == 1:
        return build_layer1(W, dbg, stop_after, npairs, substage)
    if l == 2:
        return build_layer2(W, dbg, stop_after, npairs, substage)
    if l == 3:
        return build_layer3(W, dbg, stop_after, npairs, substage)
    raise NotImplementedError(f"layer {l}")


BODIES = {0: body_layer0, 1: body_layer1, 2: body_layer2, 3: body_layer3}


def build_fused(W):
    keys = sum((LAYER_KEYS[l] for l in range(4)), [])
    C = LayerProg(W, keys, no_g=True, need=[])
    C._keep_attn_scope = False
    yout = C.y
    scr = [C.nc.dram_tensor(f"xs{l}", [T, D], F32).ap() for l in (1, 2, 3)]
    for l in range(4):
        if l > 0:
            C.x = scr[l - 1]
        C.y = scr[l] if l < 3 else yout
        BODIES[l](C)
    C.close()
    return C


def kernel(**inputs):
    W, xs = host_prep(inputs)
    C = build_fused(W)
    shared = {k: W[k] for l in range(4) for k in LAYER_KEYS[l]}
    shared.update({c[1]: W[c[1]] for c in CONSTS})
    in_maps = [dict(shared, x=xs[i]) for i in range(8)]
    res = run_bass_kernel_spmd(C.nc, in_maps, core_ids=list(range(8)))
    cur = [np.asarray(r["y"], dtype=np.float32) for r in res.results]
    return np.stack(cur, axis=0)


# version 04
# speedup vs baseline: 1.0559x; 1.0559x over previous
import contextlib
import numpy as np
import ml_dtypes
import concourse.bass as bass
import concourse.mybir as mybir
from concourse.bass_utils import run_bass_kernel_spmd

F32 = mybir.dt.float32
BF16 = mybir.dt.bfloat16
AF = mybir.ActivationFunctionType
ALU = mybir.AluOpType
AX = mybir.AxisListType

T = 2048
D = 1024
NT = T // 128
NC = D // 128
H = 16
NEG = -1.0e9
LN_EPS = 1e-5
bfd = ml_dtypes.bfloat16
FVALS = [-0.5, 0.125, 1e-5, -1.0, 1.0]
FIDX = {v: i for i, v in enumerate(FVALS)}


class Sem:
    def __init__(self, nc, es, name):
        self.h = es.enter_context(nc.semaphore(name))
        self.n = 0

    def inc(self, instr, k=1):
        instr.then_inc(self.h, k)
        self.n += k
        return self.n


def on(blk, eng):
    def deco(fn):
        getattr(blk, eng)(fn)
    return deco


def host_prep(inputs):
    I = inputs
    w = {}
    ct = lambda a: np.ascontiguousarray(np.asarray(a, dtype=np.float32).T)
    cc = lambda a: np.ascontiguousarray(np.asarray(a), dtype=np.float32)
    w["l0_wqkvT"] = ct(I["L0_attn_in_proj_w"])
    w["l0_bqkv"] = cc(I["L0_attn_in_proj_b"])
    w["l0_woT"] = ct(I["L0_attn_out_proj_w"])
    w["l0_bo"] = cc(I["L0_attn_out_proj_b"])
    w["l0_basewT"] = ct(I["L0_ffn_base_w"])
    w["l0_baseb"] = cc(I["L0_ffn_base_b"])
    w["l0_loraA"] = cc(I["L0_ffn_lora_A"])
    w["l0_loraB"] = cc(np.reshape(np.asarray(I["L0_ffn_lora_B"]), (64, 1024)))
    w["l0_routerT"] = ct(I["L0_ffn_router_w"])
    w["l0_routerb"] = cc(I["L0_ffn_router_b"])
    for k in ("WQ", "WK", "WV", "WO"):
        w[f"l1_{k}T"] = ct(I[f"L1_attn_{k}"])
    for k in ("bQ", "bK", "bV", "bO"):
        w[f"l1_{k}"] = cc(I[f"L1_attn_{k}"])
    w["l1_tableT"] = ct(I["L1_attn_rel_pos_table"])
    w["l1_ffw"] = cc(I["L1_ffn_weight"])
    w["l1_ffa1"] = cc(I["L1_ffn_aux_weight1"])
    w["l1_ffa2"] = cc(I["L1_ffn_aux_weight2"])
    w["l1_ffb"] = cc(I["L1_ffn_bias"])
    for k in ("Wq", "Wk", "Wv", "Wo"):
        w[f"l2_{k}T"] = ct(I[f"L2_attn_{k}"])
    w["l2_w1T"] = ct(I["L2_ffn_w1_w"])
    w["l2_w2T"] = ct(I["L2_ffn_w2_w"])
    w["l2_h1T"] = ct(I["L2_ffn_h1_w"])
    w["l2_h2T"] = ct(I["L2_ffn_h2_w"])
    w["l2_w3T"] = ct(I["L2_ffn_w3_w"])
    for k in ("w1_s", "w2_s", "h1_s", "h2_s", "w3_s"):
        w[f"l2_{k}"] = cc(I[f"L2_ffn_{k}"])
    for k in ("Wq", "Wk", "Wv", "Wo", "bq", "bk", "bv", "bo"):
        w[f"l3_{k}"] = cc(I[f"L3_attn_{k}"])
    w["l3_routerT"] = ct(I["L3_ffn_w_router"])
    w["l3_gup"] = cc(I["L3_ffn_gate_up_proj"])
    w["l3_down"] = cc(I["L3_ffn_down_proj"])
    w["l3_wgT"] = ct(I["L3_ffn_w_gate"])
    w["l3_wuT"] = ct(I["L3_ffn_w_up"])
    w["l3_wdT"] = ct(I["L3_ffn_w_down"])
    w["identbf"] = np.eye(128, dtype=bfd)
    r = np.arange(128)
    w["tmaskbf"] = np.where(r[:, None] <= r[None, :], 0.0, NEG).astype(bfd)
    w["jexbf"] = np.ascontiguousarray(np.eye(128, dtype=np.float32)[:, ::-1]).astype(bfd)
    w["onesrowbf"] = np.ones((1, 128), dtype=bfd)
    w["onescolbf"] = np.ones((128, 1), dtype=bfd)
    w["onesrowf"] = np.ones((1, 128), dtype=np.float32)
    o64 = np.zeros((65, 128), dtype=np.float32)
    o64[64, :] = 1.0
    w["ones64f"] = o64
    w["neg1e9row"] = np.full((1, 1024), NEG, dtype=bfd)
    w["iota16"] = np.tile(np.arange(16, dtype=np.float32)[None, :], (128, 1))
    w["utri1bf"] = np.where(r[:, None] <= r[None, :], 1.0, 0.0).astype(bfd)
    w["ao1bf"] = np.ones((128, 128), dtype=bfd)
    w["iotacap"] = np.tile(np.arange(384, dtype=np.float32)[None, :], (128, 1))
    w["iota64m"] = np.tile(np.arange(16, dtype=np.float32)[None, :] - 64.0, (128, 1))
    w["eselbf"] = np.ascontiguousarray(
        np.eye(16, dtype=bfd)[:, :, None] * np.ones((1, 1, 128), dtype=bfd)
    ).reshape(16, 16 * 128).astype(bfd)
    w["fvals"] = np.tile(np.array(FVALS, dtype=np.float32)[None, :], (128, 1))
    x = np.asarray(inputs["L0_attn_x"], dtype=np.float32)
    xs = [np.ascontiguousarray(x[i]) for i in range(8)]
    return w, xs


CONSTS = [("c_ident", "identbf", [128, 128], BF16),
          ("c_tmask", "tmaskbf", [128, 128], BF16),
          ("c_jex", "jexbf", [128, 128], BF16),
          ("c_or", "onesrowbf", [1, 128], BF16),
          ("c_oc", "onescolbf", [128, 1], BF16),
          ("c_orf", "onesrowf", [1, 128], F32),
          ("c_o64", "ones64f", [65, 128], F32),
          ("c_n9", "neg1e9row", [1, 1024], BF16),
          ("c_iota", "iota16", [128, 16], F32),
          ("c_iota64m", "iota64m", [128, 16], F32),
          ("c_esel", "eselbf", [16, 16 * 128], BF16),
          ("c_ut", "utri1bf", [128, 128], BF16),
          ("c_ao", "ao1bf", [128, 128], BF16),
          ("c_icap", "iotacap", [128, 384], F32),
          ("c_fv", "fvals", [128, len(FVALS)], F32)]

LAYER_KEYS = {
    0: ["l0_wqkvT", "l0_bqkv", "l0_woT", "l0_bo", "l0_basewT", "l0_baseb",
        "l0_loraA", "l0_loraB", "l0_routerT", "l0_routerb"],
    1: ["l1_WQT", "l1_WKT", "l1_WVT", "l1_WOT", "l1_bQ", "l1_bK", "l1_bV",
        "l1_bO", "l1_tableT", "l1_ffw", "l1_ffa1", "l1_ffa2", "l1_ffb"],
    2: ["l2_WqT", "l2_WkT", "l2_WvT", "l2_WoT", "l2_w1T", "l2_w2T", "l2_h1T",
        "l2_h2T", "l2_w3T", "l2_w1_s", "l2_w2_s", "l2_h1_s", "l2_h2_s", "l2_w3_s"],
    3: ["l3_Wq", "l3_Wk", "l3_Wv", "l3_Wo", "l3_bq", "l3_bk", "l3_bv", "l3_bo",
        "l3_routerT", "l3_gup", "l3_down", "l3_wgT", "l3_wuT", "l3_wdT"],
}


CONST_NEED = {
    0: ["c_ident", "c_tmask", "c_or", "c_oc", "c_orf", "c_o64", "c_n9", "c_fv"],
    1: ["c_ident", "c_tmask", "c_jex", "c_or", "c_oc", "c_orf", "c_o64",
        "c_n9", "c_fv"],
    2: ["c_ident", "c_or", "c_oc", "c_orf", "c_o64", "c_n9", "c_fv"],
    3: ["c_ident", "c_or", "c_oc", "c_orf", "c_o64", "c_iota", "c_iota64m",
        "c_ut", "c_ao", "c_icap", "c_fv"],
}


class LayerProg:
    def __init__(self, W, wkeys, no_g=False, need=None):
        nc = bass.Bass("TRN2", target_bir_lowering=False)
        self.nc = nc
        _sb, _ps = nc.sbuf_tensor, nc.psum_tensor
        self._uid = [0]

        def _mk(f):
            def wrap(name, *a, **kw):
                self._uid[0] += 1
                return f(f"{name}_u{self._uid[0]}", *a, **kw)
            return wrap
        nc.sbuf_tensor = _mk(_sb)
        nc.psum_tensor = _mk(_ps)
        self.x = nc.dram_tensor("x", [T, D], F32, kind="ExternalInput").ap()
        self.y = nc.dram_tensor("y", [T, D], F32, kind="ExternalOutput").ap()
        self.win = {}
        for k in wkeys + [c[1] for c in CONSTS]:
            a = W[k]
            dt = BF16 if a.dtype == bfd else F32
            self.win[k] = nc.dram_tensor(k, list(a.shape), dt,
                                         kind="ExternalInput").ap()
        self.es = contextlib.ExitStack()
        self.es.__enter__()
        es = self.es
        self.blk = es.enter_context(nc.Block())
        self.xq = Sem(nc, es, "xq")
        self.wq = Sem(nc, es, "wq")
        self.pe = Sem(nc, es, "pe")
        self.dv = Sem(nc, es, "dv")
        self.ac = Sem(nc, es, "ac")
        self.sq = Sem(nc, es, "sq")
        self.sv = Sem(nc, es, "sv")
        self.se = Sem(nc, es, "se")
        self.yq = Sem(nc, es, "yq")
        self.xnf = es.enter_context(nc.sbuf_tensor("xnf", [128, NC * T], BF16))
        self.g = None if no_g else es.enter_context(
            nc.sbuf_tensor("g", [128, NT * D], F32))
        self.consts = {}
        for nm, k, sh, dt in CONSTS:
            if need is not None and nm not in need:
                setattr(self, nm, None)
                continue
            tile = es.enter_context(nc.sbuf_tensor(nm, sh, dt))
            self.consts[nm] = tile
            setattr(self, nm, tile)

    def ensure_consts(self, need, es):
        C = self
        missing = [c for c in CONSTS if c[0] in need and c[0] not in self.consts]
        if not missing:
            return
        for nm, k, sh, dt in missing:
            tile = es.enter_context(self.nc.sbuf_tensor(nm, sh, dt))
            self.consts[nm] = tile
            setattr(self, nm, tile)

        def _drop():
            for nm, k, sh, dt in missing:
                self.consts.pop(nm, None)
                setattr(self, nm, None)
        es.callback(_drop)

        @on(C.blk, 'sync')
        def _(q):
            for nm, k, sh, dt in missing:
                C.xq.inc(q.dma_start(out=C.consts[nm][:], in_=C.win[k]), 16)
        C.barrier()

    def fv(self, val):
        return self.c_fv[:, FIDX[val]:FIDX[val] + 1]

    def xnf3(self):
        return self.xnf[:].rearrange("p (c t) -> p c t", c=NC)

    def emit_const_loads(self):
        C = self

        @on(C.blk, 'sync')
        def _(q):
            for nm, k, sh, dt in CONSTS:
                if nm in C.consts:
                    C.xq.inc(q.dma_start(out=C.consts[nm][:], in_=C.win[k]), 16)
        C.barrier()

    def barrier(self, engs=("sync", "gpsimd", "vector", "scalar", "tensor")):
        C = self
        vals = [(s.h, s.n) for s in
                (C.xq, C.wq, C.pe, C.dv, C.ac, C.sq, C.sv, C.se, C.yq)
                if s.n > 0]
        for eng in engs:
            @on(C.blk, eng)
            def _(q):
                for h, n in vals:
                    q.wait_ge(h, n)

    def cast_load(self, q, dst_ap, src_ap):
        self.wq.inc(q.dma_start(out=dst_ap, in_=src_ap), 16)

    def close(self):
        self.es.__exit__(None, None, None)


def emit_ln_transpose(C, lo_dst=None):
    if lo_dst is not None:
        return emit_ln_transpose_lo(C, lo_dst)
    nc, blk = C.nc, C.blk
    es2 = contextlib.ExitStack()
    xfull = es2.enter_context(nc.sbuf_tensor("ln_xfull", [128, NT * 1024], F32))
    xntok = es2.enter_context(nc.sbuf_tensor("ln_xntok", [128, NT * 1024], BF16))
    scrF = es2.enter_context(nc.sbuf_tensor("ln_scr", [128, 1024], F32))
    st = es2.enter_context(nc.sbuf_tensor("ln_stats", [128, 6 * NT], F32))
    pts = [es2.enter_context(nc.psum_tensor(f"ln_pt{j}", [128, 1024], BF16))
           for j in range(2)]
    NS = NT
    ac0, dv0, xq0 = C.ac.n, C.dv.n, C.xq.n

    @on(blk, 'sync')
    def _(q):
        C.xq.inc(q.dma_start(
            out=xfull[:].rearrange("p (n d) -> p n d", d=1024),
            in_=C.x.rearrange("(n p) d -> p n d", p=128)), 16)

    @on(blk, 'scalar')
    def _(q):
        q.wait_ge(C.xq.h, xq0 + 16)
        for i in range(NT):
            xt = xfull[:, i * 1024:(i + 1) * 1024]
            C.ac.inc(q.activation(out=scrF[:], in_=xt, func=AF.Square,
                                  accum_out=st[:, 2 * NS + i:2 * NS + i + 1]))
            q.wait_ge(C.dv.h, dv0 + 5 * i + 4)
            q.activation(out=st[:, 3 * NS + i:3 * NS + i + 1],
                         in_=st[:, 3 * NS + i:3 * NS + i + 1],
                         func=AF.Ln, bias=C.fv(1e-5))
            C.ac.inc(q.activation(out=st[:, 4 * NS + i:4 * NS + i + 1],
                                  in_=st[:, 3 * NS + i:3 * NS + i + 1],
                                  func=AF.Exp, scale=C.fv(-0.5)))
            q.wait_ge(C.dv.h, dv0 + 5 * i + 5)
            C.ac.inc(q.activation(out=xntok[:, i * 1024:(i + 1) * 1024], in_=xt,
                                  func=AF.Identity,
                                  bias=st[:, 5 * NS + i:5 * NS + i + 1],
                                  scale=st[:, 4 * NS + i:4 * NS + i + 1]))

    @on(blk, 'vector')
    def _(q):
        q.wait_ge(C.xq.h, xq0 + 16)
        for i in range(NT):
            xt = xfull[:, i * 1024:(i + 1) * 1024]
            q.wait_ge(C.dv.h, C.dv.inc(
                q.reduce_sum(st[:, i:i + 1], xt, axis=AX.X, negate=True)))
            q.wait_ge(C.dv.h, C.dv.inc(
                q.tensor_scalar_mul(st[:, NS + i:NS + i + 1],
                                    st[:, i:i + 1], 1.0 / D)))
            q.wait_ge(C.ac.h, ac0 + 3 * i + 1)
            q.tensor_mul(st[:, i:i + 1], st[:, NS + i:NS + i + 1],
                         st[:, NS + i:NS + i + 1])
            q.wait_ge(C.dv.h, C.dv.inc(
                q.tensor_scalar_mul(st[:, 3 * NS + i:3 * NS + i + 1],
                                    st[:, 2 * NS + i:2 * NS + i + 1], 1.0 / D)))
            C.dv.inc(q.tensor_sub(st[:, 3 * NS + i:3 * NS + i + 1],
                                  st[:, 3 * NS + i:3 * NS + i + 1],
                                  st[:, i:i + 1]))
            q.wait_ge(C.ac.h, ac0 + 3 * i + 2)
            C.dv.inc(q.tensor_mul(st[:, 5 * NS + i:5 * NS + i + 1],
                                  st[:, NS + i:NS + i + 1],
                                  st[:, 4 * NS + i:4 * NS + i + 1]))

    sq0, sv0 = C.sq.n, C.sv.n

    @on(blk, 'tensor')
    def _(q):
        for i in range(NT):
            q.wait_ge(C.ac.h, ac0 + 3 * i + 3)
            if i >= 2:
                q.wait_ge(C.sv.h, sv0 + i - 1)
            pt = pts[i % 2]
            last = None
            for c in range(NC):
                last = q.transpose(
                    pt[:, c * 128:(c + 1) * 128],
                    xntok[:, i * 1024 + c * 128:i * 1024 + (c + 1) * 128],
                    C.c_ident[:])
            C.sq.inc(last)

    @on(blk, 'vector')
    def _(q):
        for i in range(NT):
            q.wait_ge(C.sq.h, sq0 + i + 1)
            C.sv.inc(q.tensor_copy(
                C.xnf3()[:, :, i * 128:(i + 1) * 128],
                pts[i % 2][:].rearrange("p (c t) -> p c t", c=NC)))

    C.barrier()
    es2.close()


def emit_ln_transpose_lo(C, lo_dst):
    nc, blk = C.nc, C.blk
    es2 = contextlib.ExitStack()
    xfull = es2.enter_context(nc.sbuf_tensor("ln_xfull", [128, NT * 1024], F32))
    xntok = es2.enter_context(nc.sbuf_tensor("ln_xntok", [128, NT * 1024], BF16))
    xlotok = es2.enter_context(nc.sbuf_tensor("ln_xlotok", [128, 1024], BF16))
    scrF = es2.enter_context(nc.sbuf_tensor("ln_scr", [128, 1024], F32))
    scr2 = es2.enter_context(nc.sbuf_tensor("ln_scr2", [128, 1024], F32))
    st = es2.enter_context(nc.sbuf_tensor("ln_stats", [128, 6 * NT], F32))
    pts = [es2.enter_context(nc.psum_tensor(f"ln_pt{j}", [128, 1024], BF16))
           for j in range(2)]
    NS = NT
    ac0, dv0, xq0 = C.ac.n, C.dv.n, C.xq.n
    sq0, sv0 = C.sq.n, C.sv.n

    @on(blk, 'sync')
    def _(q):
        C.xq.inc(q.dma_start(
            out=xfull[:].rearrange("p (n d) -> p n d", d=1024),
            in_=C.x.rearrange("(n p) d -> p n d", p=128)), 16)

    @on(blk, 'scalar')
    def _(q):
        q.wait_ge(C.xq.h, xq0 + 16)
        for i in range(NT):
            xt = xfull[:, i * 1024:(i + 1) * 1024]
            C.ac.inc(q.activation(out=scrF[:], in_=xt, func=AF.Square,
                                  accum_out=st[:, 2 * NS + i:2 * NS + i + 1]))
            q.wait_ge(C.dv.h, dv0 + 6 * i + 4)
            q.activation(out=st[:, 3 * NS + i:3 * NS + i + 1],
                         in_=st[:, 3 * NS + i:3 * NS + i + 1],
                         func=AF.Ln, bias=C.fv(1e-5))
            C.ac.inc(q.activation(out=st[:, 4 * NS + i:4 * NS + i + 1],
                                  in_=st[:, 3 * NS + i:3 * NS + i + 1],
                                  func=AF.Exp, scale=C.fv(-0.5)))
            q.wait_ge(C.dv.h, dv0 + 6 * i + 5)
            C.ac.inc(q.activation(out=xntok[:, i * 1024:(i + 1) * 1024], in_=xt,
                                  func=AF.Identity,
                                  bias=st[:, 5 * NS + i:5 * NS + i + 1],
                                  scale=st[:, 4 * NS + i:4 * NS + i + 1]))
            if i >= 1:
                q.wait_ge(C.dv.h, dv0 + 6 * i)
            C.ac.inc(q.activation(out=scr2[:], in_=xt, func=AF.Identity,
                                  bias=st[:, 5 * NS + i:5 * NS + i + 1],
                                  scale=st[:, 4 * NS + i:4 * NS + i + 1]))

    def emit_evac(q, j):
        q.wait_ge(C.sq.h, sq0 + 2 * j + 1)
        C.sv.inc(q.tensor_copy(
            C.xnf3()[:, :, j * 128:(j + 1) * 128],
            pts[0][:].rearrange("p (c t) -> p c t", c=NC)))
        q.wait_ge(C.sq.h, sq0 + 2 * j + 2)
        C.sv.inc(q.tensor_copy(
            lo_dst[:].rearrange("p (c t) -> p c t", c=NC)
            [:, :, j * 128:(j + 1) * 128],
            pts[1][:].rearrange("p (c t) -> p c t", c=NC)))

    @on(blk, 'vector')
    def _(q):
        q.wait_ge(C.xq.h, xq0 + 16)
        for i in range(NT):
            xt = xfull[:, i * 1024:(i + 1) * 1024]
            q.wait_ge(C.dv.h, C.dv.inc(
                q.reduce_sum(st[:, i:i + 1], xt, axis=AX.X, negate=True)))
            q.wait_ge(C.dv.h, C.dv.inc(
                q.tensor_scalar_mul(st[:, NS + i:NS + i + 1],
                                    st[:, i:i + 1], 1.0 / D)))
            q.wait_ge(C.ac.h, ac0 + 4 * i + 1)
            q.tensor_mul(st[:, i:i + 1], st[:, NS + i:NS + i + 1],
                         st[:, NS + i:NS + i + 1])
            q.wait_ge(C.dv.h, C.dv.inc(
                q.tensor_scalar_mul(st[:, 3 * NS + i:3 * NS + i + 1],
                                    st[:, 2 * NS + i:2 * NS + i + 1], 1.0 / D)))
            C.dv.inc(q.tensor_sub(st[:, 3 * NS + i:3 * NS + i + 1],
                                  st[:, 3 * NS + i:3 * NS + i + 1],
                                  st[:, i:i + 1]))
            q.wait_ge(C.ac.h, ac0 + 4 * i + 2)
            C.dv.inc(q.tensor_mul(st[:, 5 * NS + i:5 * NS + i + 1],
                                  st[:, NS + i:NS + i + 1],
                                  st[:, 4 * NS + i:4 * NS + i + 1]))
            if i >= 1:
                emit_evac(q, i - 1)
            q.wait_ge(C.ac.h, ac0 + 4 * i + 4)
            C.dv.inc(q.tensor_sub(xlotok[:], scr2[:],
                                  xntok[:, i * 1024:(i + 1) * 1024]))
        emit_evac(q, NT - 1)

    @on(blk, 'tensor')
    def _(q):
        for i in range(NT):
            q.wait_ge(C.ac.h, ac0 + 4 * i + 3)
            gidx = 2 * i
            if gidx >= 2:
                q.wait_ge(C.sv.h, sv0 + gidx - 1)
            pt = pts[0]
            last = None
            for c in range(NC):
                last = q.transpose(
                    pt[:, c * 128:(c + 1) * 128],
                    xntok[:, i * 1024 + c * 128:i * 1024 + (c + 1) * 128],
                    C.c_ident[:])
            C.sq.inc(last)
            q.wait_ge(C.dv.h, dv0 + 6 * i + 6)
            gidx = 2 * i + 1
            if gidx >= 2:
                q.wait_ge(C.sv.h, sv0 + gidx - 1)
            pt = pts[1]
            last = None
            for c in range(NC):
                last = q.transpose(
                    pt[:, c * 128:(c + 1) * 128],
                    xlotok[:, c * 128:(c + 1) * 128],
                    C.c_ident[:])
            C.sq.inc(last)

    C.barrier()
    es2.close()


def emit_l0_ffn(C):
    nc, blk = C.nc, C.blk
    es2 = contextlib.ExitStack()
    wbase = es2.enter_context(nc.sbuf_tensor("wbase", [128, NC * 1024], BF16))
    wrout = es2.enter_context(nc.sbuf_tensor("wrout", [128, NC * 8], BF16))
    wlA = es2.enter_context(nc.sbuf_tensor("wlA", [128, NC * 8], BF16))
    wlB2 = es2.enter_context(nc.sbuf_tensor("wlB2", [64, 1024], BF16))
    r_baseb = es2.enter_context(nc.sbuf_tensor("r_baseb", [1, 1024], BF16))
    r_routb = es2.enter_context(nc.sbuf_tensor("r_routb", [1, 8], BF16))
    gate = es2.enter_context(nc.sbuf_tensor("gate", [128, 16], BF16))
    tsb = es2.enter_context(nc.sbuf_tensor("tsb", [128, 64], BF16))
    tTall = es2.enter_context(nc.sbuf_tensor("tTall", [64, NT * 128], BF16))
    fst = es2.enter_context(nc.sbuf_tensor("fst", [128, 4], F32))
    gatef = es2.enter_context(nc.sbuf_tensor("gatef", [128, 8], F32))
    pff = [es2.enter_context(nc.psum_tensor(f"pff{i}", [128, 512], F32))
           for i in range(4)]
    pr8 = es2.enter_context(nc.psum_tensor("pr8", [128, 8], F32))
    pm8 = es2.enter_context(nc.psum_tensor("pm8", [128, 8], F32))
    ptT = es2.enter_context(nc.psum_tensor("ptT", [64, 128], BF16))

    @on(blk, 'gpsimd')
    def _(q):
        C.cast_load(q, wbase[:].rearrange("p (c o) -> p c o", c=NC),
                    C.win["l0_basewT"].rearrange("(c p) o -> p c o", p=128))
        C.cast_load(q, wrout[:].rearrange("p (c o) -> p c o", c=NC),
                    C.win["l0_routerT"].rearrange("(c p) o -> p c o", p=128))
        C.cast_load(q, wlA[:].rearrange("p (c o) -> p c o", c=NC),
                    C.win["l0_loraA"].rearrange("(c p) o -> p c o", p=128))
        C.cast_load(q, wlB2[:], C.win["l0_loraB"])
        C.cast_load(q, r_baseb[:], C.win["l0_baseb"].unsqueeze(0))
        C.cast_load(q, r_routb[:], C.win["l0_routerb"].unsqueeze(0))
    wq_ff = C.wq.n

    @on(blk, 'vector')
    def _(q):
        q.wait_ge(C.wq.h, wq_ff)
        C.dv.inc(q.tensor_scalar_mul(wlB2[:], wlB2[:], 2.0))
    dv_lb2 = C.dv.n

    pe0, dv0, ac0 = C.pe.n, C.dv.n, C.ac.n

    @on(blk, 'tensor')
    def _(q):
        q.wait_ge(C.wq.h, wq_ff)
        q.wait_ge(C.dv.h, dv_lb2)
        for i in range(NT):
            if i >= 1:
                q.wait_ge(C.ac.h, ac0 + i)
                q.wait_ge(C.dv.h, dv0 + 6 * (i - 1) + 6)
            for c in range(NC):
                lhs = C.xnf3()[:, c, i * 128:(i + 1) * 128]
                q.matmul(pr8[:], lhs, wrout[:, c * 8:(c + 1) * 8],
                         start=(c == 0), stop=False)
                q.matmul(pm8[:], lhs, wlA[:, c * 8:(c + 1) * 8],
                         start=(c == 0), stop=(c == NC - 1))
            C.pe.inc(q.matmul(pr8[:], C.c_or[:1, :], r_routb[:1, :],
                              start=False, stop=True))
            q.wait_ge(C.dv.h, dv0 + 6 * i + 5)
            C.pe.inc(q.transpose(ptT[:], tsb[:], C.c_ident[:]))

    @on(blk, 'vector')
    def _(q):
        for i in range(NT):
            q.wait_ge(C.pe.h, pe0 + 2 * i + 1)
            C.dv.inc(q.reduce_max(fst[:, 0:1], pr8[:], axis=AX.X, negate=True))
            q.wait_ge(C.ac.h, ac0 + i + 1)
            q.wait_ge(C.dv.h, C.dv.inc(q.reciprocal(fst[:, 2:3], fst[:, 1:2])))
            q.wait_ge(C.dv.h, C.dv.inc(
                q.tensor_scalar_mul(gatef[:], gatef[:], fst[:, 2:3])))
            q.tensor_copy(gate[:, 0:8], gatef[:])
            q.wait_ge(C.dv.h, C.dv.inc(q.tensor_copy(gate[:, 8:16], pm8[:])))
            u = gate[:, 0:8].unsqueeze(2).broadcast_to([128, 8, 8])
            v = gate[:, 8:16].unsqueeze(1).broadcast_to([128, 8, 8])
            C.dv.inc(q.tensor_tensor(tsb[:].rearrange("p (e r) -> p e r", e=8),
                                     u, v, ALU.mult))
            q.wait_ge(C.pe.h, pe0 + 2 * i + 2)
            C.dv.inc(q.tensor_copy(tTall[:, i * 128:(i + 1) * 128], ptT[:]))

    @on(blk, 'scalar')
    def _(q):
        for i in range(NT):
            q.wait_ge(C.dv.h, dv0 + 6 * i + 1)
            C.ac.inc(q.activation(out=gatef[:], in_=pr8[:], func=AF.Exp,
                                  bias=fst[:, 0:1], accum_out=fst[:, 1:2]))

    C.barrier(("tensor", "scalar"))

    pe0, ac0 = C.pe.n, C.ac.n

    @on(blk, 'tensor')
    def _(q):
        for i in range(NT):
            if i >= 2:
                q.wait_ge(C.ac.h, ac0 + (i - 1))
            pa, pb = pff[2 * (i % 2)], pff[2 * (i % 2) + 1]
            for c in range(NC):
                lhs = C.xnf3()[:, c, i * 128:(i + 1) * 128]
                stf = dict(start=(c == 0), stop=False)
                q.matmul(pa[:], lhs, wbase[:, c * 1024:c * 1024 + 512], **stf)
                q.matmul(pb[:], lhs, wbase[:, c * 1024 + 512:(c + 1) * 1024], **stf)
            tt = tTall[:, i * 128:(i + 1) * 128]
            q.matmul(pa[:], tt, wlB2[:, 0:512], start=False, stop=False)
            q.matmul(pb[:], tt, wlB2[:, 512:1024], start=False, stop=False)
            q.matmul(pa[:], C.c_or[:1, :], r_baseb[:1, 0:512], start=False, stop=True)
            C.pe.inc(q.matmul(pb[:], C.c_or[:1, :], r_baseb[:1, 512:1024],
                              start=False, stop=True))

    @on(blk, 'scalar')
    def _(q):
        for i in range(NT):
            q.wait_ge(C.pe.h, pe0 + i + 1)
            q.activation(out=C.g[:, i * 1024:i * 1024 + 512],
                         in_=pff[2 * (i % 2)][:], func=AF.Silu)
            C.ac.inc(q.activation(out=C.g[:, i * 1024 + 512:(i + 1) * 1024],
                                  in_=pff[2 * (i % 2) + 1][:], func=AF.Silu))

    C.barrier()
    es2.close()


def emit_l1_ffn(C):
    nc, blk = C.nc, C.blk
    es2 = contextlib.ExitStack()
    wbuf = es2.enter_context(nc.sbuf_tensor("l1w", [128, NC * 1024], BF16))
    h1 = es2.enter_context(nc.sbuf_tensor("l1h1", [128, NC * T], BF16))
    h2 = es2.enter_context(nc.sbuf_tensor("l1h2", [128, NC * T], BF16))
    r_b = es2.enter_context(nc.sbuf_tensor("l1rb", [1, 1024], BF16))
    tmp = es2.enter_context(nc.sbuf_tensor("l1tmp", [128, 512], F32))
    pf = [es2.enter_context(nc.psum_tensor(f"l1p{j}", [128, 512], F32))
          for j in range(4)]

    def load_w(k, first):
        pe_now = C.pe.n

        @on(blk, 'gpsimd')
        def _(q):
            if not first:
                q.wait_ge(C.pe.h, pe_now)
            C.cast_load(q, wbuf[:].rearrange("p (c o) -> p c o", c=NC),
                        C.win[k].rearrange("(c p) o -> p c o", p=128))
            if first:
                C.cast_load(q, r_b[:], C.win["l1_ffb"].unsqueeze(0))
        return C.wq.n

    def feat_stage(wsb, src3, dst, pe_base, dv_base, wq0):
        @on(blk, 'tensor')
        def _(q):
            q.wait_ge(C.wq.h, wq0)
            u = 0
            for dtile in range(NC):
                for tch in range(4):
                    if u >= 4:
                        q.wait_ge(C.dv.h, dv_base + u - 3)
                    last = None
                    for c in range(NC):
                        last = q.matmul(
                            pf[u % 4][:], wsb[:, c * 1024 + dtile * 128:
                                              c * 1024 + (dtile + 1) * 128],
                            src3[:, c, tch * 512:(tch + 1) * 512],
                            start=(c == 0), stop=(c == NC - 1))
                    C.pe.inc(last)
                    u += 1

        @on(blk, 'vector')
        def _(q):
            dst3 = dst[:].rearrange("p (c t) -> p c t", c=NC)
            u = 0
            for dtile in range(NC):
                for tch in range(4):
                    q.wait_ge(C.pe.h, pe_base + u + 1)
                    C.dv.inc(q.tensor_copy(
                        dst3[:, dtile, tch * 512:(tch + 1) * 512], pf[u % 4][:]))
                    u += 1

    wq0 = load_w("l1_ffw", True)
    feat_stage(wbuf, C.xnf3(), h1, C.pe.n, C.dv.n, wq0)
    wq0 = load_w("l1_ffa1", False)
    feat_stage(wbuf, h1[:].rearrange("p (c t) -> p c t", c=NC), h2,
               C.pe.n, C.dv.n, wq0)
    wq0 = load_w("l1_ffa2", False)

    pe0, ac0 = C.pe.n, C.ac.n
    h23 = h2[:].rearrange("p (c t) -> p c t", c=NC)

    @on(blk, 'tensor')
    def _(q):
        q.wait_ge(C.wq.h, wq0)
        u = 0
        for i in range(NT):
            for oc in range(2):
                if u >= 4:
                    q.wait_ge(C.ac.h, ac0 + u - 3)
                last = None
                for c in range(NC):
                    last = q.matmul(pf[u % 4][:], h23[:, c, i * 128:(i + 1) * 128],
                                    wbuf[:, c * 1024 + oc * 512:
                                         c * 1024 + (oc + 1) * 512],
                                    start=(c == 0), stop=False)
                last = q.matmul(pf[u % 4][:], C.c_or[:1, :],
                                r_b[:1, oc * 512:(oc + 1) * 512],
                                start=False, stop=True)
                C.pe.inc(last)
                u += 1

    @on(blk, 'scalar')
    def _(q):
        u = 0
        for i in range(NT):
            for oc in range(2):
                q.wait_ge(C.pe.h, pe0 + u + 1)
                q.activation(out=tmp[:], in_=pf[u % 4][:], func=AF.Tanh)
                C.ac.inc(q.activation(
                    out=C.g[:, i * 1024 + oc * 512:i * 1024 + (oc + 1) * 512],
                    in_=tmp[:], func=AF.Silu))
                u += 1

    C.barrier()
    es2.close()


def body_layer1(C):
    esl = contextlib.ExitStack()
    C.ensure_consts(CONST_NEED[1], esl)
    C.g = esl.enter_context(C.nc.sbuf_tensor("g1", [128, NT * D], F32))
    with C.nc.named_scope("l1_ln"):
        emit_ln_transpose(C)
    with C.nc.named_scope("l1_ffn"):
        emit_l1_ffn(C)
    aof = esl.enter_context(C.nc.sbuf_tensor("aof1", [128, NC * T], BF16))
    with C.nc.named_scope("l1_attn"):
        emit_st_attention(C, 1, aof)
    with C.nc.named_scope("l1_comb"):
        emit_combine(C, aof, "l1_WOT", "l1_bO")
        C.barrier()
    esl.close()


def build_layer1(W, dbg=False, stop_after=None, npairs=H // 2, substage=None):
    C = LayerProg(W, LAYER_KEYS[1], no_g=True, need=CONST_NEED[1])
    C._keep_attn_scope = dbg
    C.emit_const_loads()
    body_layer1(C)
    C.close()
    return C


def emit_l2_ffn(C, g_dram):
    nc, blk = C.nc, C.blk
    es2 = contextlib.ExitStack()
    FH = 4096
    NFC = FH // 128
    hA = es2.enter_context(nc.sbuf_tensor("l2hA", [128, NFC * T], BF16))
    stg = es2.enter_context(nc.sbuf_tensor("l2stg", [128, 1024], BF16))
    hB_dram = nc.dram_tensor("l2_hscr", [FH, T], BF16).ap()
    scol = es2.enter_context(nc.sbuf_tensor("l2scol", [128, 8], F32))
    s3row = es2.enter_context(nc.sbuf_tensor("l2s3r", [1, 1024], F32))
    s3b = es2.enter_context(nc.sbuf_tensor("l2s3b", [128, 1024], F32))
    tmp2 = es2.enter_context(nc.sbuf_tensor("l2tmp2", [128, 1024], F32))
    pp = [es2.enter_context(nc.psum_tensor(f"l2pp{j}", [128, 512], F32))
          for j in range(4)]
    es3 = contextlib.ExitStack()
    w12 = [es3.enter_context(nc.sbuf_tensor(f"l2w12_{j}", [128, NC * 2 * 128], BF16))
           for j in range(2)]
    whs = [es3.enter_context(nc.sbuf_tensor(f"l2whs{j}", [128, NFC * 128], BF16))
           for j in range(2)]
    tmp = es3.enter_context(nc.sbuf_tensor("l2tmp", [128, 1024], F32))

    hA3 = hA[:].rearrange("p (c t) -> p c t", c=NFC)

    pe0, ac0, dv0, wq0 = C.pe.n, C.ac.n, C.dv.n, C.wq.n

    @on(blk, 'gpsimd')
    def _(q):
        for m in range(NFC):
            if m >= 2:
                q.wait_ge(C.pe.h, pe0 + 4 * (m - 1))
            wv = w12[m % 2][:].rearrange("p (c j o) -> p c j o", c=NC, j=2)
            C.cast_load(q, wv[:, :, 0, :],
                        C.win["l2_w1T"][:, m * 128:(m + 1) * 128]
                        .rearrange("(c p) o -> p c o", p=128))
            C.cast_load(q, wv[:, :, 1, :],
                        C.win["l2_w2T"][:, m * 128:(m + 1) * 128]
                        .rearrange("(c p) o -> p c o", p=128))

    @on(blk, 'sync')
    def _(q):
        for m in range(NFC):
            if m >= 2:
                q.wait_ge(C.ac.h, ac0 + 4 * (m - 1))
                q.wait_ge(C.dv.h, dv0 + 4 * (m - 1))
            C.xq.inc(q.dma_start(
                out=scol[:, 2 * (m % 2):2 * (m % 2) + 1],
                in_=C.win["l2_w1_s"][m * 128:(m + 1) * 128].unsqueeze(1)), 16)
            C.xq.inc(q.dma_start(
                out=scol[:, 2 * (m % 2) + 1:2 * (m % 2) + 2],
                in_=C.win["l2_w2_s"][m * 128:(m + 1) * 128].unsqueeze(1)), 16)
    xq0 = C.xq.n

    @on(blk, 'tensor')
    def _(q):
        for m in range(NFC):
            q.wait_ge(C.wq.h, wq0 + 32 * (m + 1))
            wv = w12[m % 2][:].rearrange("p (c j o) -> p c j o", c=NC, j=2)
            for tc in range(4):
                u = 4 * m + tc
                if u >= 2:
                    q.wait_ge(C.dv.h, dv0 + u - 1)
                for c in range(NC):
                    q.matmul(pp[2 * (u % 2)][:], wv[:, c, 0, :],
                             C.xnf3()[:, c, tc * 512:(tc + 1) * 512],
                             start=(c == 0), stop=(c == NC - 1))
                for c in range(NC):
                    last = q.matmul(pp[2 * (u % 2) + 1][:], wv[:, c, 1, :],
                                    C.xnf3()[:, c, tc * 512:(tc + 1) * 512],
                                    start=(c == 0), stop=(c == NC - 1))
                C.pe.inc(last)

    @on(blk, 'scalar')
    def _(q):
        for m in range(NFC):
            for tc in range(4):
                u = 4 * m + tc
                q.wait_ge(C.pe.h, pe0 + u + 1)
                if tc == 0:
                    q.wait_ge(C.xq.h, xq0 - 16 * 2 * (NFC - 1 - m))
                if u >= 2:
                    q.wait_ge(C.dv.h, dv0 + u - 1)
                C.ac.inc(q.activation(out=tmp[:, (u % 2) * 512:(u % 2) * 512 + 512],
                                      in_=pp[2 * (u % 2)][:],
                                      func=AF.Silu,
                                      scale=scol[:, 2 * (m % 2):2 * (m % 2) + 1]))

    @on(blk, 'vector')
    def _(q):
        for m in range(NFC):
            for tc in range(4):
                u = 4 * m + tc
                q.wait_ge(C.ac.h, ac0 + u + 1)
                C.dv.inc(q.scalar_tensor_tensor(
                    hA3[:, m, tc * 512:(tc + 1) * 512],
                    pp[2 * (u % 2) + 1][:],
                    scol[:, 2 * (m % 2) + 1:2 * (m % 2) + 2],
                    tmp[:, (u % 2) * 512:(u % 2) * 512 + 512],
                    ALU.mult, ALU.mult))

    C.barrier()

    def mid_stage(wkey, skey):
        pe0, ac0, wq0, yq0 = C.pe.n, C.ac.n, C.wq.n, C.yq.n

        @on(blk, 'gpsimd')
        def _(q):
            for m in range(NFC):
                if m >= 2:
                    q.wait_ge(C.pe.h, pe0 + 4 * (m - 1))
                C.cast_load(q, whs[m % 2][:].rearrange("p (c o) -> p c o", c=NFC),
                            C.win[wkey][:, m * 128:(m + 1) * 128]
                            .rearrange("(c p) o -> p c o", p=128))

        scol_xq = []

        def store_u(q, u):
            m2, tc2 = u // 4, u % 4
            q.wait_ge(C.ac.h, ac0 + u + 1)
            C.yq.inc(q.dma_start(
                out=hB_dram[m2 * 128:(m2 + 1) * 128,
                            tc2 * 512:(tc2 + 1) * 512],
                in_=stg[:, (u % 2) * 512:(u % 2) * 512 + 512]), 16)

        @on(blk, 'sync')
        def _(q):
            for u in range(4 * NFC):
                if u % 4 == 0:
                    m = u // 4
                    if m >= 2:
                        q.wait_ge(C.ac.h, ac0 + 4 * (m - 1))
                    C.xq.inc(q.dma_start(
                        out=scol[:, 4 + (m % 2):5 + (m % 2)],
                        in_=C.win[skey][m * 128:(m + 1) * 128].unsqueeze(1)), 16)
                    scol_xq.append(C.xq.n)
                if u >= 4:
                    store_u(q, u - 4)
            for u in range(4 * NFC - 4, 4 * NFC):
                store_u(q, u)
            q.wait_ge(C.yq.h, C.yq.n)

        @on(blk, 'tensor')
        def _(q):
            for m in range(NFC):
                q.wait_ge(C.wq.h, wq0 + 16 * (m + 1))
                ws = whs[m % 2][:].rearrange("p (c o) -> p c o", c=NFC)
                for tc in range(4):
                    u = 4 * m + tc
                    if u >= 4:
                        q.wait_ge(C.ac.h, ac0 + u - 3)
                    last = None
                    for c in range(NFC):
                        last = q.matmul(pp[u % 4][:], ws[:, c, :],
                                        hA3[:, c, tc * 512:(tc + 1) * 512],
                                        start=(c == 0), stop=(c == NFC - 1))
                    C.pe.inc(last)

        @on(blk, 'scalar')
        def _(q):
            for m in range(NFC):
                for tc in range(4):
                    u = 4 * m + tc
                    q.wait_ge(C.pe.h, pe0 + u + 1)
                    if tc == 0:
                        q.wait_ge(C.xq.h, scol_xq[m])
                    if u >= 2:
                        q.wait_ge(C.yq.h, yq0 + 16 * (u - 1))
                    C.ac.inc(q.activation(
                        out=stg[:, (u % 2) * 512:(u % 2) * 512 + 512],
                        in_=pp[u % 4][:], func=AF.Gelu,
                        scale=scol[:, 4 + (m % 2):5 + (m % 2)]))

        @on(blk, 'sync')
        def _(q):
            for m in range(NFC):
                for tc in range(4):
                    u = 4 * m + tc
                    q.wait_ge(C.ac.h, ac0 + u + 1)
                    C.yq.inc(q.dma_start(
                        out=hB_dram[m * 128:(m + 1) * 128,
                                    tc * 512:(tc + 1) * 512],
                        in_=stg[:, (u % 2) * 512:(u % 2) * 512 + 512]), 16)
            q.wait_ge(C.yq.h, C.yq.n)
        C.barrier()

        @on(blk, 'sync')
        def _(q):
            C.xq.inc(q.dma_start(
                out=hA3, in_=hB_dram.rearrange("(c p) t -> p c t", p=128)), 16)
        C.barrier()

    mid_stage("l2_h1T", "l2_h1_s")
    mid_stage("l2_h2T", "l2_h2_s")
    es3.close()

    pe0, ac0, dv0, wq0 = C.pe.n, C.ac.n, C.dv.n, C.wq.n
    sq0 = C.sq.n

    @on(blk, 'sync')
    def _(q):
        C.xq.inc(q.dma_start(out=s3row[:],
                             in_=C.win["l2_w3_s"].unsqueeze(0)), 16)
    xq_s3 = C.xq.n

    @on(blk, 'tensor')
    def _(q):
        q.wait_ge(C.xq.h, xq_s3)
        for half in range(2):
            C.sq.inc(q.matmul(pp[3][:], C.c_orf[0:1, :],
                              s3row[0:1, half * 512:(half + 1) * 512],
                              start=True, stop=True))
            q.wait_ge(C.dv.h, dv0 + half + 1)

    @on(blk, 'vector')
    def _(q):
        for half in range(2):
            q.wait_ge(C.sq.h, sq0 + half + 1)
            C.dv.inc(q.tensor_copy(s3b[:, half * 512:(half + 1) * 512], pp[3][:]))
    dv_s3 = C.dv.n

    NOC = 4
    OW = 1024 // NOC
    w3h = es2.enter_context(nc.sbuf_tensor("l2w3h", [128, NFC * OW], BF16))
    gsb = es2.enter_context(nc.sbuf_tensor("l2gsb", [128, 2 * OW], F32))
    w3h3 = w3h[:].rearrange("p (c o) -> p c o", c=NFC)
    pe4, ac4, dv4, yq0 = C.pe.n, C.ac.n, C.dv.n, C.yq.n

    @on(blk, 'gpsimd')
    def _(q):
        for oc in range(NOC):
            if oc >= 1:
                q.wait_ge(C.pe.h, pe4 + oc * NT)
            C.cast_load(q, w3h3,
                        C.win["l2_w3T"][:, oc * OW:(oc + 1) * OW]
                        .rearrange("(c p) o -> p c o", p=128))
    wq_w3 = C.wq.n

    @on(blk, 'tensor')
    def _(q):
        for u in range(NOC * NT):
            oc, i = u // NT, u % NT
            q.wait_ge(C.wq.h, wq_w3 - 16 * (NOC - 1 - oc))
            if u >= 2:
                q.wait_ge(C.dv.h, dv_s3 + u - 1)
            last = None
            for c in range(NFC):
                last = q.matmul(pp[u % 2][:, 0:OW],
                                hA3[:, c, i * 128:(i + 1) * 128],
                                w3h3[:, c, :],
                                start=(c == 0), stop=(c == NFC - 1))
            C.pe.inc(last)

    @on(blk, 'vector')
    def _(q):
        for u in range(NOC * NT):
            oc, i = u // NT, u % NT
            q.wait_ge(C.pe.h, pe4 + u + 1)
            if u >= 2:
                q.wait_ge(C.ac.h, ac4 + u - 1)
            C.dv.inc(q.tensor_mul(tmp2[:, (u % 2) * OW:(u % 2) * OW + OW],
                                  pp[u % 2][:, 0:OW],
                                  s3b[:, oc * OW:(oc + 1) * OW]))

    @on(blk, 'scalar')
    def _(q):
        for u in range(NOC * NT):
            oc, i = u // NT, u % NT
            q.wait_ge(C.dv.h, dv_s3 + u + 1)
            if u >= 2:
                q.wait_ge(C.yq.h, yq0 + 16 * (u - 1))
            C.ac.inc(q.activation(
                out=gsb[:, (u % 2) * OW:(u % 2) * OW + OW],
                in_=tmp2[:, (u % 2) * OW:(u % 2) * OW + OW], func=AF.Silu))

    @on(blk, 'sync')
    def _(q):
        for u in range(NOC * NT):
            oc, i = u // NT, u % NT
            q.wait_ge(C.ac.h, ac4 + u + 1)
            C.yq.inc(q.dma_start(
                out=g_dram[i * 128:(i + 1) * 128, oc * OW:(oc + 1) * OW],
                in_=gsb[:, (u % 2) * OW:(u % 2) * OW + OW]), 16)
        q.wait_ge(C.yq.h, C.yq.n)

    C.barrier()
    es2.close()


def emit_l3_ffn(C, xnlo=None, es_lo=None, g_es=None, msktm=None,
                scol=None, rtm=None):
    nc, blk = C.nc, C.blk
    es2 = contextlib.ExitStack()
    FE = 2048
    NEF = FE // 128
    TH = 1024

    msk3 = msktm[:].rearrange("p (t e) -> p t e", t=NT)
    rtm3 = rtm[:].rearrange("p (t e) -> p t e", t=NT)
    es3 = contextlib.ExitStack()
    wr32 = es3.enter_context(nc.sbuf_tensor("l3wr32", [128, NC * 16], F32))
    wr = es3.enter_context(nc.sbuf_tensor("l3wr", [128, NC * 16], BF16))
    wrl = es3.enter_context(nc.sbuf_tensor("l3wrl", [128, NC * 16], BF16))
    rst = es3.enter_context(nc.sbuf_tensor("l3rst", [128, 8], F32))
    oh1 = es3.enter_context(nc.sbuf_tensor("l3oh1", [128, 16], F32))
    pr = es3.enter_context(nc.psum_tensor("l3pr", [128, 16], F32))

    @on(blk, 'sync')
    def _(q):
        C.xq.inc(q.dma_start(
            out=wr32[:].rearrange("p (c o) -> p c o", c=NC),
            in_=C.win["l3_routerT"].rearrange("(c p) o -> p c o", p=128)), 16)
    xq_r = C.xq.n

    @on(blk, 'vector')
    def _(q):
        q.wait_ge(C.xq.h, xq_r)
        q.wait_ge(C.dv.h, C.dv.inc(q.tensor_copy(wr[:], wr32[:])))
        C.dv.inc(q.tensor_sub(wrl[:], wr32[:], wr[:]))
    wq0 = C.wq.n
    dv_wr = C.dv.n
    pe0, dv0, ac0, sq0, sv0 = C.pe.n, C.dv.n, C.ac.n, C.sq.n, C.sv.n

    @on(blk, 'tensor')
    def _(q):
        q.wait_ge(C.dv.h, dv_wr)
        for i in range(NT):
            if i >= 1:
                q.wait_ge(C.dv.h, dv0 + 7 * i)
            last = None
            for c in range(NC):
                q.matmul(pr[:], C.xnf3()[:, c, i * 128:(i + 1) * 128],
                         wr[:, c * 16:(c + 1) * 16],
                         start=(c == 0), stop=False)
            for c in range(NC):
                q.matmul(pr[:], C.xnf3()[:, c, i * 128:(i + 1) * 128],
                         wrl[:, c * 16:(c + 1) * 16],
                         start=False, stop=False)
            for c in range(NC):
                last = q.matmul(
                    pr[:],
                    xnlo[:].rearrange("p (c t) -> p c t", c=NC)
                    [:, c, i * 128:(i + 1) * 128],
                    wr[:, c * 16:(c + 1) * 16],
                    start=False, stop=(c == NC - 1))
            C.pe.inc(last)

    @on(blk, 'vector')
    def _(q):
        for i in range(NT):
            q.wait_ge(C.pe.h, pe0 + i + 1)
            q.wait_ge(C.dv.h, C.dv.inc(
                q.reduce_max(rst[:, 0:1], pr[:], axis=AX.X)))
            q.wait_ge(C.dv.h, C.dv.inc(
                q.tensor_scalar(oh1[:], pr[:], rst[:, 0:1], None, ALU.is_ge)))
            q.wait_ge(C.dv.h, C.dv.inc(
                q.tensor_mul(oh1[:], oh1[:], C.c_iota64m[:])))
            q.wait_ge(C.dv.h, C.dv.inc(
                q.tensor_reduce(rst[:, 1:2], oh1[:], AX.X, ALU.min)))
            q.wait_ge(C.dv.h, C.dv.inc(
                q.tensor_scalar_add(rst[:, 1:2], rst[:, 1:2], 64.0)))
            q.wait_ge(C.ac.h, ac0 + i + 1)
            C.dv.inc(q.tensor_scalar(msk3[:, i, :], C.c_iota[:], rst[:, 1:2],
                                     None, ALU.is_equal))
            C.dv.inc(q.tensor_copy(scol[:, i:i + 1], rst[:, 2:3]))

    @on(blk, 'scalar')
    def _(q):
        for i in range(NT):
            q.wait_ge(C.dv.h, dv0 + 7 * i + 1)
            C.ac.inc(q.activation(out=rst[:, 2:3], in_=rst[:, 0:1],
                                  func=AF.Sigmoid))
    C.barrier()
    es3.close()
    if es_lo is not None:
        es_lo.close()
    if C.g is None:
        C.g = (g_es or C.es).enter_context(
            nc.sbuf_tensor("g3", [128, NT * D], F32))

    escs = contextlib.ExitStack()
    prc = [escs.enter_context(nc.psum_tensor(f"l3prc{j}", [128, 512], F32))
           for j in range(2)]
    pecs, dvcs = C.pe.n, C.dv.n

    @on(blk, 'tensor')
    def _(q):
        for i in range(NT):
            if i >= 2:
                q.wait_ge(C.dv.h, dvcs + 2 * (i - 1))
            last = None
            for j in range(i + 1):
                last = q.matmul(prc[i % 2][:, 0:16],
                                (C.c_ut if j == i else C.c_ao)[:],
                                msk3[:, j, :], start=(j == 0), stop=(j == i))
            C.pe.inc(last)

    @on(blk, 'vector')
    def _(q):
        for i in range(NT):
            q.wait_ge(C.pe.h, pecs + i + 1)
            q.wait_ge(C.dv.h, C.dv.inc(q.scalar_tensor_tensor(
                rtm3[:, i, :], prc[i % 2][:, 0:16], 1.0, msk3[:, i, :],
                ALU.mult, ALU.mult)))
            C.dv.inc(q.tensor_scalar_add(rtm3[:, i, :], rtm3[:, i, :], -1.0))
    C.barrier()
    escs.close()

    es4 = contextlib.ExitStack()
    hsh = es4.enter_context(nc.sbuf_tensor("l3hsh", [128, NEF * T], BF16))
    wsl = [es4.enter_context(nc.sbuf_tensor(f"l3wsl{j}", [128, NC * 128], BF16))
           for j in range(2)]
    pg = [es4.enter_context(nc.psum_tensor(f"l3pg{j}", [128, 512], F32))
          for j in range(4)]
    hsh3 = hsh[:].rearrange("p (c t) -> p c t", c=NEF)

    def sh_pass(wkey, second):
        pe1, ac1, dv1, wq1 = C.pe.n, C.ac.n, C.dv.n, C.wq.n

        @on(blk, 'gpsimd')
        def _(q):
            for m in range(NEF):
                if m >= 2:
                    q.wait_ge(C.pe.h, pe1 + 4 * (m - 1))
                C.cast_load(q, wsl[m % 2][:].rearrange("p (c o) -> p c o", c=NC),
                            C.win[wkey][:, m * 128:(m + 1) * 128]
                            .rearrange("(c p) o -> p c o", p=128))

        @on(blk, 'tensor')
        def _(q):
            for m in range(NEF):
                q.wait_ge(C.wq.h, wq1 + 16 * (m + 1))
                ws = wsl[m % 2][:].rearrange("p (c o) -> p c o", c=NC)
                for tc in range(4):
                    u = 4 * m + tc
                    if u >= 4:
                        if second:
                            q.wait_ge(C.dv.h, dv1 + u - 3)
                        else:
                            q.wait_ge(C.ac.h, ac1 + u - 3)
                    last = None
                    for c in range(NC):
                        last = q.matmul(pg[u % 4][:], ws[:, c, :],
                                        C.xnf3()[:, c, tc * 512:(tc + 1) * 512],
                                        start=(c == 0), stop=(c == NC - 1))
                    C.pe.inc(last)

        if not second:
            @on(blk, 'scalar')
            def _(q):
                for u in range(4 * NEF):
                    q.wait_ge(C.pe.h, pe1 + u + 1)
                    m, tc = u // 4, u % 4
                    C.ac.inc(q.activation(
                        out=hsh3[:, m, tc * 512:(tc + 1) * 512],
                        in_=pg[u % 4][:], func=AF.Silu))
        else:
            @on(blk, 'vector')
            def _(q):
                for u in range(4 * NEF):
                    q.wait_ge(C.pe.h, pe1 + u + 1)
                    m, tc = u // 4, u % 4
                    C.dv.inc(q.scalar_tensor_tensor(
                        hsh3[:, m, tc * 512:(tc + 1) * 512],
                        pg[u % 4][:], 1.0,
                        hsh3[:, m, tc * 512:(tc + 1) * 512],
                        ALU.mult, ALU.mult))
        C.barrier()

    sh_pass("l3_wgT", False)
    sh_pass("l3_wuT", True)

    wdl = es4.enter_context(nc.sbuf_tensor("l3wdl", [128, NEF * 512], BF16))
    pe2, dv2, wq2 = C.pe.n, C.dv.n, C.wq.n

    @on(blk, 'gpsimd')
    def _(q):
        for oc in range(2):
            if oc == 1:
                q.wait_ge(C.pe.h, pe2 + NT)
            C.cast_load(q, wdl[:].rearrange("p (c o) -> p c o", c=NEF),
                        C.win["l3_wdT"][:, oc * 512:(oc + 1) * 512]
                        .rearrange("(c p) o -> p c o", p=128))
    wq3 = C.wq.n
    wdl3 = wdl[:].rearrange("p (c o) -> p c o", c=NEF)

    @on(blk, 'tensor')
    def _(q):
        for u in range(2 * NT):
            oc, i = u // NT, u % NT
            q.wait_ge(C.wq.h, wq3 - 16 * (1 - oc))
            if u >= 2:
                q.wait_ge(C.dv.h, dv2 + u - 1)
            last = None
            for c in range(NEF):
                last = q.matmul(pg[u % 2][:], hsh3[:, c, i * 128:(i + 1) * 128],
                                wdl3[:, c, :], start=(c == 0), stop=(c == NEF - 1))
            C.pe.inc(last)

    @on(blk, 'vector')
    def _(q):
        for u in range(2 * NT):
            oc, i = u // NT, u % NT
            q.wait_ge(C.pe.h, pe2 + u + 1)
            C.dv.inc(q.tensor_copy(
                C.g[:, i * 1024 + oc * 512:i * 1024 + (oc + 1) * 512],
                pg[u % 2][:]))
    C.barrier()
    es4.close()

    CAP = 384
    NSC = CAP // 128
    es5 = contextlib.ExitStack()
    xntm = es5.enter_context(nc.sbuf_tensor("l3xntm", [128, NT * 1024], BF16))
    xg = es5.enter_context(nc.sbuf_tensor("l3xg", [128, 8 * CAP], BF16))
    yg = es5.enter_context(nc.sbuf_tensor("l3yg", [128, NSC * 1024], BF16))
    wge = [es5.enter_context(nc.sbuf_tensor(f"l3wge{j}", [128, NC * 512], BF16))
           for j in range(2)]
    wde = es5.enter_context(nc.sbuf_tensor("l3wde", [128, 16 * 512], BF16))
    ptp = [es5.enter_context(nc.psum_tensor(f"l3ptp{j}", [128, 1024], BF16))
           for j in range(2)]
    pga = [es5.enter_context(nc.psum_tensor(f"l3pga{j}", [128, 512], F32))
           for j in range(2)]
    pm = [es5.enter_context(nc.psum_tensor(f"l3pm{j}", [128, 512], F32))
          for j in range(4)]
    xnt3 = xntm[:].rearrange("p (t f) -> p t f", t=NT)
    xg3 = xg[:].rearrange("p (c s) -> p c s", c=8)
    yg3 = yg[:].rearrange("p (s f) -> p s f", s=NSC)
    wde3 = wde[:].rearrange("p (c o) -> p c o", c=16)

    sqX0, svX0 = C.sq.n, C.sv.n

    @on(blk, 'tensor')
    def _(q):
        for g in range(32):
            if g >= 2:
                q.wait_ge(C.sv.h, svX0 + g - 1)
            last = None
            for s in range(4):
                tt, fc = g // 2, (g % 2) * 4 + s
                last = q.transpose(ptp[g % 2][:, s * 128:(s + 1) * 128],
                                   C.xnf3()[:, fc, tt * 128:(tt + 1) * 128],
                                   C.c_ident[:])
            C.sq.inc(last)

    @on(blk, 'vector')
    def _(q):
        for g in range(32):
            q.wait_ge(C.sq.h, sqX0 + g + 1)
            tt, fc0 = g // 2, (g % 2) * 4
            C.sv.inc(q.tensor_copy(
                xnt3[:, tt, fc0 * 128:(fc0 + 4) * 128],
                ptp[g % 2][:, 0:512]))
    C.barrier()

    prev_scat_pe = None
    prev_gup_pe = None
    for e in range(16):
        es_e = contextlib.ExitStack()
        PT = es_e.enter_context(
            nc.sbuf_tensor("l3PT", [128, NSC * T], BF16))
        PT3 = PT[:].rearrange("p (s t) -> p s t", s=NSC)
        esP = contextlib.ExitStack()
        P = esP.enter_context(nc.sbuf_tensor("l3P", [128, NT * CAP], BF16))
        P3 = P[:].rearrange("p (t s) -> p t s", t=NT)

        dvPb0 = C.dv.n

        @on(blk, 'vector')
        def _(q, e=e, psc=prev_scat_pe):
            if psc is not None:
                q.wait_ge(C.pe.h, psc)
            for tt in range(NT):
                C.dv.inc(q.tensor_scalar(
                    P3[:, tt, :], C.c_icap[:, 0:CAP], rtm3[:, tt, e:e + 1],
                    None, ALU.is_equal))
        dvP01 = C.dv.n

        sqT0, svT0 = C.sq.n, C.sv.n

        @on(blk, 'tensor')
        def _(q, e=e):
            q.wait_ge(C.dv.h, dvP01)
            for g in range(4 * NSC):
                if g >= 2:
                    q.wait_ge(C.sv.h, svT0 + g - 1)
                last = None
                for s in range(4):
                    b = 4 * g + s
                    sc, tt = b // NT, b % NT
                    last = q.transpose(
                        ptp[g % 2][:, s * 128:(s + 1) * 128],
                        P3[:, tt, sc * 128:(sc + 1) * 128], C.c_ident[:])
                C.sq.inc(last)

        @on(blk, 'vector')
        def _(q, e=e):
            for g in range(4 * NSC):
                q.wait_ge(C.sq.h, sqT0 + g + 1)
                sc, tt0 = (4 * g) // NT, (4 * g) % NT
                C.sv.inc(q.tensor_copy(
                    PT3[:, sc, tt0 * 128:(tt0 + 4) * 128],
                    ptp[g % 2][:, 0:512]))
        svT1 = C.sv.n

        sqT1 = C.sq.n

        @on(blk, 'vector')
        def _(q, e=e):
            q.wait_ge(C.sq.h, sqT1)
            for tt in range(NT):
                C.dv.inc(q.tensor_scalar(
                    P3[:, tt, :], P3[:, tt, :], scol[:, tt:tt + 1],
                    None, ALU.mult))
        dvPb1 = C.dv.n

        peG0 = C.pe.n

        @on(blk, 'tensor')
        def _(q, e=e):
            q.wait_ge(C.dv.h, dvPb1)
            for ft in range(8):
                if ft >= 2:
                    q.wait_ge(C.dv.h, dvPb1 + ft - 1)
                last = None
                for tt in range(NT):
                    last = q.matmul(pga[ft % 2][:, 0:CAP],
                                    xnt3[:, tt, ft * 128:(ft + 1) * 128],
                                    P3[:, tt, :],
                                    start=(tt == 0), stop=(tt == NT - 1))
                C.pe.inc(last)

        @on(blk, 'vector')
        def _(q, e=e):
            for ft in range(8):
                q.wait_ge(C.pe.h, peG0 + ft + 1)
                C.dv.inc(q.tensor_copy(xg3[:, ft, :], pga[ft % 2][:, 0:CAP]))
        esP.close()
        hge = es_e.enter_context(
            nc.sbuf_tensor("l3hge", [128, 16 * CAP], BF16))
        hge3 = hge[:].rearrange("p (h s) -> p h s", h=16)

        peU0, acU0, wqU0 = C.pe.n, C.ac.n, C.wq.n
        dvUp = dvPb1 + 8

        @on(blk, 'gpsimd')
        def _(q, e=e, pgu=prev_gup_pe):
            for w8 in range(8):
                if w8 < 2:
                    if pgu is not None:
                        q.wait_ge(C.pe.h, pgu)
                else:
                    q.wait_ge(C.pe.h, peU0 + 4 * (w8 - 1))
                C.cast_load(
                    q, wge[w8 % 2][:].rearrange("p (c o) -> p c o", c=NC),
                    C.win["l3_gup"][e][:, w8 * 512:(w8 + 1) * 512]
                    .rearrange("(c p) o -> p c o", p=128))

        @on(blk, 'tensor')
        def _(q, e=e):
            q.wait_ge(C.dv.h, dvPb1 + 8)
            for w8 in range(8):
                q.wait_ge(C.wq.h, wqU0 + 16 * (w8 + 1))
                wg3 = wge[w8 % 2][:].rearrange("p (c o) -> p c o", c=NC)
                for sub in range(4):
                    u = 4 * w8 + sub
                    if u >= 4:
                        if w8 < 4:
                            q.wait_ge(C.ac.h, acU0 + u - 3)
                        else:
                            q.wait_ge(C.ac.h, acU0 + 16)
                            q.wait_ge(C.dv.h, dvUp + max(0, u - 19))
                    last = None
                    for c in range(NC):
                        last = q.matmul(
                            pm[u % 4][:, 0:CAP],
                            wg3[:, c, sub * 128:(sub + 1) * 128],
                            xg3[:, c, :], start=(c == 0), stop=(c == NC - 1))
                    C.pe.inc(last)
        prev_gup_pe = C.pe.n

        @on(blk, 'scalar')
        def _(q, e=e):
            for u in range(16):
                q.wait_ge(C.pe.h, peU0 + u + 1)
                C.ac.inc(q.activation(out=hge3[:, u, :],
                                      in_=pm[u % 4][:, 0:CAP], func=AF.Silu))

        @on(blk, 'vector')
        def _(q, e=e):
            for k in range(16):
                q.wait_ge(C.pe.h, peU0 + 16 + k + 1)
                q.wait_ge(C.ac.h, acU0 + k + 1)
                C.dv.inc(q.scalar_tensor_tensor(
                    hge3[:, k, :], pm[(16 + k) % 4][:, 0:CAP], 1.0,
                    hge3[:, k, :], ALU.mult, ALU.mult))

        peD0, wqD0 = C.pe.n, C.wq.n
        dvD0 = dvUp + 16

        @on(blk, 'gpsimd')
        def _(q, e=e):
            for oc in range(2):
                if oc == 0:
                    q.wait_ge(C.pe.h, peU0)
                else:
                    q.wait_ge(C.pe.h, peD0 + NSC)
                C.cast_load(
                    q, wde3, C.win["l3_down"][e][:, oc * 512:(oc + 1) * 512]
                    .rearrange("(c p) o -> p c o", p=128))

        @on(blk, 'tensor')
        def _(q, e=e):
            q.wait_ge(C.dv.h, dvD0)
            q.wait_ge(C.ac.h, acU0 + 16)
            for u in range(2 * NSC):
                oc, st = u // NSC, u % NSC
                q.wait_ge(C.wq.h, wqD0 + 16 * (oc + 1))
                if u >= 2:
                    q.wait_ge(C.dv.h, dvD0 + u - 1)
                last = None
                for hc in range(16):
                    last = q.matmul(
                        pga[u % 2][:], hge3[:, hc, st * 128:(st + 1) * 128],
                        wde3[:, hc, :], start=(hc == 0), stop=(hc == 15))
                C.pe.inc(last)

        @on(blk, 'vector')
        def _(q, e=e):
            for u in range(2 * NSC):
                oc, st = u // NSC, u % NSC
                q.wait_ge(C.pe.h, peD0 + u + 1)
                C.dv.inc(q.tensor_copy(yg3[:, st, oc * 512:(oc + 1) * 512],
                                       pga[u % 2][:]))

        peS0 = C.pe.n
        dvS0 = dvD0 + 2 * NSC

        @on(blk, 'tensor')
        def _(q, e=e):
            q.wait_ge(C.dv.h, dvS0)
            q.wait_ge(C.sv.h, svT1)
            for g in range(32):
                tt, nh = g // 2, g % 2
                if g >= 4:
                    q.wait_ge(C.dv.h, dvS0 + g - 3)
                last = None
                for sc in range(NSC):
                    last = q.matmul(
                        pm[g % 4][:], PT3[:, sc, tt * 128:(tt + 1) * 128],
                        yg3[:, sc, nh * 512:(nh + 1) * 512],
                        start=(sc == 0), stop=(sc == NSC - 1))
                C.pe.inc(last)
        prev_scat_pe = C.pe.n

        @on(blk, 'vector')
        def _(q, e=e):
            for g in range(32):
                tt, nh = g // 2, g % 2
                q.wait_ge(C.pe.h, peS0 + g + 1)
                gsl = C.g[:, tt * 1024 + nh * 512:tt * 1024 + (nh + 1) * 512]
                C.dv.inc(q.tensor_add(gsl, gsl, pm[g % 4][:]))
        es_e.close()

    C.barrier()
    es5.close()

    @on(blk, 'scalar')
    def _(q):
        for i in range(NT):
            C.ac.inc(q.activation(out=C.g[:, i * 1024:(i + 1) * 1024],
                                  in_=C.g[:, i * 1024:(i + 1) * 1024],
                                  func=AF.Silu))
    C.barrier()
    if getattr(C, "_dump_g3", False):
        dgt = nc.dram_tensor("dbg_g3", [128, NT * 1024], F32,
                             kind="ExternalOutput").ap()
        drt = nc.dram_tensor("dbg_rtm", [128, NT * 16], F32,
                             kind="ExternalOutput").ap()
        dsc = nc.dram_tensor("dbg_scol", [128, NT], F32,
                             kind="ExternalOutput").ap()

        @on(blk, 'sync')
        def _(q):
            C.yq.inc(q.dma_start(out=dgt, in_=C.g[:]), 16)
            C.yq.inc(q.dma_start(out=drt, in_=rtm[:]), 16)
            C.yq.inc(q.dma_start(out=dsc, in_=scol[:]), 16)
            q.wait_ge(C.yq.h, C.yq.n)
    es2.close()


def emit_l3_attn(C, aof):
    nc, blk = C.nc, C.blk
    es2 = contextlib.ExitStack()
    EPS = 1e-6
    aot = es2.enter_context(nc.sbuf_tensor("l3aot", [128, NT * D], BF16))
    wp = es2.enter_context(nc.sbuf_tensor("l3wp", [128, NC * 128], BF16))
    qf = es2.enter_context(nc.sbuf_tensor("l3qf", [128, T], BF16))
    kf = es2.enter_context(nc.sbuf_tensor("l3kf", [128, T], BF16))
    vf = es2.enter_context(nc.sbuf_tensor("l3vf", [128, T], BF16))
    qt = es2.enter_context(nc.sbuf_tensor("l3qt", [128, NT * 128], BF16))
    kt = es2.enter_context(nc.sbuf_tensor("l3kt", [128, NT * 128], BF16))
    vt = es2.enter_context(nc.sbuf_tensor("l3vt", [128, NT * 128], BF16))
    bcol = es2.enter_context(nc.sbuf_tensor("l3bcol", [128, 4], F32))
    sums = es2.enter_context(nc.sbuf_tensor("l3sums", [128, 4], F32))
    sumb = es2.enter_context(nc.sbuf_tensor("l3sumb", [128, 4], BF16))
    tcol = es2.enter_context(nc.sbuf_tensor("l3tcol", [128, 4], F32))
    tcolb = es2.enter_context(nc.sbuf_tensor("l3tcolb", [128, 4], BF16))
    esc = es2.enter_context(nc.sbuf_tensor("l3esc", [1, 8], F32))
    escb = es2.enter_context(nc.sbuf_tensor("l3escb", [1, 8], BF16))
    stat = es2.enter_context(nc.sbuf_tensor("l3stat", [128, 4 * NT], F32))
    stb = es2.enter_context(nc.sbuf_tensor("l3stb", [128, 2 * NT], BF16))
    csr = es2.enter_context(nc.sbuf_tensor("l3csr", [1, T], F32))
    csb = vf
    rr = es2.enter_context(nc.sbuf_tensor("l3rr", [1, 4], F32))
    sct = es2.enter_context(nc.sbuf_tensor("l3sct", [128, NT], F32))
    vsb = es2.enter_context(nc.sbuf_tensor("l3vsb", [128, 2 * 64], BF16))
    kvb = es2.enter_context(nc.sbuf_tensor("l3kvb", [128, 64], BF16))
    pA = [es2.enter_context(nc.psum_tensor(f"l3pA{j}", [128, 1024], F32))
          for j in range(2)]
    pB = es2.enter_context(nc.psum_tensor("l3pB", [128, 512], F32))
    pS = es2.enter_context(nc.psum_tensor("l3pS", [128, 512], BF16))
    pD = es2.enter_context(nc.psum_tensor("l3pD", [128, 512], F32))
    pE = es2.enter_context(nc.psum_tensor("l3pE", [128, 512], F32))

    wnames = ("l3_Wq", "l3_Wk", "l3_Wv")
    bnames = ("l3_bq", "l3_bk", "l3_bv")

    for p in range(H // 2):
        wv3 = wp[:].rearrange("p (c o) -> p c o", c=NC)
        sqld = C.sq.n

        @on(blk, 'gpsimd')
        def _(q, p=p, sqn=C.sq.n):
            if p >= 1:
                q.wait_ge(C.sq.h, sqn)
            for j in range(3):
                if j >= 1:
                    q.wait_ge(C.sq.h, sqld + 2 * j)
                C.cast_load(q, wv3,
                            C.win[wnames[j]][:, p * 128:(p + 1) * 128]
                            .rearrange("(c q) o -> q c o", q=128))

        @on(blk, 'sync')
        def _(q, p=p, acn=C.ac.n):
            if p >= 1:
                q.wait_ge(C.ac.h, acn)
            for j in range(3):
                C.xq.inc(q.dma_start(out=bcol[:, j:j + 1],
                                     in_=C.win[bnames[j]][p * 128:(p + 1) * 128]
                                     .unsqueeze(1)), 16)
        wq_p, xq_p = C.wq.n, C.xq.n
        sq0, ac0 = C.sq.n, C.ac.n

        @on(blk, 'tensor')
        def _(q, p=p, sen=C.se.n, svn=C.sv.n, dvn=C.dv.n):
            if p >= 1:
                q.wait_ge(C.sv.h, svn)
                q.wait_ge(C.dv.h, dvn)
            for u in range(6):
                j, hv = u // 2, u % 2
                q.wait_ge(C.wq.h, wq_p - 16 * (2 - j))
                if u >= 2:
                    q.wait_ge(C.ac.h, ac0 + u - 1)
                dst = pA[u % 2]
                last = None
                for half in range(2):
                    for c in range(NC):
                        last = q.matmul(
                            dst[:, half * 512:(half + 1) * 512],
                            wv3[:, c, :],
                            C.xnf3()[:, c, hv * 1024 + half * 512:
                                     hv * 1024 + (half + 1) * 512],
                            start=(c == 0), stop=(c == NC - 1))
                C.sq.inc(last)

        @on(blk, 'scalar')
        def _(q, p=p):
            q.wait_ge(C.xq.h, xq_p)
            for u in range(6):
                j, hv = u // 2, u % 2
                tgt = (qf, kf, vf)[j]
                q.wait_ge(C.sq.h, sq0 + u + 1)
                fn = AF.Sigmoid if j < 2 else AF.Identity
                C.ac.inc(q.activation(out=tgt[:, hv * 1024:(hv + 1) * 1024],
                                      in_=pA[u % 2][:], func=fn,
                                      bias=bcol[:, j:j + 1]))
        ac_pr = C.ac.n

        sq1, sv1 = C.sq.n, C.sv.n

        @on(blk, 'tensor')
        def _(q, p=p):
            q.wait_ge(C.ac.h, ac_pr)
            g = 0
            for (srcT, dstT) in ((qf, qt), (kf, kt), (vf, vt)):
                for grp in range(4):
                    if g >= 1:
                        q.wait_ge(C.sv.h, sv1 + g)
                    last = None
                    for s in range(4):
                        i = grp * 4 + s
                        last = q.transpose(pS[:, s * 128:(s + 1) * 128],
                                           srcT[:, i * 128:(i + 1) * 128],
                                           C.c_ident[:])
                    C.sq.inc(last)
                    g += 1

        @on(blk, 'vector')
        def _(q, p=p):
            g = 0
            for (srcT, dstT) in ((qf, qt), (kf, kt), (vf, vt)):
                for grp in range(4):
                    q.wait_ge(C.sq.h, sq1 + g + 1)
                    C.sv.inc(q.tensor_copy(
                        dstT[:, grp * 512:(grp + 1) * 512], pS[:]))
                    g += 1
        sv_tr = C.sv.n

        @on(blk, 'vector')
        def _(q, p=p):
            q.wait_ge(C.ac.h, ac_pr)
            q.wait_ge(C.dv.h, C.dv.inc(
                q.reduce_sum(sums[:, 0:1], kf[:], axis=AX.X)))
            q.wait_ge(C.dv.h, C.dv.inc(
                q.tensor_scalar_add(sums[:, 0:1], sums[:, 0:1], EPS)))
            q.wait_ge(C.dv.h, C.dv.inc(
                q.reduce_sum(sums[:, 1:2], qf[:], axis=AX.X)))
            q.wait_ge(C.dv.h, C.dv.inc(
                q.tensor_scalar_add(sums[:, 1:2], sums[:, 1:2], EPS)))
            C.dv.inc(q.tensor_copy(sumb[:, 0:2], sums[:, 0:2]))
        dv_sums = C.dv.n

        for hh in range(2):
            h = 2 * p + hh
            base = hh * 64

            sq2 = C.sq.n

            @on(blk, 'tensor')
            def _(q, hh=hh, base=base):
                q.wait_ge(C.dv.h, dv_sums)
                q.wait_ge(C.sv.h, sv_tr)
                C.sq.inc(q.matmul(pD[0:1, 20:21], sumb[base:base + 64, 0:1],
                                  C.c_oc[base:base + 64, 0:1],
                                  start=True, stop=True))
                C.sq.inc(q.matmul(pD[0:1, 21:22], sumb[base:base + 64, 1:2],
                                  C.c_oc[base:base + 64, 0:1],
                                  start=True, stop=True))

            @on(blk, 'vector')
            def _(q, hh=hh):
                q.wait_ge(C.sq.h, sq2 + 2)
                q.wait_ge(C.dv.h, C.dv.inc(
                    q.tensor_scalar_mul(esc[0:1, 0:2], pD[0:1, 20:22], EPS)))
                C.dv.inc(q.tensor_copy(escb[0:1, 0:2], esc[0:1, 0:2]))
            dv_es = C.dv.n

            sq3 = C.sq.n

            @on(blk, 'tensor')
            def _(q, hh=hh, base=base):
                q.wait_ge(C.dv.h, dv_es)
                last = None
                for i in range(NT):
                    q.matmul(pD[:, i:i + 1], qf[base:base + 64, i * 128:(i + 1) * 128],
                             sumb[base:base + 64, 0:1], start=True, stop=False)
                    last = q.matmul(pD[:, i:i + 1], C.c_or[:1, :],
                                    escb[0:1, 0:1], start=False, stop=True)
                C.sq.inc(last)
                last = None
                for i in range(NT):
                    q.matmul(pE[:, i:i + 1], kf[base:base + 64, i * 128:(i + 1) * 128],
                             sumb[base:base + 64, 1:2], start=True, stop=False)
                    last = q.matmul(pE[:, i:i + 1], C.c_or[:1, :],
                                    escb[0:1, 1:2], start=False, stop=True)
                C.sq.inc(last)

            @on(blk, 'vector')
            def _(q, hh=hh):
                q.wait_ge(C.sq.h, sq3 + 1)
                q.wait_ge(C.dv.h, C.dv.inc(
                    q.reciprocal(stat[:, 0:NT], pD[:, 0:NT])))
                q.wait_ge(C.dv.h, C.dv.inc(
                    q.tensor_copy(stb[:, 0:NT], stat[:, 0:NT])))
                q.wait_ge(C.sq.h, sq3 + 2)
                q.wait_ge(C.dv.h, C.dv.inc(
                    q.reciprocal(stat[:, 3 * NT:4 * NT], pE[:, 0:NT])))
                C.dv.inc(q.tensor_copy(stb[:, NT:2 * NT], stat[:, 3 * NT:4 * NT]))
            dv_sio = C.dv.n

            sq4 = C.sq.n

            @on(blk, 'tensor')
            def _(q, hh=hh, base=base):
                q.wait_ge(C.dv.h, dv_sio)
                last = None
                for i in range(NT):
                    last = q.matmul(pD[base:base + 64, 16:17],
                                    kt[:, i * 128 + base:i * 128 + base + 64],
                                    stb[:, NT + i:NT + i + 1],
                                    start=(i == 0), stop=(i == NT - 1))
                C.sq.inc(last)
                last = None
                for i in range(NT):
                    last = q.matmul(pD[base:base + 64, 17:18],
                                    qt[:, i * 128 + base:i * 128 + base + 64],
                                    stb[:, i:i + 1],
                                    start=(i == 0), stop=(i == NT - 1))
                C.sq.inc(last)

            @on(blk, 'vector')
            def _(q, hh=hh, base=base):
                q.wait_ge(C.sq.h, sq4 + 2)
                q.wait_ge(C.dv.h, C.dv.inc(
                    q.tensor_scalar_add(tcol[base:base + 64, 0:2],
                                        pD[base:base + 64, 16:18], EPS)))
                C.dv.inc(q.tensor_copy(tcolb[base:base + 64, 0:2],
                                       tcol[base:base + 64, 0:2]))
            dv_t = C.dv.n
            sq5 = C.sq.n

            @on(blk, 'tensor')
            def _(q, hh=hh, base=base):
                q.wait_ge(C.dv.h, dv_t)
                C.sq.inc(q.matmul(pE[0:1, 20:21], tcolb[base:base + 64, 0:1],
                                  C.c_oc[base:base + 64, 0:1],
                                  start=True, stop=True))
                C.sq.inc(q.matmul(pE[0:1, 21:22], tcolb[base:base + 64, 1:2],
                                  C.c_oc[base:base + 64, 0:1],
                                  start=True, stop=True))

            @on(blk, 'vector')
            def _(q, hh=hh):
                q.wait_ge(C.sq.h, sq5 + 2)
                q.wait_ge(C.dv.h, C.dv.inc(
                    q.tensor_scalar_mul(esc[0:1, 2:4], pE[0:1, 20:22], EPS)))
                C.dv.inc(q.tensor_copy(escb[0:1, 2:4], esc[0:1, 2:4]))
            dv_es2 = C.dv.n

            sq6 = C.sq.n

            @on(blk, 'tensor')
            def _(q, hh=hh, base=base):
                q.wait_ge(C.dv.h, dv_es2)
                last = None
                for i in range(NT):
                    q.matmul(pB[:, i:i + 1], qf[base:base + 64, i * 128:(i + 1) * 128],
                             tcolb[base:base + 64, 0:1], start=True, stop=False)
                    last = q.matmul(pB[:, i:i + 1], C.c_or[:1, :],
                                    escb[0:1, 2:3], start=False, stop=True)
                C.sq.inc(last)
            ac1 = C.ac.n

            @on(blk, 'scalar')
            def _(q, hh=hh):
                q.wait_ge(C.sq.h, sq6 + 1)
                C.ac.inc(q.activation(out=stat[:, NT:2 * NT], in_=pB[:, 0:NT],
                                      func=AF.Exp, scale=C.fv(-1.0)))

            @on(blk, 'vector')
            def _(q, hh=hh):
                q.wait_ge(C.ac.h, ac1 + 1)
                q.wait_ge(C.dv.h, C.dv.inc(
                    q.tensor_scalar_add(stat[:, NT:2 * NT],
                                        stat[:, NT:2 * NT], 1.0)))
                q.wait_ge(C.dv.h, C.dv.inc(
                    q.reciprocal(stat[:, NT:2 * NT], stat[:, NT:2 * NT])))
                C.dv.inc(q.tensor_mul(stat[:, 2 * NT:3 * NT], stat[:, 0:NT],
                                      stat[:, NT:2 * NT]))
            dv_sa = C.dv.n

            sq7 = C.sq.n

            @on(blk, 'tensor')
            def _(q, hh=hh, base=base, dvn=C.dv.n):
                q.wait_ge(C.dv.h, dvn)
                for ch in range(4):
                    if ch >= 2:
                        q.wait_ge(C.dv.h, dvn + ch - 1)
                    last = None
                    q.matmul(pA[ch % 2][0:1, 0:512],
                             tcolb[base:base + 64, 1:2],
                             kf[base:base + 64, ch * 512:(ch + 1) * 512],
                             start=True, stop=False)
                    for r4 in range(4):
                        last = q.matmul(pA[ch % 2][0:1, r4 * 128:(r4 + 1) * 128],
                                        escb[0:1, 3:4], C.c_or[0:1, :],
                                        start=False, stop=(r4 == 3))
                    C.sq.inc(last)

            @on(blk, 'vector')
            def _(q, hh=hh, dvn=C.dv.n):
                for ch in range(4):
                    q.wait_ge(C.sq.h, sq7 + ch + 1)
                    C.dv.inc(q.tensor_copy(csr[0:1, ch * 512:(ch + 1) * 512],
                                           pA[ch % 2][0:1, 0:512]))
                q.wait_ge(C.dv.h, C.dv.inc(
                    q.tensor_scalar(csr[0:1, :], csr[0:1, :], 1.0, -1.0,
                                    ALU.min, ALU.max)))
                C.dv.inc(q.reduce_max(rr[0:1, 0:1], csr[0:1, :], axis=AX.X,
                                      negate=True))
            dv_cs = C.dv.n
            ac2 = C.ac.n

            @on(blk, 'scalar')
            def _(q, hh=hh):
                q.wait_ge(C.dv.h, dv_cs)
                C.ac.inc(q.activation(out=csr[0:1, :], in_=csr[0:1, :],
                                      func=AF.Exp, bias=rr[0:1, 0:1],
                                      accum_out=rr[0:1, 1:2]))

            @on(blk, 'vector')
            def _(q, hh=hh):
                q.wait_ge(C.ac.h, ac2 + 1)
                q.wait_ge(C.dv.h, C.dv.inc(
                    q.reciprocal(rr[0:1, 2:3], rr[0:1, 1:2])))
                C.dv.inc(q.tensor_scalar(csb[0:1, :], csr[0:1, :],
                                         rr[0:1, 2:3], float(T),
                                         ALU.mult, ALU.mult))
            dv_sc = C.dv.n
            sq8 = C.sq.n

            @on(blk, 'tensor')
            def _(q, hh=hh):
                q.wait_ge(C.dv.h, dv_sc)
                last = None
                for i in range(NT):
                    last = q.transpose(pS[:, 2 * i:2 * i + 1],
                                       csb[0:1, i * 128:(i + 1) * 128],
                                       C.c_ident[0:1, 0:1])
                C.sq.inc(last)

            @on(blk, 'vector')
            def _(q, hh=hh):
                q.wait_ge(C.sq.h, sq8 + 1)
                C.dv.inc(q.tensor_copy(
                    sct[:, 0:NT],
                    bass.AP(pS.tensor if hasattr(pS, "tensor") else pS, 0,
                            [[512, 128], [2, NT]]) if False else
                    pS[:, 0:2 * NT].rearrange("p (t two) -> p t two", two=2)[:, :, 0]))
            dv_sct = C.dv.n

            sq9, dv_v0 = C.sq.n, C.dv.n

            @on(blk, 'vector')
            def _(q, hh=hh, base=base):
                q.wait_ge(C.dv.h, dv_sct)
                for i in range(NT):
                    if i >= 2:
                        q.wait_ge(C.sq.h, sq9 + i - 1)
                    C.dv.inc(q.tensor_scalar_mul(
                        vsb[:, (i % 2) * 64:(i % 2) * 64 + 64],
                        vt[:, i * 128 + base:i * 128 + base + 64],
                        sct[:, i:i + 1]))

            @on(blk, 'tensor')
            def _(q, hh=hh, base=base):
                for i in range(NT):
                    q.wait_ge(C.dv.h, dv_v0 + i + 1)
                    C.sq.inc(q.matmul(pE[base:base + 64, 32:96],
                                      kt[:, i * 128 + base:i * 128 + base + 64],
                                      vsb[:, (i % 2) * 64:(i % 2) * 64 + 64],
                                      start=(i == 0), stop=(i == NT - 1)))

            @on(blk, 'vector')
            def _(q, hh=hh, base=base):
                q.wait_ge(C.sq.h, C.sq.n)
                C.dv.inc(q.tensor_copy(kvb[base:base + 64, :],
                                       pE[base:base + 64, 32:96]))
            dv_kv = C.dv.n

            sqA = C.sq.n

            @on(blk, 'tensor')
            def _(q, hh=hh, base=base, dvn=C.dv.n):
                q.wait_ge(C.dv.h, dv_kv)
                for i in range(NT):
                    if i >= 2:
                        q.wait_ge(C.dv.h, dvn + i - 1)
                    C.sq.inc(q.matmul(pA[i % 2][:, 0:64],
                                      qf[base:base + 64, i * 128:(i + 1) * 128],
                                      kvb[base:base + 64, :],
                                      start=True, stop=True))

            @on(blk, 'vector')
            def _(q, hh=hh, h=h):
                for i in range(NT):
                    q.wait_ge(C.sq.h, sqA + i + 1)
                    C.dv.inc(q.tensor_scalar_mul(
                        aot[:, i * 1024 + h * 64:i * 1024 + h * 64 + 64],
                        pA[i % 2][:, 0:64],
                        stat[:, 2 * NT + i:2 * NT + i + 1]))

    C.barrier()
    sqT, svT = C.sq.n, C.sv.n

    @on(blk, 'tensor')
    def _(q):
        for i in range(NT):
            for gg in range(2):
                gidx = 2 * i + gg
                if gidx >= 1:
                    q.wait_ge(C.sv.h, svT + gidx)
                last = None
                for s in range(4):
                    c = gg * 4 + s
                    last = q.transpose(
                        pS[:, s * 128:(s + 1) * 128],
                        aot[:, i * 1024 + c * 128:i * 1024 + (c + 1) * 128],
                        C.c_ident[:])
                C.sq.inc(last)

    @on(blk, 'vector')
    def _(q):
        for i in range(NT):
            for gg in range(2):
                q.wait_ge(C.sq.h, sqT + 2 * i + gg + 1)
                C.sv.inc(q.tensor_copy(
                    aof[:].rearrange("p (c t) -> p c t", c=NC)
                    [:, gg * 4:(gg + 1) * 4, i * 128:(i + 1) * 128],
                    pS[:].rearrange("p (s t) -> p s t", s=4)))

    C.barrier()
    es2.close()


def emit_st_attention(C, layer, aof, npairs=H // 2, substage=None):
    nc, blk = C.nc, C.blk
    es2 = contextlib.ExitStack()
    wpair = es2.enter_context(nc.sbuf_tensor("wpair", [128, NC * 3 * 128], BF16))
    qp = es2.enter_context(nc.sbuf_tensor("qpair", [128, T], BF16))
    kp = es2.enter_context(nc.sbuf_tensor("kpair", [128, T], BF16))
    vf = es2.enter_context(nc.sbuf_tensor("vfeat", [128, T], BF16))
    va = [es2.enter_context(nc.sbuf_tensor(f"vaug{j}", [128, NT * 80], BF16))
          for j in range(2)]
    ptb = [es2.enter_context(nc.sbuf_tensor(f"ptb{j}", [128, 1024], BF16))
           for j in range(2)]
    scrb = es2.enter_context(nc.sbuf_tensor("scrb", [128, T], BF16))
    nrow = es2.enter_context(nc.sbuf_tensor("nrow", [1, T], F32))
    nbrow = es2.enter_context(nc.sbuf_tensor("nbrow", [1, T], BF16))
    zrow = es2.enter_context(nc.sbuf_tensor("zrow", [65, 1024], F32))
    zbsb = es2.enter_context(nc.sbuf_tensor("zbsb", [64, 512], F32))
    mvsb = es2.enter_context(nc.sbuf_tensor("mvsb", [64, 512], BF16))
    kst = es2.enter_context(nc.sbuf_tensor("kst", [1, 8], F32))
    kcol = es2.enter_context(nc.sbuf_tensor("kcol", [128, 1], BF16))
    bcol = es2.enter_context(nc.sbuf_tensor("bcol", [128, 4], F32))
    ps = [es2.enter_context(nc.psum_tensor(f"psc{j}", [128, 1024], F32))
          for j in range(2)]
    ppv = es2.enter_context(nc.psum_tensor("ppv", [65, 1024], F32))
    pzb = es2.enter_context(nc.psum_tensor("pzb", [64, 512], F32))
    ptv = es2.enter_context(nc.psum_tensor("ptv", [128, 512], BF16))

    QCH = 1024
    NQC = T // QCH
    scale = 0.125

    if layer == 1:
        bdall = es2.enter_context(nc.sbuf_tensor("bdall", [128, 16 * 128], BF16))
        hkb = es2.enter_context(nc.sbuf_tensor("hkb", [128, 16 * 128], BF16))
        tblf = es2.enter_context(nc.sbuf_tensor("tblf", [64, 1024], F32))
        tcolb = es2.enter_context(nc.sbuf_tensor("tcolb", [64, 1], BF16))

        @on(blk, 'vector')
        def _(q):
            C.dv.inc(q.memset(tblf[:], 0.0))
        dv_ms = C.dv.n

        @on(blk, 'sync')
        def _(q):
            q.wait_ge(C.dv.h, dv_ms)
            th = C.win["l1_tableT"].tensor
            C.xq.inc(q.dma_start(out=tblf[0:63, :],
                                 in_=bass.AP(th, 0, [[1024, 63], [1, 1024]])), 16)
            C.xq.inc(q.dma_start(out=tblf[63:64, 0:1008],
                                 in_=bass.AP(th, 63 * 1024, [[1008, 1], [1, 1008]])), 16)
        xq_t = C.xq.n

        @on(blk, 'vector')
        def _(q):
            q.wait_ge(C.xq.h, xq_t)
            C.dv.inc(q.reduce_max(tcolb[:], tblf[:], axis=AX.X,
                                  apply_absolute_value=True))
        dv_t = C.dv.n

        @on(blk, 'tensor')
        def _(q):
            q.wait_ge(C.dv.h, dv_t)
            C.pe.inc(q.transpose(ptv[0:1, 0:64], tcolb[:], C.c_ident[0:64, 0:64]))
        pe_t = C.pe.n

        @on(blk, 'vector')
        def _(q):
            q.wait_ge(C.pe.h, pe_t)
            q.wait_ge(C.dv.h, C.dv.inc(
                q.reduce_max(kst[0:1, 4:5], ptv[0:1, 0:64], axis=AX.X)))
            C.dv.inc(q.tensor_scalar_add(kst[0:1, 4:5], kst[0:1, 4:5], 1.0))
    else:
        @on(blk, 'vector')
        def _(q):
            C.dv.inc(q.memset(kst[0:1, 4:5], 1.0))

    @on(blk, 'vector')
    def _(q):
        q.memset(kst[:], 0.0)
        q.memset(nbrow[:], 0.0)
        q.memset(nrow[:], 0.0)
        for j in range(2):
            q.memset(va[j][:], 0.0)
            C.dv.inc(q.memset(
                va[j][:].rearrange("p (n e) -> p n e", e=80)[:, :, 64:65], 1.0))

    C.barrier()

    wsl = {0: ("l0_wqkvT",) * 3, 1: ("l1_WQT", "l1_WKT", "l1_WVT"),
           2: ("l2_WqT", "l2_WkT", "l2_WvT")}[layer]
    woff = (0, 1024, 2048) if layer == 0 else (0, 0, 0)

    def kranges(qc):
        if layer == 0:
            return list(range(0, min(NT, qc * 8 + 10)))
        if layer == 1:
            return list(range(0, qc * 8 + 8))
        return list(range(NT))

    for p in range(npairs):
        wv3 = wpair[:].rearrange("p (c j o) -> p c j o", c=NC, j=3)
        pe_prev, sv_prev, se_prev = C.pe.n, C.sv.n, C.se.n

        @on(blk, 'gpsimd')
        def _(q, p=p):
            if p >= 1:
                q.wait_ge(C.sq.h, C.sq.n)
            for j in range(3):
                C.cast_load(q, wv3[:, :, j, :],
                            C.win[wsl[j]][:, woff[j] + p * 128:woff[j] + (p + 1) * 128]
                            .rearrange("(c q) o -> q c o", q=128))

        if layer in (0, 1):
            @on(blk, 'sync')
            def _(q, p=p, acn=C.ac.n):
                if p >= 1:
                    q.wait_ge(C.ac.h, acn)
                if layer == 0:
                    b = C.win["l0_bqkv"]
                    srcs = [b[p * 128:(p + 1) * 128],
                            b[1024 + p * 128:1024 + (p + 1) * 128],
                            b[2048 + p * 128:2048 + (p + 1) * 128]]
                else:
                    srcs = [C.win[nm][p * 128:(p + 1) * 128]
                            for nm in ("l1_bQ", "l1_bK", "l1_bV")]
                for j in range(3):
                    C.xq.inc(q.dma_start(out=bcol[:, j:j + 1],
                                         in_=srcs[j].unsqueeze(1)), 16)

            @on(blk, 'vector')
            def _(q):
                q.wait_ge(C.xq.h, C.xq.n)
                C.dv.inc(q.tensor_scalar_mul(bcol[:, 0:1], bcol[:, 0:1], scale))

        if substage == "load":
            continue
        wq_p, dv_p, xq_p = C.wq.n, C.dv.n, C.xq.n
        sq0, ac0 = C.sq.n, C.ac.n

        NU = 6 if not (substage or "").startswith("proj") or substage == "proj" \
            else int(substage[4:])
        @on(blk, 'tensor')
        def _(q, p=p):
            q.wait_ge(C.wq.h, wq_p)
            q.wait_ge(C.dv.h, dv_p)
            if p >= 1:
                q.wait_ge(C.se.h, se_prev)
                q.wait_ge(C.sv.h, sv_prev)
            for u in range(NU):
                j, hv = u // 2, u % 2
                if u >= 2:
                    q.wait_ge(C.ac.h, ac0 + u - 1)
                dst = ps[u % 2]
                for c in range(NC):
                    q.matmul(dst[:, 0:512], wv3[:, c, j, :],
                             C.xnf3()[:, c, hv * 1024:hv * 1024 + 512],
                             start=(c == 0), stop=(c == NC - 1))
                for c in range(NC):
                    last = q.matmul(dst[:, 512:1024], wv3[:, c, j, :],
                                    C.xnf3()[:, c, hv * 1024 + 512:hv * 1024 + 1024],
                                    start=(c == 0), stop=(c == NC - 1))
                C.sq.inc(last)

        @on(blk, 'scalar')
        def _(q, p=p):
            for u in range(NU):
                j, hv = u // 2, u % 2
                q.wait_ge(C.sq.h, sq0 + u + 1)
                tgt = (qp, kp, vf)[j]
                kw = {}
                if j == 0:
                    kw["scale"] = C.fv(0.125)
                if layer in (0, 1):
                    kw["bias"] = bcol[:, j:j + 1]
                C.ac.inc(q.activation(out=tgt[:, hv * 1024:(hv + 1) * 1024],
                                      in_=ps[u % 2][:], func=AF.Identity, **kw))

        if substage == "proj":
            continue
        sq1, sv1 = C.sq.n, C.sv.n

        @on(blk, 'tensor')
        def _(q, p=p):
            q.wait_ge(C.ac.h, ac0 + 6)
            for grp in range(4):
                if grp >= 1:
                    q.wait_ge(C.sv.h, sv1 + grp)
                last = None
                for t4 in range(4):
                    i = grp * 4 + t4
                    last = q.transpose(ptv[:, t4 * 128:(t4 + 1) * 128],
                                       vf[:, i * 128:(i + 1) * 128], C.c_ident[:])
                C.sq.inc(last)

        @on(blk, 'vector')
        def _(q, p=p):
            for grp in range(4):
                q.wait_ge(C.sq.h, sq1 + grp + 1)
                p3 = ptv[:].rearrange("p (n e) -> p n e", e=128)
                q.tensor_copy(
                    va[0][:].rearrange("p (n e) -> p n e", e=80)
                    [:, grp * 4:(grp + 1) * 4, 0:64], p3[:, :, 0:64])
                C.sv.inc(q.tensor_copy(
                    va[1][:].rearrange("p (n e) -> p n e", e=80)
                    [:, grp * 4:(grp + 1) * 4, 0:64], p3[:, :, 64:128]))
        sv_after_v = C.sv.n

        if substage == "vtr":
            continue
        if layer in (0, 1):
            ac1 = C.ac.n

            @on(blk, 'scalar')
            def _(q, p=p):
                if p >= 1:
                    q.wait_ge(C.sq.h, C.sq.n)
                C.ac.inc(q.activation(out=scrb[:], in_=qp[:], func=AF.Abs))
            dv1 = C.dv.n

            @on(blk, 'vector')
            def _(q, p=p):
                q.wait_ge(C.ac.h, ac1)
                C.dv.inc(q.reduce_max(kcol[:], kp[:], axis=AX.X,
                                      apply_absolute_value=True))
            sq2 = C.sq.n

            @on(blk, 'tensor')
            def _(q, p=p):
                q.wait_ge(C.dv.h, dv1 + 1)
                q.wait_ge(C.sv.h, sv_after_v)
                C.sq.inc(q.transpose(ptv[0:1, 0:128], kcol[:], C.c_ident[:]))

            @on(blk, 'vector')
            def _(q, p=p):
                q.wait_ge(C.sq.h, sq2 + 1)
                for hh in range(2):
                    q.wait_ge(C.dv.h, C.dv.inc(
                        q.reduce_max(kst[0:1, hh:hh + 1],
                                     ptv[0:1, hh * 64:(hh + 1) * 64], axis=AX.X)))
                    C.dv.inc(q.tensor_scalar_mul(kst[0:1, hh:hh + 1],
                                                 kst[0:1, hh:hh + 1], -1.02))

        if substage == "bound":
            continue
        for hh in range(2):
            h = 2 * p + hh
            base = hh * 64

            if layer == 1:
                wqh = C.wq.n

                @on(blk, 'gpsimd')
                def _(q, h=h):
                    if True:
                        q.wait_ge(C.sq.h, C.sq.n)
                    src = bass.AP(C.win["l1_tableT"].tensor, h * 4095 + 1920,
                                  [[1, 128], [128, 16], [1, 128]])
                    C.cast_load(q, hkb[:].rearrange("r (d c) -> r d c", d=16), src)
                sq3, sv3 = C.sq.n, C.sv.n

                @on(blk, 'tensor')
                def _(q, h=h):
                    q.wait_ge(C.wq.h, C.wq.n)
                    q.wait_ge(C.dv.h, C.dv.n)
                    q.wait_ge(C.sv.h, sv3)
                    for g in range(4):
                        if g >= 1:
                            q.wait_ge(C.sv.h, sv3 + g)
                        last = None
                        for s in range(4):
                            d = g * 4 + s
                            last = q.transpose(ptv[:, s * 128:(s + 1) * 128],
                                               hkb[:, d * 128:(d + 1) * 128],
                                               C.c_jex[:])
                        C.sq.inc(last)

                @on(blk, 'vector')
                def _(q, h=h):
                    for g in range(4):
                        q.wait_ge(C.sq.h, sq3 + g + 1)
                        C.sv.inc(q.tensor_copy(
                            bdall[:, g * 512:(g + 1) * 512], ptv[:]))

            if layer in (0, 1):
                sq4, sv4 = C.sq.n, C.sv.n

                @on(blk, 'tensor')
                def _(q, base=base):
                    q.wait_ge(C.ac.h, C.ac.n)
                    for qc4 in range(4):
                        if qc4 >= 1:
                            q.wait_ge(C.sv.h, sv4 + qc4)
                        C.sq.inc(q.matmul(pzb[0:1, 0:512],
                                          C.c_oc[base:base + 64, 0:1],
                                          scrb[base:base + 64,
                                               qc4 * 512:(qc4 + 1) * 512],
                                          start=True, stop=True))

                @on(blk, 'vector')
                def _(q, base=base, hh=hh, sqn=C.sq.n):
                    for qc4 in range(4):
                        q.wait_ge(C.sq.h, sq4 + qc4 + 1)
                        C.sv.inc(q.tensor_copy(nrow[0:1, qc4 * 512:(qc4 + 1) * 512],
                                               pzb[0:1, 0:512]))
                    q.wait_ge(C.sq.h, sqn)
                    q.wait_ge(C.sv.h, C.sv.n)
                    q.wait_ge(C.dv.h, C.dv.n)
                    C.dv.inc(q.tensor_scalar(nbrow[0:1, :], nrow[0:1, :],
                                             kst[0:1, hh:hh + 1],
                                             kst[0:1, 4:5],
                                             ALU.mult, ALU.subtract))

            for qc in range(NQC):
                ks = kranges(qc)
                s0, e0, v0 = C.sq.n, C.se.n, C.sv.n
                dvq = C.dv.n

                @on(blk, 'tensor')
                def _(q, h=h, base=base, qc=qc, ks=ks, s0=s0, e0=e0, v0=v0):
                    nk = len(ks)

                    def emit_pv(idx):
                        tjp = ks[idx]
                        last = None
                        for half in range(2):
                            last = q.matmul(
                                ppv[:, half * 512:(half + 1) * 512],
                                va[hh][:, tjp * 80:tjp * 80 + 65],
                                ptb[idx % 2][:, half * 512:(half + 1) * 512],
                                start=(idx == 0), stop=(idx == nk - 1))
                        C.sv.inc(last)

                    for idx, tj in enumerate(ks):
                        if idx >= 2:
                            q.wait_ge(C.se.h, e0 + idx - 1)
                        elif idx == 0:
                            q.wait_ge(C.se.h, e0)
                            q.wait_ge(C.dv.h, dvq)
                            q.wait_ge(C.sv.h, v0)
                        dst = ps[idx % 2]
                        mms = []
                        for half in range(2):
                            mms.append((dst[:, half * 512:(half + 1) * 512],
                                        kp[base:base + 64, tj * 128:(tj + 1) * 128],
                                        qp[base:base + 64,
                                           qc * QCH + half * 512:
                                           qc * QCH + (half + 1) * 512]))
                        if layer in (0, 1):
                            for half in range(2):
                                mms.append((dst[:, half * 512:(half + 1) * 512],
                                            C.c_or[:1, :],
                                            nbrow[0:1, qc * QCH + half * 512:
                                                  qc * QCH + (half + 1) * 512]))
                        if layer == 0:
                            pw = (tj - 2) * 128 - qc * QCH
                        elif layer == 1:
                            pw = tj * 128 - qc * QCH
                        else:
                            pw = -1
                        if layer == 1:
                            lo = max(qc * 8, tj)
                            hi = qc * 8 + 8
                            for ti in range(lo, hi):
                                d = ti - tj
                                off = (ti - qc * 8) * 128
                                mms.append((dst[:, off:off + 128],
                                            bdall[:, d * 128:(d + 1) * 128],
                                            C.c_ident[:]))
                        if layer in (0, 1) and pw > 0:
                            wv = min(pw, QCH)
                            off = 0
                            while off < wv:
                                ws = min(512, wv - off)
                                mms.append((dst[:, off:off + ws],
                                            C.c_or[:1, :], C.c_n9[0:1, 0:ws]))
                                off += ws
                        if layer in (0, 1) and 0 <= pw < QCH:
                            mms.append((dst[:, pw:pw + 128],
                                        C.c_ident[:], C.c_tmask[:]))
                        bank_ids = [int(o_.offset) // 512 for (o_, l_, r_) in mms]
                        first_of, last_of = {}, {}
                        for mi, b in enumerate(bank_ids):
                            first_of.setdefault(b, mi)
                            last_of[b] = mi
                        for mi, (o_, l_, r_) in enumerate(mms):
                            b = bank_ids[mi]
                            last = q.matmul(o_, l_, r_,
                                            start=(first_of[b] == mi),
                                            stop=(last_of[b] == mi))
                        C.sq.inc(last)
                        if idx >= 1:
                            q.wait_ge(C.se.h, e0 + idx)
                            emit_pv(idx - 1)
                    q.wait_ge(C.se.h, e0 + nk)
                    emit_pv(nk - 1)

                @on(blk, 'scalar')
                def _(q, qc=qc, ks=ks, s0=s0, e0=e0, v0=v0):
                    func = AF.Sigmoid if layer == 2 else AF.Exp
                    for idx in range(len(ks)):
                        q.wait_ge(C.sq.h, s0 + idx + 1)
                        if idx >= 2:
                            q.wait_ge(C.sv.h, v0 + idx - 1)
                        elif idx == 0:
                            q.wait_ge(C.sv.h, v0)
                        C.se.inc(q.activation(out=ptb[idx % 2][:],
                                              in_=ps[idx % 2][:], func=func))

                sq5 = C.sq.n

                @on(blk, 'vector')
                def _(q, qc=qc, ks=ks, v0=v0):
                    q.wait_ge(C.sv.h, v0 + len(ks))
                    if layer == 2:
                        q.wait_ge(C.dv.h, C.dv.inc(
                            q.tensor_scalar_add(zrow[64:65, 0:QCH],
                                                ppv[64:65, 0:QCH], 1e-6)))
                        C.dv.inc(q.reciprocal(zrow[64:65, 0:QCH],
                                              zrow[64:65, 0:QCH]))
                    else:
                        C.dv.inc(q.reciprocal(zrow[64:65, 0:QCH],
                                              ppv[64:65, 0:QCH]))
                dv2 = C.dv.n

                for half in range(2):
                    sq6 = C.sq.n

                    @on(blk, 'tensor')
                    def _(q, half=half, dv2=dv2):
                        q.wait_ge(C.dv.h, dv2)
                        if half == 1:
                            q.wait_ge(C.dv.h, C.dv.n)
                        C.sq.inc(q.matmul(
                            pzb[:, 0:512], C.c_o64[64:65, 0:64],
                            zrow[64:65, half * 512:half * 512 + 512],
                            start=True, stop=True))

                    @on(blk, 'vector')
                    def _(q, h=h, hh=hh, p=p, qc=qc, half=half, sq6=sq6):
                        q.wait_ge(C.sq.h, sq6 + 1)
                        q.wait_ge(C.dv.h, C.dv.inc(
                            q.tensor_copy(zbsb[:], pzb[:, 0:512])))
                        col = p * T + qc * QCH + half * 512
                        if hh == 0:
                            C.dv.inc(q.tensor_mul(
                                aof[0:64, col:col + 512],
                                ppv[0:64, half * 512:(half + 1) * 512], zbsb[:]))
                        else:
                            C.dv.inc(q.tensor_mul(
                                mvsb[:],
                                ppv[0:64, half * 512:(half + 1) * 512], zbsb[:]))
                    if hh == 1:
                        dv3, sq7 = C.dv.n, C.sq.n

                        @on(blk, 'tensor')
                        def _(q, dv3=dv3):
                            q.wait_ge(C.dv.h, dv3)
                            C.sq.inc(q.matmul(ps[0][64:128, 0:512],
                                              C.c_ident[0:64, 0:64], mvsb[:],
                                              start=True, stop=True))

                        @on(blk, 'vector')
                        def _(q, p=p, qc=qc, half=half, sq7=sq7):
                            q.wait_ge(C.sq.h, sq7 + 1)
                            col = p * T + qc * QCH + half * 512
                            C.dv.inc(q.tensor_copy(aof[64:128, col:col + 512],
                                                   ps[0][64:128, 0:512]))
                dv4 = C.dv.n

                @on(blk, 'tensor')
                def _(q, dv4=dv4):
                    q.wait_ge(C.dv.h, dv4)

    C.barrier()
    C._attn_dbg = dict(qp=qp, kp=kp, vf=vf, va=va, nbrow=nbrow, nrow=nrow,
                       kst=kst, scrb=scrb, ptb=ptb)
    if getattr(C, "_keep_attn_scope", False):
        C._attn_es = es2
    else:
        es2.close()


def emit_combine(C, aof, wkey, bkey, g_dram=None):
    nc, blk = C.nc, C.blk
    es2 = contextlib.ExitStack()
    wout = es2.enter_context(nc.sbuf_tensor("wout", [128, NC * 1024], BF16))
    r_bo = es2.enter_context(nc.sbuf_tensor("r_bo", [1, 1024], BF16))
    xrb = [es2.enter_context(nc.sbuf_tensor(f"xrb{j}", [128, 1024], F32))
           for j in range(2)]
    ysb = [es2.enter_context(nc.sbuf_tensor(f"ysb{j}", [128, 1024], F32))
           for j in range(2)]
    grb = [es2.enter_context(nc.sbuf_tensor(f"grb{j}", [128, 1024], F32))
           for j in range(2)] if g_dram is not None else None
    tmp = es2.enter_context(nc.sbuf_tensor("cmb_tmp", [128, 1024], F32))
    pcmb = [es2.enter_context(nc.psum_tensor(f"pcmb{j}", [128, 512], F32))
            for j in range(4)]

    @on(blk, 'gpsimd')
    def _(q):
        C.cast_load(q, wout[:].rearrange("p (c o) -> p c o", c=NC),
                    C.win[wkey].rearrange("(c p) o -> p c o", p=128))
        if bkey is not None:
            C.cast_load(q, r_bo[:], C.win[bkey].unsqueeze(0))
    wq_c = C.wq.n

    xq0, pe0, dv0, yq0 = C.xq.n, C.pe.n, C.dv.n, C.yq.n
    aof3 = aof[:].rearrange("p (c t) -> p c t", c=NC)

    @on(blk, 'sync')
    def _(q):
        for i in range(NT):
            if i >= 1:
                q.wait_ge(C.dv.h, dv0 + 2 * (i - 1) + 2)
                C.yq.inc(q.dma_start(out=C.y[(i - 1) * 128:i * 128, :],
                                     in_=ysb[(i - 1) % 2][:]), 16)
            C.xq.inc(q.dma_start(out=xrb[i % 2][:],
                                 in_=C.x[i * 128:(i + 1) * 128, :]), 16)
            if g_dram is not None:
                C.xq.inc(q.dma_start(out=grb[i % 2][:],
                                     in_=g_dram[i * 128:(i + 1) * 128, :]), 16)
        q.wait_ge(C.dv.h, dv0 + 2 * (NT - 1) + 2)
        C.yq.inc(q.dma_start(out=C.y[(NT - 1) * 128:NT * 128, :],
                             in_=ysb[(NT - 1) % 2][:]), 16)
        q.wait_ge(C.yq.h, C.yq.n)

    @on(blk, 'tensor')
    def _(q):
        q.wait_ge(C.wq.h, wq_c)
        for i in range(NT):
            if i >= 2:
                q.wait_ge(C.dv.h, dv0 + 2 * (i - 1))
            last = None
            for oc in range(2):
                dst = pcmb[2 * (i % 2) + oc]
                for c in range(NC):
                    last = q.matmul(dst[:], aof3[:, c, i * 128:(i + 1) * 128],
                                    wout[:, c * 1024 + oc * 512:c * 1024 + (oc + 1) * 512],
                                    start=(c == 0), stop=(bkey is None and c == NC - 1))
                if bkey is not None:
                    last = q.matmul(dst[:], C.c_or[:1, :],
                                    r_bo[:1, oc * 512:(oc + 1) * 512],
                                    start=False, stop=True)
            C.pe.inc(last)

    @on(blk, 'vector')
    def _(q):
        NL = 16 if g_dram is None else 32
        for i in range(NT):
            q.wait_ge(C.pe.h, pe0 + i + 1)
            q.wait_ge(C.xq.h, xq0 + NL * (i + 1))
            if i >= 2:
                q.wait_ge(C.yq.h, yq0 + 16 * (i - 1))
            for oc in range(2):
                gsl = (C.g[:, i * 1024 + oc * 512:i * 1024 + (oc + 1) * 512]
                       if g_dram is None else
                       grb[i % 2][:, oc * 512:(oc + 1) * 512])
                q.tensor_mul(tmp[:, oc * 512:(oc + 1) * 512],
                             pcmb[2 * (i % 2) + oc][:], gsl)
            C.dv.inc(q.tensor_add(ysb[i % 2][:], tmp[:], xrb[i % 2][:]), 2)

    es2.close()


def emit_debug_dumps(C, aof):
    nc, blk = C.nc, C.blk
    dxnf = nc.dram_tensor("dbg_xnf", [128, NC * T], BF16, kind="ExternalOutput").ap()
    dg = nc.dram_tensor("dbg_g", [128, NT * D], F32, kind="ExternalOutput").ap()
    daof = nc.dram_tensor("dbg_aof", [128, NC * T], BF16, kind="ExternalOutput").ap()
    C.barrier()

    A = getattr(C, "_attn_dbg", None)
    dqp = nc.dram_tensor("dbg_qp", [128, T], BF16, kind="ExternalOutput").ap()
    dkp = nc.dram_tensor("dbg_kp", [128, T], BF16, kind="ExternalOutput").ap()
    dnb = nc.dram_tensor("dbg_nb", [1, T], BF16, kind="ExternalOutput").ap()
    dva0 = nc.dram_tensor("dbg_va0", [128, NT * 80], BF16, kind="ExternalOutput").ap()
    dkst = nc.dram_tensor("dbg_kst", [1, 5], F32, kind="ExternalOutput").ap()

    @on(blk, 'sync')
    def _(q):
        C.yq.inc(q.dma_start(out=dxnf, in_=C.xnf[:]), 16)
        if C.g is not None:
            C.yq.inc(q.dma_start(out=dg, in_=C.g[:]), 16)
        C.yq.inc(q.dma_start(out=daof, in_=aof[:]), 16)
        if A is not None:
            C.yq.inc(q.dma_start(out=dqp, in_=A["qp"][:]), 16)
            C.yq.inc(q.dma_start(out=dkp, in_=A["kp"][:]), 16)
            C.yq.inc(q.dma_start(out=dnb, in_=A["nbrow"][:]), 16)
            C.yq.inc(q.dma_start(out=dva0, in_=A["va"][0][:]), 16)
            C.yq.inc(q.dma_start(out=dkst, in_=A["kst"][0:1, 0:5]), 16)
        q.wait_ge(C.yq.h, C.yq.n)


def emit_trivial_y(C):
    nc, blk = C.nc, C.blk
    es2 = contextlib.ExitStack()
    tb = es2.enter_context(nc.sbuf_tensor("triv", [128, 1024], F32))
    C.barrier()

    @on(blk, 'sync')
    def _(q):
        for i in range(NT):
            C.xq.inc(q.dma_start(out=tb[:], in_=C.x[i * 128:(i + 1) * 128, :]), 16)
            q.wait_ge(C.xq.h, C.xq.n)
            C.yq.inc(q.dma_start(out=C.y[i * 128:(i + 1) * 128, :], in_=tb[:]), 16)
            q.wait_ge(C.yq.h, C.yq.n)
    es2.close()


def body_layer0(C):
    esl = contextlib.ExitStack()
    C.ensure_consts(CONST_NEED[0], esl)
    C.g = esl.enter_context(C.nc.sbuf_tensor("g0", [128, NT * D], F32))
    with C.nc.named_scope("l0_ln"):
        emit_ln_transpose(C)
    with C.nc.named_scope("l0_ffn"):
        emit_l0_ffn(C)
    aof = esl.enter_context(C.nc.sbuf_tensor("aof0", [128, NC * T], BF16))
    with C.nc.named_scope("l0_attn"):
        emit_st_attention(C, 0, aof)
    with C.nc.named_scope("l0_comb"):
        emit_combine(C, aof, "l0_woT", "l0_bo")
        C.barrier()
    esl.close()


def build_layer0(W, dbg=False, stop_after=None, npairs=H // 2, substage=None):
    C = LayerProg(W, LAYER_KEYS[0], no_g=True, need=CONST_NEED[0])
    C._keep_attn_scope = dbg
    C.emit_const_loads()
    body_layer0(C)
    C.close()
    return C


def body_layer2(C):
    esl = contextlib.ExitStack()
    C.ensure_consts(CONST_NEED[2], esl)
    C.g = None
    g_dram = C.nc.dram_tensor("g_scratch", [T, D], F32).ap()
    with C.nc.named_scope("l2_ln"):
        emit_ln_transpose(C)
    with C.nc.named_scope("l2_ffn"):
        emit_l2_ffn(C, g_dram)
    aof = esl.enter_context(C.nc.sbuf_tensor("aof2", [128, NC * T], BF16))
    with C.nc.named_scope("l2_attn"):
        emit_st_attention(C, 2, aof)
    with C.nc.named_scope("l2_comb"):
        emit_combine(C, aof, "l2_WoT", None, g_dram=g_dram)
        C.barrier()
    esl.close()


def build_layer2(W, dbg=False, stop_after=None, npairs=H // 2, substage=None):
    C = LayerProg(W, LAYER_KEYS[2], no_g=True, need=CONST_NEED[2])
    C._keep_attn_scope = dbg
    C.emit_const_loads()
    body_layer2(C)
    C.close()
    return C


def body_layer3(C):
    esl = contextlib.ExitStack()
    C.ensure_consts(CONST_NEED[3], esl)
    C.g = None
    msktm = esl.enter_context(C.nc.sbuf_tensor("l3msk", [128, NT * 16], BF16))
    scol = esl.enter_context(C.nc.sbuf_tensor("l3scol", [128, NT], F32))
    rtm = esl.enter_context(C.nc.sbuf_tensor("l3rtm", [128, NT * 16], F32))
    es_lo = contextlib.ExitStack()
    xnlo = es_lo.enter_context(C.nc.sbuf_tensor("xnlo", [128, NC * T], BF16))
    with C.nc.named_scope("l3_ln"):
        emit_ln_transpose(C, lo_dst=xnlo)
    with C.nc.named_scope("l3_ffn"):
        emit_l3_ffn(C, xnlo=xnlo, es_lo=es_lo, g_es=esl,
                    msktm=msktm, scol=scol, rtm=rtm)
    aof = esl.enter_context(C.nc.sbuf_tensor("aof3", [128, NC * T], BF16))
    with C.nc.named_scope("l3_attn"):
        emit_l3_attn(C, aof)
    with C.nc.named_scope("l3_comb"):
        emit_combine(C, aof, "l3_Wo", "l3_bo")
        C.barrier()
    esl.close()


def build_layer3(W, dbg=False, stop_after=None, npairs=H // 2, substage=None):
    C = LayerProg(W, LAYER_KEYS[3], no_g=True, need=CONST_NEED[3])
    C._keep_attn_scope = False
    C.emit_const_loads()
    body_layer3(C)
    C.close()
    return C


def build_layer(l, W, dbg=False, stop_after=None, npairs=H // 2, substage=None):
    if l == 0:
        return build_layer0(W, dbg, stop_after, npairs, substage)
    if l == 1:
        return build_layer1(W, dbg, stop_after, npairs, substage)
    if l == 2:
        return build_layer2(W, dbg, stop_after, npairs, substage)
    if l == 3:
        return build_layer3(W, dbg, stop_after, npairs, substage)
    raise NotImplementedError(f"layer {l}")


BODIES = {0: body_layer0, 1: body_layer1, 2: body_layer2, 3: body_layer3}


def build_fused(W):
    keys = sum((LAYER_KEYS[l] for l in range(4)), [])
    C = LayerProg(W, keys, no_g=True, need=[])
    C._keep_attn_scope = False
    yout = C.y
    scr = [C.nc.dram_tensor(f"xs{l}", [T, D], F32).ap() for l in (1, 2, 3)]
    for l in range(4):
        if l > 0:
            C.x = scr[l - 1]
        C.y = scr[l] if l < 3 else yout
        BODIES[l](C)
    C.close()
    return C


def kernel(**inputs):
    W, xs = host_prep(inputs)
    C = build_fused(W)
    shared = {k: W[k] for l in range(4) for k in LAYER_KEYS[l]}
    shared.update({c[1]: W[c[1]] for c in CONSTS})
    in_maps = [dict(shared, x=xs[i]) for i in range(8)]
    res = run_bass_kernel_spmd(C.nc, in_maps, core_ids=list(range(8)))
    cur = [np.asarray(r["y"], dtype=np.float32) for r in res.results]
    return np.stack(cur, axis=0)


# version 05 (final)
# speedup vs baseline: 1.2310x; 1.1658x over previous
import contextlib
import numpy as np
import ml_dtypes
import concourse.bass as bass
import concourse.mybir as mybir
from concourse.bass_utils import run_bass_kernel_spmd

F32 = mybir.dt.float32
BF16 = mybir.dt.bfloat16
AF = mybir.ActivationFunctionType
ALU = mybir.AluOpType
AX = mybir.AxisListType

T = 2048
D = 1024
NT = T // 128
NC = D // 128
H = 16
NEG = -1.0e9
LN_EPS = 1e-5
bfd = ml_dtypes.bfloat16
FVALS = [-0.5, 0.125, 1e-5, -1.0, 1.0]
FIDX = {v: i for i, v in enumerate(FVALS)}


class Sem:
    def __init__(self, nc, es, name):
        self.h = es.enter_context(nc.semaphore(name))
        self.n = 0

    def inc(self, instr, k=1):
        instr.then_inc(self.h, k)
        self.n += k
        return self.n


def on(blk, eng):
    def deco(fn):
        getattr(blk, eng)(fn)
    return deco


def host_prep(inputs):
    I = inputs
    w = {}
    ct = lambda a: np.ascontiguousarray(np.asarray(a, dtype=np.float32).T)
    cc = lambda a: np.ascontiguousarray(np.asarray(a), dtype=np.float32)
    w["l0_wqkvT"] = ct(I["L0_attn_in_proj_w"])
    w["l0_bqkv"] = cc(I["L0_attn_in_proj_b"])
    w["l0_woT"] = ct(I["L0_attn_out_proj_w"])
    w["l0_bo"] = cc(I["L0_attn_out_proj_b"])
    w["l0_basewT"] = ct(I["L0_ffn_base_w"])
    w["l0_baseb"] = cc(I["L0_ffn_base_b"])
    w["l0_loraA"] = cc(I["L0_ffn_lora_A"])
    w["l0_loraB"] = cc(np.reshape(np.asarray(I["L0_ffn_lora_B"]), (64, 1024)))
    w["l0_routerT"] = ct(I["L0_ffn_router_w"])
    w["l0_routerb"] = cc(I["L0_ffn_router_b"])
    for k in ("WQ", "WK", "WV", "WO"):
        w[f"l1_{k}T"] = ct(I[f"L1_attn_{k}"])
    for k in ("bQ", "bK", "bV", "bO"):
        w[f"l1_{k}"] = cc(I[f"L1_attn_{k}"])
    w["l1_tableT"] = ct(I["L1_attn_rel_pos_table"])
    w["l1_ffw"] = cc(I["L1_ffn_weight"])
    w["l1_ffa1"] = cc(I["L1_ffn_aux_weight1"])
    w["l1_ffa2"] = cc(I["L1_ffn_aux_weight2"])
    w["l1_ffb"] = cc(I["L1_ffn_bias"])
    for k in ("Wq", "Wk", "Wv", "Wo"):
        w[f"l2_{k}T"] = ct(I[f"L2_attn_{k}"])
    w["l2_w1T"] = ct(I["L2_ffn_w1_w"])
    w["l2_w2T"] = ct(I["L2_ffn_w2_w"])
    w["l2_h1T"] = ct(I["L2_ffn_h1_w"])
    w["l2_h2T"] = ct(I["L2_ffn_h2_w"])
    w["l2_w3T"] = ct(I["L2_ffn_w3_w"])
    for k in ("w1_s", "w2_s", "h1_s", "h2_s", "w3_s"):
        w[f"l2_{k}"] = cc(I[f"L2_ffn_{k}"])
    for k in ("Wq", "Wk", "Wv", "Wo", "bq", "bk", "bv", "bo"):
        w[f"l3_{k}"] = cc(I[f"L3_attn_{k}"])
    w["l3_routerT"] = ct(I["L3_ffn_w_router"])
    w["l3_gup"] = cc(I["L3_ffn_gate_up_proj"])
    w["l3_down"] = cc(I["L3_ffn_down_proj"])
    w["l3_wgT"] = ct(I["L3_ffn_w_gate"])
    w["l3_wuT"] = ct(I["L3_ffn_w_up"])
    w["l3_wdT"] = ct(I["L3_ffn_w_down"])
    w["identbf"] = np.eye(128, dtype=bfd)
    r = np.arange(128)
    w["tmaskbf"] = np.where(r[:, None] <= r[None, :], 0.0, NEG).astype(bfd)
    w["jexbf"] = np.ascontiguousarray(np.eye(128, dtype=np.float32)[:, ::-1]).astype(bfd)
    w["onesrowbf"] = np.ones((1, 128), dtype=bfd)
    w["onescolbf"] = np.ones((128, 1), dtype=bfd)
    w["onesrowf"] = np.ones((1, 128), dtype=np.float32)
    o64 = np.zeros((65, 128), dtype=np.float32)
    o64[64, :] = 1.0
    w["ones64f"] = o64
    w["neg1e9row"] = np.full((1, 1024), NEG, dtype=bfd)
    w["iota16"] = np.tile(np.arange(16, dtype=np.float32)[None, :], (128, 1))
    w["utri1bf"] = np.where(r[:, None] <= r[None, :], 1.0, 0.0).astype(bfd)
    w["ao1bf"] = np.ones((128, 128), dtype=bfd)
    w["iotacap"] = np.tile(np.arange(384, dtype=np.float32)[None, :], (128, 1))
    w["iota64m"] = np.tile(np.arange(16, dtype=np.float32)[None, :] - 64.0, (128, 1))
    w["eselbf"] = np.ascontiguousarray(
        np.eye(16, dtype=bfd)[:, :, None] * np.ones((1, 1, 128), dtype=bfd)
    ).reshape(16, 16 * 128).astype(bfd)
    w["fvals"] = np.tile(np.array(FVALS, dtype=np.float32)[None, :], (128, 1))
    x = np.asarray(inputs["L0_attn_x"], dtype=np.float32)
    xs = [np.ascontiguousarray(x[i]) for i in range(8)]
    return w, xs


CONSTS = [("c_ident", "identbf", [128, 128], BF16),
          ("c_tmask", "tmaskbf", [128, 128], BF16),
          ("c_jex", "jexbf", [128, 128], BF16),
          ("c_or", "onesrowbf", [1, 128], BF16),
          ("c_oc", "onescolbf", [128, 1], BF16),
          ("c_orf", "onesrowf", [1, 128], F32),
          ("c_o64", "ones64f", [65, 128], F32),
          ("c_n9", "neg1e9row", [1, 1024], BF16),
          ("c_iota", "iota16", [128, 16], F32),
          ("c_iota64m", "iota64m", [128, 16], F32),
          ("c_esel", "eselbf", [16, 16 * 128], BF16),
          ("c_ut", "utri1bf", [128, 128], BF16),
          ("c_ao", "ao1bf", [128, 128], BF16),
          ("c_icap", "iotacap", [128, 384], F32),
          ("c_fv", "fvals", [128, len(FVALS)], F32)]

LAYER_KEYS = {
    0: ["l0_wqkvT", "l0_bqkv", "l0_woT", "l0_bo", "l0_basewT", "l0_baseb",
        "l0_loraA", "l0_loraB", "l0_routerT", "l0_routerb"],
    1: ["l1_WQT", "l1_WKT", "l1_WVT", "l1_WOT", "l1_bQ", "l1_bK", "l1_bV",
        "l1_bO", "l1_tableT", "l1_ffw", "l1_ffa1", "l1_ffa2", "l1_ffb"],
    2: ["l2_WqT", "l2_WkT", "l2_WvT", "l2_WoT", "l2_w1T", "l2_w2T", "l2_h1T",
        "l2_h2T", "l2_w3T", "l2_w1_s", "l2_w2_s", "l2_h1_s", "l2_h2_s", "l2_w3_s"],
    3: ["l3_Wq", "l3_Wk", "l3_Wv", "l3_Wo", "l3_bq", "l3_bk", "l3_bv", "l3_bo",
        "l3_routerT", "l3_gup", "l3_down", "l3_wgT", "l3_wuT", "l3_wdT"],
}


CONST_NEED = {
    0: ["c_ident", "c_tmask", "c_or", "c_oc", "c_orf", "c_o64", "c_n9", "c_fv"],
    1: ["c_ident", "c_tmask", "c_jex", "c_or", "c_oc", "c_orf", "c_o64",
        "c_n9", "c_fv"],
    2: ["c_ident", "c_or", "c_oc", "c_orf", "c_o64", "c_n9", "c_fv"],
    3: ["c_ident", "c_or", "c_oc", "c_orf", "c_o64", "c_iota", "c_iota64m",
        "c_ut", "c_ao", "c_icap", "c_fv"],
}


class LayerProg:
    def __init__(self, W, wkeys, no_g=False, need=None):
        nc = bass.Bass("TRN2", target_bir_lowering=False)
        self.nc = nc
        _sb, _ps = nc.sbuf_tensor, nc.psum_tensor
        self._uid = [0]

        def _mk(f):
            def wrap(name, *a, **kw):
                self._uid[0] += 1
                return f(f"{name}_u{self._uid[0]}", *a, **kw)
            return wrap
        nc.sbuf_tensor = _mk(_sb)
        nc.psum_tensor = _mk(_ps)
        self.x = nc.dram_tensor("x", [T, D], F32, kind="ExternalInput").ap()
        self.y = nc.dram_tensor("y", [T, D], F32, kind="ExternalOutput").ap()
        self.win = {}
        for k in wkeys + [c[1] for c in CONSTS]:
            a = W[k]
            dt = BF16 if a.dtype == bfd else F32
            self.win[k] = nc.dram_tensor(k, list(a.shape), dt,
                                         kind="ExternalInput").ap()
        self.es = contextlib.ExitStack()
        self.es.__enter__()
        es = self.es
        self.blk = es.enter_context(nc.Block())
        self.xq = Sem(nc, es, "xq")
        self.wq = Sem(nc, es, "wq")
        self.pe = Sem(nc, es, "pe")
        self.dv = Sem(nc, es, "dv")
        self.ac = Sem(nc, es, "ac")
        self.sq = Sem(nc, es, "sq")
        self.sv = Sem(nc, es, "sv")
        self.se = Sem(nc, es, "se")
        self.yq = Sem(nc, es, "yq")
        self.xnf = es.enter_context(nc.sbuf_tensor("xnf", [128, NC * T], BF16))
        self.g = None if no_g else es.enter_context(
            nc.sbuf_tensor("g", [128, NT * D], F32))
        self.consts = {}
        for nm, k, sh, dt in CONSTS:
            if need is not None and nm not in need:
                setattr(self, nm, None)
                continue
            tile = es.enter_context(nc.sbuf_tensor(nm, sh, dt))
            self.consts[nm] = tile
            setattr(self, nm, tile)

    def ensure_consts(self, need, es):
        C = self
        missing = [c for c in CONSTS if c[0] in need and c[0] not in self.consts]
        if not missing:
            return
        for nm, k, sh, dt in missing:
            tile = es.enter_context(self.nc.sbuf_tensor(nm, sh, dt))
            self.consts[nm] = tile
            setattr(self, nm, tile)

        def _drop():
            for nm, k, sh, dt in missing:
                self.consts.pop(nm, None)
                setattr(self, nm, None)
        es.callback(_drop)

        @on(C.blk, 'sync')
        def _(q):
            for nm, k, sh, dt in missing:
                C.xq.inc(q.dma_start(out=C.consts[nm][:], in_=C.win[k]), 16)
        C.barrier()

    def fv(self, val):
        return self.c_fv[:, FIDX[val]:FIDX[val] + 1]

    def xnf3(self):
        return self.xnf[:].rearrange("p (c t) -> p c t", c=NC)

    def emit_const_loads(self):
        C = self

        @on(C.blk, 'sync')
        def _(q):
            for nm, k, sh, dt in CONSTS:
                if nm in C.consts:
                    C.xq.inc(q.dma_start(out=C.consts[nm][:], in_=C.win[k]), 16)
        C.barrier()

    def barrier(self, engs=("sync", "gpsimd", "vector", "scalar", "tensor")):
        C = self
        vals = [(s.h, s.n) for s in
                (C.xq, C.wq, C.pe, C.dv, C.ac, C.sq, C.sv, C.se, C.yq)
                if s.n > 0]
        for eng in engs:
            @on(C.blk, eng)
            def _(q):
                for h, n in vals:
                    q.wait_ge(h, n)

    def cast_load(self, q, dst_ap, src_ap):
        self.wq.inc(q.dma_start(out=dst_ap, in_=src_ap), 16)

    def close(self):
        self.es.__exit__(None, None, None)


def emit_ln_transpose(C, lo_dst=None):
    if lo_dst is not None:
        return emit_ln_transpose_lo(C, lo_dst)
    nc, blk = C.nc, C.blk
    es2 = contextlib.ExitStack()
    xfull = es2.enter_context(nc.sbuf_tensor("ln_xfull", [128, NT * 1024], F32))
    xntok = es2.enter_context(nc.sbuf_tensor("ln_xntok", [128, NT * 1024], BF16))
    scrF = es2.enter_context(nc.sbuf_tensor("ln_scr", [128, 1024], F32))
    st = es2.enter_context(nc.sbuf_tensor("ln_stats", [128, 6 * NT], F32))
    pts = [es2.enter_context(nc.psum_tensor(f"ln_pt{j}", [128, 1024], BF16))
           for j in range(2)]
    NS = NT
    ac0, dv0, xq0 = C.ac.n, C.dv.n, C.xq.n

    @on(blk, 'sync')
    def _(q):
        C.xq.inc(q.dma_start(
            out=xfull[:].rearrange("p (n d) -> p n d", d=1024),
            in_=C.x.rearrange("(n p) d -> p n d", p=128)), 16)

    @on(blk, 'scalar')
    def _(q):
        q.wait_ge(C.xq.h, xq0 + 16)
        for i in range(NT):
            xt = xfull[:, i * 1024:(i + 1) * 1024]
            C.ac.inc(q.activation(out=scrF[:], in_=xt, func=AF.Square,
                                  accum_out=st[:, 2 * NS + i:2 * NS + i + 1]))
            q.wait_ge(C.dv.h, dv0 + 5 * i + 4)
            q.activation(out=st[:, 3 * NS + i:3 * NS + i + 1],
                         in_=st[:, 3 * NS + i:3 * NS + i + 1],
                         func=AF.Ln, bias=C.fv(1e-5))
            C.ac.inc(q.activation(out=st[:, 4 * NS + i:4 * NS + i + 1],
                                  in_=st[:, 3 * NS + i:3 * NS + i + 1],
                                  func=AF.Exp, scale=C.fv(-0.5)))
            q.wait_ge(C.dv.h, dv0 + 5 * i + 5)
            C.ac.inc(q.activation(out=xntok[:, i * 1024:(i + 1) * 1024], in_=xt,
                                  func=AF.Identity,
                                  bias=st[:, 5 * NS + i:5 * NS + i + 1],
                                  scale=st[:, 4 * NS + i:4 * NS + i + 1]))

    @on(blk, 'vector')
    def _(q):
        q.wait_ge(C.xq.h, xq0 + 16)
        for i in range(NT):
            xt = xfull[:, i * 1024:(i + 1) * 1024]
            q.wait_ge(C.dv.h, C.dv.inc(
                q.reduce_sum(st[:, i:i + 1], xt, axis=AX.X, negate=True)))
            q.wait_ge(C.dv.h, C.dv.inc(
                q.tensor_scalar_mul(st[:, NS + i:NS + i + 1],
                                    st[:, i:i + 1], 1.0 / D)))
            q.wait_ge(C.ac.h, ac0 + 3 * i + 1)
            q.tensor_mul(st[:, i:i + 1], st[:, NS + i:NS + i + 1],
                         st[:, NS + i:NS + i + 1])
            q.wait_ge(C.dv.h, C.dv.inc(
                q.tensor_scalar_mul(st[:, 3 * NS + i:3 * NS + i + 1],
                                    st[:, 2 * NS + i:2 * NS + i + 1], 1.0 / D)))
            C.dv.inc(q.tensor_sub(st[:, 3 * NS + i:3 * NS + i + 1],
                                  st[:, 3 * NS + i:3 * NS + i + 1],
                                  st[:, i:i + 1]))
            q.wait_ge(C.ac.h, ac0 + 3 * i + 2)
            C.dv.inc(q.tensor_mul(st[:, 5 * NS + i:5 * NS + i + 1],
                                  st[:, NS + i:NS + i + 1],
                                  st[:, 4 * NS + i:4 * NS + i + 1]))

    sq0, sv0 = C.sq.n, C.sv.n

    @on(blk, 'tensor')
    def _(q):
        for i in range(NT):
            q.wait_ge(C.ac.h, ac0 + 3 * i + 3)
            if i >= 2:
                q.wait_ge(C.sv.h, sv0 + i - 1)
            pt = pts[i % 2]
            last = None
            for c in range(NC):
                last = q.transpose(
                    pt[:, c * 128:(c + 1) * 128],
                    xntok[:, i * 1024 + c * 128:i * 1024 + (c + 1) * 128],
                    C.c_ident[:])
            C.sq.inc(last)

    @on(blk, 'vector')
    def _(q):
        for i in range(NT):
            q.wait_ge(C.sq.h, sq0 + i + 1)
            C.sv.inc(q.tensor_copy(
                C.xnf3()[:, :, i * 128:(i + 1) * 128],
                pts[i % 2][:].rearrange("p (c t) -> p c t", c=NC)))

    C.barrier()
    es2.close()


def emit_ln_transpose_lo(C, lo_dst):
    nc, blk = C.nc, C.blk
    es2 = contextlib.ExitStack()
    xfull = es2.enter_context(nc.sbuf_tensor("ln_xfull", [128, NT * 1024], F32))
    xntok = es2.enter_context(nc.sbuf_tensor("ln_xntok", [128, NT * 1024], BF16))
    xlotok = es2.enter_context(nc.sbuf_tensor("ln_xlotok", [128, 1024], BF16))
    scrF = es2.enter_context(nc.sbuf_tensor("ln_scr", [128, 1024], F32))
    scr2 = es2.enter_context(nc.sbuf_tensor("ln_scr2", [128, 1024], F32))
    st = es2.enter_context(nc.sbuf_tensor("ln_stats", [128, 6 * NT], F32))
    pts = [es2.enter_context(nc.psum_tensor(f"ln_pt{j}", [128, 1024], BF16))
           for j in range(2)]
    NS = NT
    ac0, dv0, xq0 = C.ac.n, C.dv.n, C.xq.n
    sq0, sv0 = C.sq.n, C.sv.n

    @on(blk, 'sync')
    def _(q):
        C.xq.inc(q.dma_start(
            out=xfull[:].rearrange("p (n d) -> p n d", d=1024),
            in_=C.x.rearrange("(n p) d -> p n d", p=128)), 16)

    @on(blk, 'scalar')
    def _(q):
        q.wait_ge(C.xq.h, xq0 + 16)
        for i in range(NT):
            xt = xfull[:, i * 1024:(i + 1) * 1024]
            C.ac.inc(q.activation(out=scrF[:], in_=xt, func=AF.Square,
                                  accum_out=st[:, 2 * NS + i:2 * NS + i + 1]))
            q.wait_ge(C.dv.h, dv0 + 6 * i + 4)
            q.activation(out=st[:, 3 * NS + i:3 * NS + i + 1],
                         in_=st[:, 3 * NS + i:3 * NS + i + 1],
                         func=AF.Ln, bias=C.fv(1e-5))
            C.ac.inc(q.activation(out=st[:, 4 * NS + i:4 * NS + i + 1],
                                  in_=st[:, 3 * NS + i:3 * NS + i + 1],
                                  func=AF.Exp, scale=C.fv(-0.5)))
            q.wait_ge(C.dv.h, dv0 + 6 * i + 5)
            C.ac.inc(q.activation(out=xntok[:, i * 1024:(i + 1) * 1024], in_=xt,
                                  func=AF.Identity,
                                  bias=st[:, 5 * NS + i:5 * NS + i + 1],
                                  scale=st[:, 4 * NS + i:4 * NS + i + 1]))
            if i >= 1:
                q.wait_ge(C.dv.h, dv0 + 6 * i)
            C.ac.inc(q.activation(out=scr2[:], in_=xt, func=AF.Identity,
                                  bias=st[:, 5 * NS + i:5 * NS + i + 1],
                                  scale=st[:, 4 * NS + i:4 * NS + i + 1]))

    def emit_evac(q, j):
        q.wait_ge(C.sq.h, sq0 + 2 * j + 1)
        C.sv.inc(q.tensor_copy(
            C.xnf3()[:, :, j * 128:(j + 1) * 128],
            pts[0][:].rearrange("p (c t) -> p c t", c=NC)))
        q.wait_ge(C.sq.h, sq0 + 2 * j + 2)
        C.sv.inc(q.tensor_copy(
            lo_dst[:].rearrange("p (c t) -> p c t", c=NC)
            [:, :, j * 128:(j + 1) * 128],
            pts[1][:].rearrange("p (c t) -> p c t", c=NC)))

    @on(blk, 'vector')
    def _(q):
        q.wait_ge(C.xq.h, xq0 + 16)
        for i in range(NT):
            xt = xfull[:, i * 1024:(i + 1) * 1024]
            q.wait_ge(C.dv.h, C.dv.inc(
                q.reduce_sum(st[:, i:i + 1], xt, axis=AX.X, negate=True)))
            q.wait_ge(C.dv.h, C.dv.inc(
                q.tensor_scalar_mul(st[:, NS + i:NS + i + 1],
                                    st[:, i:i + 1], 1.0 / D)))
            q.wait_ge(C.ac.h, ac0 + 4 * i + 1)
            q.tensor_mul(st[:, i:i + 1], st[:, NS + i:NS + i + 1],
                         st[:, NS + i:NS + i + 1])
            q.wait_ge(C.dv.h, C.dv.inc(
                q.tensor_scalar_mul(st[:, 3 * NS + i:3 * NS + i + 1],
                                    st[:, 2 * NS + i:2 * NS + i + 1], 1.0 / D)))
            C.dv.inc(q.tensor_sub(st[:, 3 * NS + i:3 * NS + i + 1],
                                  st[:, 3 * NS + i:3 * NS + i + 1],
                                  st[:, i:i + 1]))
            q.wait_ge(C.ac.h, ac0 + 4 * i + 2)
            C.dv.inc(q.tensor_mul(st[:, 5 * NS + i:5 * NS + i + 1],
                                  st[:, NS + i:NS + i + 1],
                                  st[:, 4 * NS + i:4 * NS + i + 1]))
            if i >= 1:
                emit_evac(q, i - 1)
            q.wait_ge(C.ac.h, ac0 + 4 * i + 4)
            C.dv.inc(q.tensor_sub(xlotok[:], scr2[:],
                                  xntok[:, i * 1024:(i + 1) * 1024]))
        emit_evac(q, NT - 1)

    @on(blk, 'tensor')
    def _(q):
        for i in range(NT):
            q.wait_ge(C.ac.h, ac0 + 4 * i + 3)
            gidx = 2 * i
            if gidx >= 2:
                q.wait_ge(C.sv.h, sv0 + gidx - 1)
            pt = pts[0]
            last = None
            for c in range(NC):
                last = q.transpose(
                    pt[:, c * 128:(c + 1) * 128],
                    xntok[:, i * 1024 + c * 128:i * 1024 + (c + 1) * 128],
                    C.c_ident[:])
            C.sq.inc(last)
            q.wait_ge(C.dv.h, dv0 + 6 * i + 6)
            gidx = 2 * i + 1
            if gidx >= 2:
                q.wait_ge(C.sv.h, sv0 + gidx - 1)
            pt = pts[1]
            last = None
            for c in range(NC):
                last = q.transpose(
                    pt[:, c * 128:(c + 1) * 128],
                    xlotok[:, c * 128:(c + 1) * 128],
                    C.c_ident[:])
            C.sq.inc(last)

    C.barrier()
    es2.close()


def emit_l0_ffn(C):
    nc, blk = C.nc, C.blk
    es2 = contextlib.ExitStack()
    wbase = es2.enter_context(nc.sbuf_tensor("wbase", [128, NC * 1024], BF16))
    wrout = es2.enter_context(nc.sbuf_tensor("wrout", [128, NC * 8], BF16))
    wlA = es2.enter_context(nc.sbuf_tensor("wlA", [128, NC * 8], BF16))
    wlB2 = es2.enter_context(nc.sbuf_tensor("wlB2", [64, 1024], BF16))
    r_baseb = es2.enter_context(nc.sbuf_tensor("r_baseb", [1, 1024], BF16))
    r_routb = es2.enter_context(nc.sbuf_tensor("r_routb", [1, 8], BF16))
    gate = es2.enter_context(nc.sbuf_tensor("gate", [128, 16], BF16))
    tsb = es2.enter_context(nc.sbuf_tensor("tsb", [128, 64], BF16))
    tTall = es2.enter_context(nc.sbuf_tensor("tTall", [64, NT * 128], BF16))
    fst = es2.enter_context(nc.sbuf_tensor("fst", [128, 4], F32))
    gatef = es2.enter_context(nc.sbuf_tensor("gatef", [128, 8], F32))
    pff = [es2.enter_context(nc.psum_tensor(f"pff{i}", [128, 512], F32))
           for i in range(4)]
    pr8 = es2.enter_context(nc.psum_tensor("pr8", [128, 8], F32))
    pm8 = es2.enter_context(nc.psum_tensor("pm8", [128, 8], F32))
    ptT = es2.enter_context(nc.psum_tensor("ptT", [64, 128], BF16))

    @on(blk, 'gpsimd')
    def _(q):
        C.cast_load(q, wbase[:].rearrange("p (c o) -> p c o", c=NC),
                    C.win["l0_basewT"].rearrange("(c p) o -> p c o", p=128))
        C.cast_load(q, wrout[:].rearrange("p (c o) -> p c o", c=NC),
                    C.win["l0_routerT"].rearrange("(c p) o -> p c o", p=128))
        C.cast_load(q, wlA[:].rearrange("p (c o) -> p c o", c=NC),
                    C.win["l0_loraA"].rearrange("(c p) o -> p c o", p=128))
        C.cast_load(q, wlB2[:], C.win["l0_loraB"])
        C.cast_load(q, r_baseb[:], C.win["l0_baseb"].unsqueeze(0))
        C.cast_load(q, r_routb[:], C.win["l0_routerb"].unsqueeze(0))
    wq_ff = C.wq.n

    @on(blk, 'vector')
    def _(q):
        q.wait_ge(C.wq.h, wq_ff)
        C.dv.inc(q.tensor_scalar_mul(wlB2[:], wlB2[:], 2.0))
    dv_lb2 = C.dv.n

    pe0, dv0, ac0 = C.pe.n, C.dv.n, C.ac.n

    @on(blk, 'tensor')
    def _(q):
        q.wait_ge(C.wq.h, wq_ff)
        q.wait_ge(C.dv.h, dv_lb2)
        for i in range(NT):
            if i >= 1:
                q.wait_ge(C.ac.h, ac0 + i)
                q.wait_ge(C.dv.h, dv0 + 6 * (i - 1) + 6)
            for c in range(NC):
                lhs = C.xnf3()[:, c, i * 128:(i + 1) * 128]
                q.matmul(pr8[:], lhs, wrout[:, c * 8:(c + 1) * 8],
                         start=(c == 0), stop=False)
                q.matmul(pm8[:], lhs, wlA[:, c * 8:(c + 1) * 8],
                         start=(c == 0), stop=(c == NC - 1))
            C.pe.inc(q.matmul(pr8[:], C.c_or[:1, :], r_routb[:1, :],
                              start=False, stop=True))
            q.wait_ge(C.dv.h, dv0 + 6 * i + 5)
            C.pe.inc(q.transpose(ptT[:], tsb[:], C.c_ident[:]))

    @on(blk, 'vector')
    def _(q):
        for i in range(NT):
            q.wait_ge(C.pe.h, pe0 + 2 * i + 1)
            C.dv.inc(q.reduce_max(fst[:, 0:1], pr8[:], axis=AX.X, negate=True))
            q.wait_ge(C.ac.h, ac0 + i + 1)
            q.wait_ge(C.dv.h, C.dv.inc(q.reciprocal(fst[:, 2:3], fst[:, 1:2])))
            q.wait_ge(C.dv.h, C.dv.inc(
                q.tensor_scalar_mul(gatef[:], gatef[:], fst[:, 2:3])))
            q.tensor_copy(gate[:, 0:8], gatef[:])
            q.wait_ge(C.dv.h, C.dv.inc(q.tensor_copy(gate[:, 8:16], pm8[:])))
            u = gate[:, 0:8].unsqueeze(2).broadcast_to([128, 8, 8])
            v = gate[:, 8:16].unsqueeze(1).broadcast_to([128, 8, 8])
            C.dv.inc(q.tensor_tensor(tsb[:].rearrange("p (e r) -> p e r", e=8),
                                     u, v, ALU.mult))
            q.wait_ge(C.pe.h, pe0 + 2 * i + 2)
            C.dv.inc(q.tensor_copy(tTall[:, i * 128:(i + 1) * 128], ptT[:]))

    @on(blk, 'scalar')
    def _(q):
        for i in range(NT):
            q.wait_ge(C.dv.h, dv0 + 6 * i + 1)
            C.ac.inc(q.activation(out=gatef[:], in_=pr8[:], func=AF.Exp,
                                  bias=fst[:, 0:1], accum_out=fst[:, 1:2]))

    C.barrier(("tensor", "scalar"))

    pe0, ac0 = C.pe.n, C.ac.n

    @on(blk, 'tensor')
    def _(q):
        for i in range(NT):
            if i >= 2:
                q.wait_ge(C.ac.h, ac0 + (i - 1))
            pa, pb = pff[2 * (i % 2)], pff[2 * (i % 2) + 1]
            for c in range(NC):
                lhs = C.xnf3()[:, c, i * 128:(i + 1) * 128]
                stf = dict(start=(c == 0), stop=False)
                q.matmul(pa[:], lhs, wbase[:, c * 1024:c * 1024 + 512], **stf)
                q.matmul(pb[:], lhs, wbase[:, c * 1024 + 512:(c + 1) * 1024], **stf)
            tt = tTall[:, i * 128:(i + 1) * 128]
            q.matmul(pa[:], tt, wlB2[:, 0:512], start=False, stop=False)
            q.matmul(pb[:], tt, wlB2[:, 512:1024], start=False, stop=False)
            q.matmul(pa[:], C.c_or[:1, :], r_baseb[:1, 0:512], start=False, stop=True)
            C.pe.inc(q.matmul(pb[:], C.c_or[:1, :], r_baseb[:1, 512:1024],
                              start=False, stop=True))

    @on(blk, 'scalar')
    def _(q):
        for i in range(NT):
            q.wait_ge(C.pe.h, pe0 + i + 1)
            q.activation(out=C.g[:, i * 1024:i * 1024 + 512],
                         in_=pff[2 * (i % 2)][:], func=AF.Silu)
            C.ac.inc(q.activation(out=C.g[:, i * 1024 + 512:(i + 1) * 1024],
                                  in_=pff[2 * (i % 2) + 1][:], func=AF.Silu))

    C.barrier()
    es2.close()


def emit_l1_ffn(C):
    nc, blk = C.nc, C.blk
    es2 = contextlib.ExitStack()
    wbuf = es2.enter_context(nc.sbuf_tensor("l1w", [128, NC * 1024], BF16))
    h1 = es2.enter_context(nc.sbuf_tensor("l1h1", [128, NC * T], BF16))
    h2 = es2.enter_context(nc.sbuf_tensor("l1h2", [128, NC * T], BF16))
    r_b = es2.enter_context(nc.sbuf_tensor("l1rb", [1, 1024], BF16))
    tmp = es2.enter_context(nc.sbuf_tensor("l1tmp", [128, 512], F32))
    pf = [es2.enter_context(nc.psum_tensor(f"l1p{j}", [128, 512], F32))
          for j in range(4)]

    def load_w(k, first):
        pe_now = C.pe.n

        @on(blk, 'gpsimd')
        def _(q):
            if not first:
                q.wait_ge(C.pe.h, pe_now)
            C.cast_load(q, wbuf[:].rearrange("p (c o) -> p c o", c=NC),
                        C.win[k].rearrange("(c p) o -> p c o", p=128))
            if first:
                C.cast_load(q, r_b[:], C.win["l1_ffb"].unsqueeze(0))
        return C.wq.n

    def feat_stage(wsb, src3, dst, pe_base, dv_base, wq0):
        @on(blk, 'tensor')
        def _(q):
            q.wait_ge(C.wq.h, wq0)
            u = 0
            for dtile in range(NC):
                for tch in range(4):
                    if u >= 4:
                        q.wait_ge(C.dv.h, dv_base + u - 3)
                    last = None
                    for c in range(NC):
                        last = q.matmul(
                            pf[u % 4][:], wsb[:, c * 1024 + dtile * 128:
                                              c * 1024 + (dtile + 1) * 128],
                            src3[:, c, tch * 512:(tch + 1) * 512],
                            start=(c == 0), stop=(c == NC - 1))
                    C.pe.inc(last)
                    u += 1

        @on(blk, 'vector')
        def _(q):
            dst3 = dst[:].rearrange("p (c t) -> p c t", c=NC)
            u = 0
            for dtile in range(NC):
                for tch in range(4):
                    q.wait_ge(C.pe.h, pe_base + u + 1)
                    C.dv.inc(q.tensor_copy(
                        dst3[:, dtile, tch * 512:(tch + 1) * 512], pf[u % 4][:]))
                    u += 1

    wq0 = load_w("l1_ffw", True)
    feat_stage(wbuf, C.xnf3(), h1, C.pe.n, C.dv.n, wq0)
    wq0 = load_w("l1_ffa1", False)
    feat_stage(wbuf, h1[:].rearrange("p (c t) -> p c t", c=NC), h2,
               C.pe.n, C.dv.n, wq0)
    wq0 = load_w("l1_ffa2", False)

    pe0, ac0 = C.pe.n, C.ac.n
    h23 = h2[:].rearrange("p (c t) -> p c t", c=NC)

    @on(blk, 'tensor')
    def _(q):
        q.wait_ge(C.wq.h, wq0)
        u = 0
        for i in range(NT):
            for oc in range(2):
                if u >= 4:
                    q.wait_ge(C.ac.h, ac0 + u - 3)
                last = None
                for c in range(NC):
                    last = q.matmul(pf[u % 4][:], h23[:, c, i * 128:(i + 1) * 128],
                                    wbuf[:, c * 1024 + oc * 512:
                                         c * 1024 + (oc + 1) * 512],
                                    start=(c == 0), stop=False)
                last = q.matmul(pf[u % 4][:], C.c_or[:1, :],
                                r_b[:1, oc * 512:(oc + 1) * 512],
                                start=False, stop=True)
                C.pe.inc(last)
                u += 1

    @on(blk, 'scalar')
    def _(q):
        u = 0
        for i in range(NT):
            for oc in range(2):
                q.wait_ge(C.pe.h, pe0 + u + 1)
                q.activation(out=tmp[:], in_=pf[u % 4][:], func=AF.Tanh)
                C.ac.inc(q.activation(
                    out=C.g[:, i * 1024 + oc * 512:i * 1024 + (oc + 1) * 512],
                    in_=tmp[:], func=AF.Silu))
                u += 1

    C.barrier()
    es2.close()


def body_layer1(C):
    esl = contextlib.ExitStack()
    C.ensure_consts(CONST_NEED[1], esl)
    C.g = esl.enter_context(C.nc.sbuf_tensor("g1", [128, NT * D], F32))
    with C.nc.named_scope("l1_ln"):
        emit_ln_transpose(C)
    with C.nc.named_scope("l1_ffn"):
        emit_l1_ffn(C)
    aof = esl.enter_context(C.nc.sbuf_tensor("aof1", [128, NC * T], BF16))
    with C.nc.named_scope("l1_attn"):
        emit_st_attention(C, 1, aof)
    with C.nc.named_scope("l1_comb"):
        emit_combine(C, aof, "l1_WOT", "l1_bO")
        C.barrier()
    esl.close()


def build_layer1(W, dbg=False, stop_after=None, npairs=H // 2, substage=None):
    C = LayerProg(W, LAYER_KEYS[1], no_g=True, need=CONST_NEED[1])
    C._keep_attn_scope = dbg
    C.emit_const_loads()
    body_layer1(C)
    C.close()
    return C


def emit_l2_ffn(C, g_dram):
    nc, blk = C.nc, C.blk
    es2 = contextlib.ExitStack()
    FH = 4096
    NFC = FH // 128
    hA = es2.enter_context(nc.sbuf_tensor("l2hA", [128, NFC * T], BF16))
    stg = es2.enter_context(nc.sbuf_tensor("l2stg", [128, 1024], BF16))
    hB_dram = nc.dram_tensor("l2_hscr", [FH, T], BF16).ap()
    scol = es2.enter_context(nc.sbuf_tensor("l2scol", [128, 8], F32))
    s3row = es2.enter_context(nc.sbuf_tensor("l2s3r", [1, 1024], F32))
    s3b = es2.enter_context(nc.sbuf_tensor("l2s3b", [128, 1024], F32))
    tmp2 = es2.enter_context(nc.sbuf_tensor("l2tmp2", [128, 1024], F32))
    pp = [es2.enter_context(nc.psum_tensor(f"l2pp{j}", [128, 512], F32))
          for j in range(4)]
    es3 = contextlib.ExitStack()
    w12 = [es3.enter_context(nc.sbuf_tensor(f"l2w12_{j}", [128, NC * 2 * 128], BF16))
           for j in range(2)]
    whs = [es3.enter_context(nc.sbuf_tensor(f"l2whs{j}", [128, NFC * 128], BF16))
           for j in range(2)]
    tmp = es3.enter_context(nc.sbuf_tensor("l2tmp", [128, 1024], F32))

    hA3 = hA[:].rearrange("p (c t) -> p c t", c=NFC)

    pe0, ac0, dv0, wq0 = C.pe.n, C.ac.n, C.dv.n, C.wq.n

    @on(blk, 'gpsimd')
    def _(q):
        for m in range(NFC):
            if m >= 2:
                q.wait_ge(C.pe.h, pe0 + 4 * (m - 1))
            wv = w12[m % 2][:].rearrange("p (c j o) -> p c j o", c=NC, j=2)
            C.cast_load(q, wv[:, :, 0, :],
                        C.win["l2_w1T"][:, m * 128:(m + 1) * 128]
                        .rearrange("(c p) o -> p c o", p=128))
            C.cast_load(q, wv[:, :, 1, :],
                        C.win["l2_w2T"][:, m * 128:(m + 1) * 128]
                        .rearrange("(c p) o -> p c o", p=128))

    @on(blk, 'sync')
    def _(q):
        for m in range(NFC):
            if m >= 2:
                q.wait_ge(C.ac.h, ac0 + 4 * (m - 1))
                q.wait_ge(C.dv.h, dv0 + 4 * (m - 1))
            C.xq.inc(q.dma_start(
                out=scol[:, 2 * (m % 2):2 * (m % 2) + 1],
                in_=C.win["l2_w1_s"][m * 128:(m + 1) * 128].unsqueeze(1)), 16)
            C.xq.inc(q.dma_start(
                out=scol[:, 2 * (m % 2) + 1:2 * (m % 2) + 2],
                in_=C.win["l2_w2_s"][m * 128:(m + 1) * 128].unsqueeze(1)), 16)
    xq0 = C.xq.n

    @on(blk, 'tensor')
    def _(q):
        for m in range(NFC):
            q.wait_ge(C.wq.h, wq0 + 32 * (m + 1))
            wv = w12[m % 2][:].rearrange("p (c j o) -> p c j o", c=NC, j=2)
            for tc in range(4):
                u = 4 * m + tc
                if u >= 2:
                    q.wait_ge(C.dv.h, dv0 + u - 1)
                for c in range(NC):
                    q.matmul(pp[2 * (u % 2)][:], wv[:, c, 0, :],
                             C.xnf3()[:, c, tc * 512:(tc + 1) * 512],
                             start=(c == 0), stop=(c == NC - 1))
                for c in range(NC):
                    last = q.matmul(pp[2 * (u % 2) + 1][:], wv[:, c, 1, :],
                                    C.xnf3()[:, c, tc * 512:(tc + 1) * 512],
                                    start=(c == 0), stop=(c == NC - 1))
                C.pe.inc(last)

    @on(blk, 'scalar')
    def _(q):
        for m in range(NFC):
            for tc in range(4):
                u = 4 * m + tc
                q.wait_ge(C.pe.h, pe0 + u + 1)
                if tc == 0:
                    q.wait_ge(C.xq.h, xq0 - 16 * 2 * (NFC - 1 - m))
                if u >= 2:
                    q.wait_ge(C.dv.h, dv0 + u - 1)
                C.ac.inc(q.activation(out=tmp[:, (u % 2) * 512:(u % 2) * 512 + 512],
                                      in_=pp[2 * (u % 2)][:],
                                      func=AF.Silu,
                                      scale=scol[:, 2 * (m % 2):2 * (m % 2) + 1]))

    @on(blk, 'vector')
    def _(q):
        for m in range(NFC):
            for tc in range(4):
                u = 4 * m + tc
                q.wait_ge(C.ac.h, ac0 + u + 1)
                C.dv.inc(q.scalar_tensor_tensor(
                    hA3[:, m, tc * 512:(tc + 1) * 512],
                    pp[2 * (u % 2) + 1][:],
                    scol[:, 2 * (m % 2) + 1:2 * (m % 2) + 2],
                    tmp[:, (u % 2) * 512:(u % 2) * 512 + 512],
                    ALU.mult, ALU.mult))

    C.barrier()

    def mid_stage(wkey, skey):
        pe0, ac0, wq0, yq0 = C.pe.n, C.ac.n, C.wq.n, C.yq.n

        @on(blk, 'gpsimd')
        def _(q):
            for m in range(NFC):
                if m >= 2:
                    q.wait_ge(C.pe.h, pe0 + 4 * (m - 1))
                C.cast_load(q, whs[m % 2][:].rearrange("p (c o) -> p c o", c=NFC),
                            C.win[wkey][:, m * 128:(m + 1) * 128]
                            .rearrange("(c p) o -> p c o", p=128))

        scol_xq = []

        def store_u(q, u):
            m2, tc2 = u // 4, u % 4
            q.wait_ge(C.ac.h, ac0 + u + 1)
            C.yq.inc(q.dma_start(
                out=hB_dram[m2 * 128:(m2 + 1) * 128,
                            tc2 * 512:(tc2 + 1) * 512],
                in_=stg[:, (u % 2) * 512:(u % 2) * 512 + 512]), 16)

        @on(blk, 'sync')
        def _(q):
            for u in range(4 * NFC):
                if u % 4 == 0:
                    m = u // 4
                    if m >= 2:
                        q.wait_ge(C.ac.h, ac0 + 4 * (m - 1))
                    C.xq.inc(q.dma_start(
                        out=scol[:, 4 + (m % 2):5 + (m % 2)],
                        in_=C.win[skey][m * 128:(m + 1) * 128].unsqueeze(1)), 16)
                    scol_xq.append(C.xq.n)
                if u >= 4:
                    store_u(q, u - 4)
            for u in range(4 * NFC - 4, 4 * NFC):
                store_u(q, u)
            q.wait_ge(C.yq.h, C.yq.n)

        @on(blk, 'tensor')
        def _(q):
            for m in range(NFC):
                q.wait_ge(C.wq.h, wq0 + 16 * (m + 1))
                ws = whs[m % 2][:].rearrange("p (c o) -> p c o", c=NFC)
                for tc in range(4):
                    u = 4 * m + tc
                    if u >= 4:
                        q.wait_ge(C.ac.h, ac0 + u - 3)
                    last = None
                    for c in range(NFC):
                        last = q.matmul(pp[u % 4][:], ws[:, c, :],
                                        hA3[:, c, tc * 512:(tc + 1) * 512],
                                        start=(c == 0), stop=(c == NFC - 1))
                    C.pe.inc(last)

        @on(blk, 'scalar')
        def _(q):
            for m in range(NFC):
                for tc in range(4):
                    u = 4 * m + tc
                    q.wait_ge(C.pe.h, pe0 + u + 1)
                    if tc == 0:
                        q.wait_ge(C.xq.h, scol_xq[m])
                    if u >= 2:
                        q.wait_ge(C.yq.h, yq0 + 16 * (u - 1))
                    C.ac.inc(q.activation(
                        out=stg[:, (u % 2) * 512:(u % 2) * 512 + 512],
                        in_=pp[u % 4][:], func=AF.Gelu,
                        scale=scol[:, 4 + (m % 2):5 + (m % 2)]))

        @on(blk, 'sync')
        def _(q):
            for m in range(NFC):
                for tc in range(4):
                    u = 4 * m + tc
                    q.wait_ge(C.ac.h, ac0 + u + 1)
                    C.yq.inc(q.dma_start(
                        out=hB_dram[m * 128:(m + 1) * 128,
                                    tc * 512:(tc + 1) * 512],
                        in_=stg[:, (u % 2) * 512:(u % 2) * 512 + 512]), 16)
            q.wait_ge(C.yq.h, C.yq.n)
        C.barrier()

        @on(blk, 'sync')
        def _(q):
            C.xq.inc(q.dma_start(
                out=hA3, in_=hB_dram.rearrange("(c p) t -> p c t", p=128)), 16)
        C.barrier()

    mid_stage("l2_h1T", "l2_h1_s")
    mid_stage("l2_h2T", "l2_h2_s")
    es3.close()

    pe0, ac0, dv0, wq0 = C.pe.n, C.ac.n, C.dv.n, C.wq.n
    sq0 = C.sq.n

    @on(blk, 'sync')
    def _(q):
        C.xq.inc(q.dma_start(out=s3row[:],
                             in_=C.win["l2_w3_s"].unsqueeze(0)), 16)
    xq_s3 = C.xq.n

    @on(blk, 'tensor')
    def _(q):
        q.wait_ge(C.xq.h, xq_s3)
        for half in range(2):
            C.sq.inc(q.matmul(pp[3][:], C.c_orf[0:1, :],
                              s3row[0:1, half * 512:(half + 1) * 512],
                              start=True, stop=True))
            q.wait_ge(C.dv.h, dv0 + half + 1)

    @on(blk, 'vector')
    def _(q):
        for half in range(2):
            q.wait_ge(C.sq.h, sq0 + half + 1)
            C.dv.inc(q.tensor_copy(s3b[:, half * 512:(half + 1) * 512], pp[3][:]))
    dv_s3 = C.dv.n

    NOC = 4
    OW = 1024 // NOC
    w3h = es2.enter_context(nc.sbuf_tensor("l2w3h", [128, NFC * OW], BF16))
    gsb = es2.enter_context(nc.sbuf_tensor("l2gsb", [128, 2 * OW], F32))
    w3h3 = w3h[:].rearrange("p (c o) -> p c o", c=NFC)
    pe4, ac4, dv4, yq0 = C.pe.n, C.ac.n, C.dv.n, C.yq.n

    @on(blk, 'gpsimd')
    def _(q):
        for oc in range(NOC):
            if oc >= 1:
                q.wait_ge(C.pe.h, pe4 + oc * NT)
            C.cast_load(q, w3h3,
                        C.win["l2_w3T"][:, oc * OW:(oc + 1) * OW]
                        .rearrange("(c p) o -> p c o", p=128))
    wq_w3 = C.wq.n

    @on(blk, 'tensor')
    def _(q):
        for u in range(NOC * NT):
            oc, i = u // NT, u % NT
            q.wait_ge(C.wq.h, wq_w3 - 16 * (NOC - 1 - oc))
            if u >= 2:
                q.wait_ge(C.dv.h, dv_s3 + u - 1)
            last = None
            for c in range(NFC):
                last = q.matmul(pp[u % 2][:, 0:OW],
                                hA3[:, c, i * 128:(i + 1) * 128],
                                w3h3[:, c, :],
                                start=(c == 0), stop=(c == NFC - 1))
            C.pe.inc(last)

    @on(blk, 'vector')
    def _(q):
        for u in range(NOC * NT):
            oc, i = u // NT, u % NT
            q.wait_ge(C.pe.h, pe4 + u + 1)
            if u >= 2:
                q.wait_ge(C.ac.h, ac4 + u - 1)
            C.dv.inc(q.tensor_mul(tmp2[:, (u % 2) * OW:(u % 2) * OW + OW],
                                  pp[u % 2][:, 0:OW],
                                  s3b[:, oc * OW:(oc + 1) * OW]))

    @on(blk, 'scalar')
    def _(q):
        for u in range(NOC * NT):
            oc, i = u // NT, u % NT
            q.wait_ge(C.dv.h, dv_s3 + u + 1)
            if u >= 2:
                q.wait_ge(C.yq.h, yq0 + 16 * (u - 1))
            C.ac.inc(q.activation(
                out=gsb[:, (u % 2) * OW:(u % 2) * OW + OW],
                in_=tmp2[:, (u % 2) * OW:(u % 2) * OW + OW], func=AF.Silu))

    @on(blk, 'sync')
    def _(q):
        for u in range(NOC * NT):
            oc, i = u // NT, u % NT
            q.wait_ge(C.ac.h, ac4 + u + 1)
            C.yq.inc(q.dma_start(
                out=g_dram[i * 128:(i + 1) * 128, oc * OW:(oc + 1) * OW],
                in_=gsb[:, (u % 2) * OW:(u % 2) * OW + OW]), 16)
        q.wait_ge(C.yq.h, C.yq.n)

    C.barrier()
    es2.close()


def emit_l3_ffn(C, xnlo=None, es_lo=None, g_es=None, msktm=None,
                scol=None, rtm=None):
    nc, blk = C.nc, C.blk
    es2 = contextlib.ExitStack()
    FE = 2048
    NEF = FE // 128
    TH = 1024

    msk3 = msktm[:].rearrange("p (t e) -> p t e", t=NT)
    rtm3 = rtm[:].rearrange("p (t e) -> p t e", t=NT)
    es3 = contextlib.ExitStack()
    wr32 = es3.enter_context(nc.sbuf_tensor("l3wr32", [128, NC * 16], F32))
    wr = es3.enter_context(nc.sbuf_tensor("l3wr", [128, NC * 16], BF16))
    wrl = es3.enter_context(nc.sbuf_tensor("l3wrl", [128, NC * 16], BF16))
    rst = es3.enter_context(nc.sbuf_tensor("l3rst", [128, 8], F32))
    oh1 = es3.enter_context(nc.sbuf_tensor("l3oh1", [128, 16], F32))
    pr = es3.enter_context(nc.psum_tensor("l3pr", [128, 16], F32))

    @on(blk, 'sync')
    def _(q):
        C.xq.inc(q.dma_start(
            out=wr32[:].rearrange("p (c o) -> p c o", c=NC),
            in_=C.win["l3_routerT"].rearrange("(c p) o -> p c o", p=128)), 16)
    xq_r = C.xq.n

    @on(blk, 'vector')
    def _(q):
        q.wait_ge(C.xq.h, xq_r)
        q.wait_ge(C.dv.h, C.dv.inc(q.tensor_copy(wr[:], wr32[:])))
        C.dv.inc(q.tensor_sub(wrl[:], wr32[:], wr[:]))
    wq0 = C.wq.n
    dv_wr = C.dv.n
    pe0, dv0, ac0, sq0, sv0 = C.pe.n, C.dv.n, C.ac.n, C.sq.n, C.sv.n

    @on(blk, 'tensor')
    def _(q):
        q.wait_ge(C.dv.h, dv_wr)
        for i in range(NT):
            if i >= 1:
                q.wait_ge(C.dv.h, dv0 + 7 * i)
            last = None
            for c in range(NC):
                q.matmul(pr[:], C.xnf3()[:, c, i * 128:(i + 1) * 128],
                         wr[:, c * 16:(c + 1) * 16],
                         start=(c == 0), stop=False)
            for c in range(NC):
                q.matmul(pr[:], C.xnf3()[:, c, i * 128:(i + 1) * 128],
                         wrl[:, c * 16:(c + 1) * 16],
                         start=False, stop=False)
            for c in range(NC):
                last = q.matmul(
                    pr[:],
                    xnlo[:].rearrange("p (c t) -> p c t", c=NC)
                    [:, c, i * 128:(i + 1) * 128],
                    wr[:, c * 16:(c + 1) * 16],
                    start=False, stop=(c == NC - 1))
            C.pe.inc(last)

    @on(blk, 'vector')
    def _(q):
        for i in range(NT):
            q.wait_ge(C.pe.h, pe0 + i + 1)
            q.wait_ge(C.dv.h, C.dv.inc(
                q.reduce_max(rst[:, 0:1], pr[:], axis=AX.X)))
            q.wait_ge(C.dv.h, C.dv.inc(
                q.tensor_scalar(oh1[:], pr[:], rst[:, 0:1], None, ALU.is_ge)))
            q.wait_ge(C.dv.h, C.dv.inc(
                q.tensor_mul(oh1[:], oh1[:], C.c_iota64m[:])))
            q.wait_ge(C.dv.h, C.dv.inc(
                q.tensor_reduce(rst[:, 1:2], oh1[:], AX.X, ALU.min)))
            q.wait_ge(C.dv.h, C.dv.inc(
                q.tensor_scalar_add(rst[:, 1:2], rst[:, 1:2], 64.0)))
            q.wait_ge(C.ac.h, ac0 + i + 1)
            C.dv.inc(q.tensor_scalar(msk3[:, i, :], C.c_iota[:], rst[:, 1:2],
                                     None, ALU.is_equal))
            C.dv.inc(q.tensor_copy(scol[:, i:i + 1], rst[:, 2:3]))

    @on(blk, 'scalar')
    def _(q):
        for i in range(NT):
            q.wait_ge(C.dv.h, dv0 + 7 * i + 1)
            C.ac.inc(q.activation(out=rst[:, 2:3], in_=rst[:, 0:1],
                                  func=AF.Sigmoid))
    C.barrier()
    es3.close()
    if es_lo is not None:
        es_lo.close()
    if C.g is None:
        C.g = (g_es or C.es).enter_context(
            nc.sbuf_tensor("g3", [128, NT * D], F32))

    escs = contextlib.ExitStack()
    prc = [escs.enter_context(nc.psum_tensor(f"l3prc{j}", [128, 512], F32))
           for j in range(2)]
    pecs, dvcs = C.pe.n, C.dv.n

    @on(blk, 'tensor')
    def _(q):
        for i in range(NT):
            if i >= 2:
                q.wait_ge(C.dv.h, dvcs + 2 * (i - 1))
            last = None
            for j in range(i + 1):
                last = q.matmul(prc[i % 2][:, 0:16],
                                (C.c_ut if j == i else C.c_ao)[:],
                                msk3[:, j, :], start=(j == 0), stop=(j == i))
            C.pe.inc(last)

    @on(blk, 'vector')
    def _(q):
        for i in range(NT):
            q.wait_ge(C.pe.h, pecs + i + 1)
            q.wait_ge(C.dv.h, C.dv.inc(q.scalar_tensor_tensor(
                rtm3[:, i, :], prc[i % 2][:, 0:16], 1.0, msk3[:, i, :],
                ALU.mult, ALU.mult)))
            C.dv.inc(q.tensor_scalar_add(rtm3[:, i, :], rtm3[:, i, :], -1.0))
    C.barrier()
    escs.close()

    es4 = contextlib.ExitStack()
    hsh = es4.enter_context(nc.sbuf_tensor("l3hsh", [128, NEF * T], BF16))
    wsl = [es4.enter_context(nc.sbuf_tensor(f"l3wsl{j}", [128, NC * 128], BF16))
           for j in range(2)]
    pg = [es4.enter_context(nc.psum_tensor(f"l3pg{j}", [128, 512], F32))
          for j in range(4)]
    hsh3 = hsh[:].rearrange("p (c t) -> p c t", c=NEF)

    def sh_pass(wkey, second):
        pe1, ac1, dv1, wq1 = C.pe.n, C.ac.n, C.dv.n, C.wq.n

        @on(blk, 'gpsimd')
        def _(q):
            for m in range(NEF):
                if m >= 2:
                    q.wait_ge(C.pe.h, pe1 + 4 * (m - 1))
                C.cast_load(q, wsl[m % 2][:].rearrange("p (c o) -> p c o", c=NC),
                            C.win[wkey][:, m * 128:(m + 1) * 128]
                            .rearrange("(c p) o -> p c o", p=128))

        @on(blk, 'tensor')
        def _(q):
            for m in range(NEF):
                q.wait_ge(C.wq.h, wq1 + 16 * (m + 1))
                ws = wsl[m % 2][:].rearrange("p (c o) -> p c o", c=NC)
                for tc in range(4):
                    u = 4 * m + tc
                    if u >= 4:
                        if second:
                            q.wait_ge(C.dv.h, dv1 + u - 3)
                        else:
                            q.wait_ge(C.ac.h, ac1 + u - 3)
                    last = None
                    for c in range(NC):
                        last = q.matmul(pg[u % 4][:], ws[:, c, :],
                                        C.xnf3()[:, c, tc * 512:(tc + 1) * 512],
                                        start=(c == 0), stop=(c == NC - 1))
                    C.pe.inc(last)

        if not second:
            @on(blk, 'scalar')
            def _(q):
                for u in range(4 * NEF):
                    q.wait_ge(C.pe.h, pe1 + u + 1)
                    m, tc = u // 4, u % 4
                    C.ac.inc(q.activation(
                        out=hsh3[:, m, tc * 512:(tc + 1) * 512],
                        in_=pg[u % 4][:], func=AF.Silu))
        else:
            @on(blk, 'vector')
            def _(q):
                for u in range(4 * NEF):
                    q.wait_ge(C.pe.h, pe1 + u + 1)
                    m, tc = u // 4, u % 4
                    C.dv.inc(q.scalar_tensor_tensor(
                        hsh3[:, m, tc * 512:(tc + 1) * 512],
                        pg[u % 4][:], 1.0,
                        hsh3[:, m, tc * 512:(tc + 1) * 512],
                        ALU.mult, ALU.mult))
        C.barrier()

    sh_pass("l3_wgT", False)
    sh_pass("l3_wuT", True)

    wdl = es4.enter_context(nc.sbuf_tensor("l3wdl", [128, NEF * 512], BF16))
    pe2, dv2, wq2 = C.pe.n, C.dv.n, C.wq.n

    @on(blk, 'gpsimd')
    def _(q):
        for oc in range(2):
            if oc == 1:
                q.wait_ge(C.pe.h, pe2 + NT)
            C.cast_load(q, wdl[:].rearrange("p (c o) -> p c o", c=NEF),
                        C.win["l3_wdT"][:, oc * 512:(oc + 1) * 512]
                        .rearrange("(c p) o -> p c o", p=128))
    wq3 = C.wq.n
    wdl3 = wdl[:].rearrange("p (c o) -> p c o", c=NEF)

    @on(blk, 'tensor')
    def _(q):
        for u in range(2 * NT):
            oc, i = u // NT, u % NT
            q.wait_ge(C.wq.h, wq3 - 16 * (1 - oc))
            if u >= 2:
                q.wait_ge(C.dv.h, dv2 + u - 1)
            last = None
            for c in range(NEF):
                last = q.matmul(pg[u % 2][:], hsh3[:, c, i * 128:(i + 1) * 128],
                                wdl3[:, c, :], start=(c == 0), stop=(c == NEF - 1))
            C.pe.inc(last)

    @on(blk, 'vector')
    def _(q):
        for u in range(2 * NT):
            oc, i = u // NT, u % NT
            q.wait_ge(C.pe.h, pe2 + u + 1)
            C.dv.inc(q.tensor_copy(
                C.g[:, i * 1024 + oc * 512:i * 1024 + (oc + 1) * 512],
                pg[u % 2][:]))
    C.barrier()
    es4.close()

    CAP = 384
    NSC = CAP // 128
    es5 = contextlib.ExitStack()
    xntm = es5.enter_context(nc.sbuf_tensor("l3xntm", [128, NT * 1024], BF16))
    xg = es5.enter_context(nc.sbuf_tensor("l3xg", [128, 8 * CAP], BF16))
    yg = es5.enter_context(nc.sbuf_tensor("l3yg", [128, NSC * 1024], BF16))
    wge = [es5.enter_context(nc.sbuf_tensor(f"l3wge{j}", [128, NC * 512], BF16))
           for j in range(2)]
    wde = es5.enter_context(nc.sbuf_tensor("l3wde", [128, 16 * 512], BF16))
    ptp = [es5.enter_context(nc.psum_tensor(f"l3ptp{j}", [128, 1024], BF16))
           for j in range(2)]
    pga = [es5.enter_context(nc.psum_tensor(f"l3pga{j}", [128, 512], F32))
           for j in range(2)]
    pm = [es5.enter_context(nc.psum_tensor(f"l3pm{j}", [128, 512], F32))
          for j in range(4)]
    xnt3 = xntm[:].rearrange("p (t f) -> p t f", t=NT)
    xg3 = xg[:].rearrange("p (c s) -> p c s", c=8)
    yg3 = yg[:].rearrange("p (s f) -> p s f", s=NSC)
    wde3 = wde[:].rearrange("p (c o) -> p c o", c=16)

    sqX0, svX0 = C.sq.n, C.sv.n

    @on(blk, 'tensor')
    def _(q):
        for g in range(32):
            if g >= 2:
                q.wait_ge(C.sv.h, svX0 + g - 1)
            last = None
            for s in range(4):
                tt, fc = g // 2, (g % 2) * 4 + s
                last = q.transpose(ptp[g % 2][:, s * 128:(s + 1) * 128],
                                   C.xnf3()[:, fc, tt * 128:(tt + 1) * 128],
                                   C.c_ident[:])
            C.sq.inc(last)

    @on(blk, 'vector')
    def _(q):
        for g in range(32):
            q.wait_ge(C.sq.h, sqX0 + g + 1)
            tt, fc0 = g // 2, (g % 2) * 4
            C.sv.inc(q.tensor_copy(
                xnt3[:, tt, fc0 * 128:(fc0 + 4) * 128],
                ptp[g % 2][:, 0:512]))
    C.barrier()

    prev_scat_pe = None
    prev_gup_pe = None
    for e in range(16):
        es_e = contextlib.ExitStack()
        PT = es_e.enter_context(
            nc.sbuf_tensor("l3PT", [128, NSC * T], BF16))
        PT3 = PT[:].rearrange("p (s t) -> p s t", s=NSC)
        esP = contextlib.ExitStack()
        P = esP.enter_context(nc.sbuf_tensor("l3P", [128, NT * CAP], BF16))
        P3 = P[:].rearrange("p (t s) -> p t s", t=NT)

        dvPb0 = C.dv.n

        @on(blk, 'vector')
        def _(q, e=e, psc=prev_scat_pe):
            if psc is not None:
                q.wait_ge(C.pe.h, psc)
            for tt in range(NT):
                C.dv.inc(q.tensor_scalar(
                    P3[:, tt, :], C.c_icap[:, 0:CAP], rtm3[:, tt, e:e + 1],
                    None, ALU.is_equal))
        dvP01 = C.dv.n

        sqT0, svT0 = C.sq.n, C.sv.n

        @on(blk, 'tensor')
        def _(q, e=e):
            q.wait_ge(C.dv.h, dvP01)
            for g in range(4 * NSC):
                if g >= 2:
                    q.wait_ge(C.sv.h, svT0 + g - 1)
                last = None
                for s in range(4):
                    b = 4 * g + s
                    sc, tt = b // NT, b % NT
                    last = q.transpose(
                        ptp[g % 2][:, s * 128:(s + 1) * 128],
                        P3[:, tt, sc * 128:(sc + 1) * 128], C.c_ident[:])
                C.sq.inc(last)

        @on(blk, 'vector')
        def _(q, e=e):
            for g in range(4 * NSC):
                q.wait_ge(C.sq.h, sqT0 + g + 1)
                sc, tt0 = (4 * g) // NT, (4 * g) % NT
                C.sv.inc(q.tensor_copy(
                    PT3[:, sc, tt0 * 128:(tt0 + 4) * 128],
                    ptp[g % 2][:, 0:512]))
        svT1 = C.sv.n

        sqT1 = C.sq.n

        @on(blk, 'vector')
        def _(q, e=e):
            q.wait_ge(C.sq.h, sqT1)
            for tt in range(NT):
                C.dv.inc(q.tensor_scalar(
                    P3[:, tt, :], P3[:, tt, :], scol[:, tt:tt + 1],
                    None, ALU.mult))
        dvPb1 = C.dv.n

        peG0 = C.pe.n

        @on(blk, 'tensor')
        def _(q, e=e):
            q.wait_ge(C.dv.h, dvPb1)
            for ft in range(8):
                if ft >= 2:
                    q.wait_ge(C.dv.h, dvPb1 + ft - 1)
                last = None
                for tt in range(NT):
                    last = q.matmul(pga[ft % 2][:, 0:CAP],
                                    xnt3[:, tt, ft * 128:(ft + 1) * 128],
                                    P3[:, tt, :],
                                    start=(tt == 0), stop=(tt == NT - 1))
                C.pe.inc(last)

        @on(blk, 'vector')
        def _(q, e=e):
            for ft in range(8):
                q.wait_ge(C.pe.h, peG0 + ft + 1)
                C.dv.inc(q.tensor_copy(xg3[:, ft, :], pga[ft % 2][:, 0:CAP]))
        esP.close()
        hge = es_e.enter_context(
            nc.sbuf_tensor("l3hge", [128, 16 * CAP], BF16))
        hge3 = hge[:].rearrange("p (h s) -> p h s", h=16)

        peU0, acU0, wqU0 = C.pe.n, C.ac.n, C.wq.n
        dvUp = dvPb1 + 8

        @on(blk, 'gpsimd')
        def _(q, e=e, pgu=prev_gup_pe):
            for w8 in range(8):
                if w8 < 2:
                    if pgu is not None:
                        q.wait_ge(C.pe.h, pgu)
                else:
                    q.wait_ge(C.pe.h, peU0 + 4 * (w8 - 1))
                C.cast_load(
                    q, wge[w8 % 2][:].rearrange("p (c o) -> p c o", c=NC),
                    C.win["l3_gup"][e][:, w8 * 512:(w8 + 1) * 512]
                    .rearrange("(c p) o -> p c o", p=128))

        @on(blk, 'tensor')
        def _(q, e=e):
            q.wait_ge(C.dv.h, dvPb1 + 8)
            for w8 in range(8):
                q.wait_ge(C.wq.h, wqU0 + 16 * (w8 + 1))
                wg3 = wge[w8 % 2][:].rearrange("p (c o) -> p c o", c=NC)
                for sub in range(4):
                    u = 4 * w8 + sub
                    if u >= 4:
                        if w8 < 4:
                            q.wait_ge(C.ac.h, acU0 + u - 3)
                        else:
                            q.wait_ge(C.ac.h, acU0 + 16)
                            q.wait_ge(C.dv.h, dvUp + max(0, u - 19))
                    last = None
                    for c in range(NC):
                        last = q.matmul(
                            pm[u % 4][:, 0:CAP],
                            wg3[:, c, sub * 128:(sub + 1) * 128],
                            xg3[:, c, :], start=(c == 0), stop=(c == NC - 1))
                    C.pe.inc(last)
        prev_gup_pe = C.pe.n

        @on(blk, 'scalar')
        def _(q, e=e):
            for u in range(16):
                q.wait_ge(C.pe.h, peU0 + u + 1)
                C.ac.inc(q.activation(out=hge3[:, u, :],
                                      in_=pm[u % 4][:, 0:CAP], func=AF.Silu))

        @on(blk, 'vector')
        def _(q, e=e):
            for k in range(16):
                q.wait_ge(C.pe.h, peU0 + 16 + k + 1)
                q.wait_ge(C.ac.h, acU0 + k + 1)
                C.dv.inc(q.scalar_tensor_tensor(
                    hge3[:, k, :], pm[(16 + k) % 4][:, 0:CAP], 1.0,
                    hge3[:, k, :], ALU.mult, ALU.mult))

        peD0, wqD0 = C.pe.n, C.wq.n
        dvD0 = dvUp + 16

        @on(blk, 'gpsimd')
        def _(q, e=e):
            for oc in range(2):
                if oc == 0:
                    q.wait_ge(C.pe.h, peU0)
                else:
                    q.wait_ge(C.pe.h, peD0 + NSC)
                C.cast_load(
                    q, wde3, C.win["l3_down"][e][:, oc * 512:(oc + 1) * 512]
                    .rearrange("(c p) o -> p c o", p=128))

        @on(blk, 'tensor')
        def _(q, e=e):
            q.wait_ge(C.dv.h, dvD0)
            q.wait_ge(C.ac.h, acU0 + 16)
            for u in range(2 * NSC):
                oc, st = u // NSC, u % NSC
                q.wait_ge(C.wq.h, wqD0 + 16 * (oc + 1))
                if u >= 2:
                    q.wait_ge(C.dv.h, dvD0 + u - 1)
                last = None
                for hc in range(16):
                    last = q.matmul(
                        pga[u % 2][:], hge3[:, hc, st * 128:(st + 1) * 128],
                        wde3[:, hc, :], start=(hc == 0), stop=(hc == 15))
                C.pe.inc(last)

        @on(blk, 'vector')
        def _(q, e=e):
            for u in range(2 * NSC):
                oc, st = u // NSC, u % NSC
                q.wait_ge(C.pe.h, peD0 + u + 1)
                C.dv.inc(q.tensor_copy(yg3[:, st, oc * 512:(oc + 1) * 512],
                                       pga[u % 2][:]))

        peS0 = C.pe.n
        dvS0 = dvD0 + 2 * NSC

        @on(blk, 'tensor')
        def _(q, e=e):
            q.wait_ge(C.dv.h, dvS0)
            q.wait_ge(C.sv.h, svT1)
            for g in range(32):
                tt, nh = g // 2, g % 2
                if g >= 4:
                    q.wait_ge(C.dv.h, dvS0 + g - 3)
                last = None
                for sc in range(NSC):
                    last = q.matmul(
                        pm[g % 4][:], PT3[:, sc, tt * 128:(tt + 1) * 128],
                        yg3[:, sc, nh * 512:(nh + 1) * 512],
                        start=(sc == 0), stop=(sc == NSC - 1))
                C.pe.inc(last)
        prev_scat_pe = C.pe.n

        @on(blk, 'vector')
        def _(q, e=e):
            for g in range(32):
                tt, nh = g // 2, g % 2
                q.wait_ge(C.pe.h, peS0 + g + 1)
                gsl = C.g[:, tt * 1024 + nh * 512:tt * 1024 + (nh + 1) * 512]
                C.dv.inc(q.tensor_add(gsl, gsl, pm[g % 4][:]))
        C.barrier(engs=("tensor", "vector", "scalar", "sync"))
        es_e.close()

    C.barrier()
    es5.close()

    @on(blk, 'scalar')
    def _(q):
        for i in range(NT):
            C.ac.inc(q.activation(out=C.g[:, i * 1024:(i + 1) * 1024],
                                  in_=C.g[:, i * 1024:(i + 1) * 1024],
                                  func=AF.Silu))
    C.barrier()
    if getattr(C, "_dump_g3", False):
        dgt = nc.dram_tensor("dbg_g3", [128, NT * 1024], F32,
                             kind="ExternalOutput").ap()
        drt = nc.dram_tensor("dbg_rtm", [128, NT * 16], F32,
                             kind="ExternalOutput").ap()
        dsc = nc.dram_tensor("dbg_scol", [128, NT], F32,
                             kind="ExternalOutput").ap()

        @on(blk, 'sync')
        def _(q):
            C.yq.inc(q.dma_start(out=dgt, in_=C.g[:]), 16)
            C.yq.inc(q.dma_start(out=drt, in_=rtm[:]), 16)
            C.yq.inc(q.dma_start(out=dsc, in_=scol[:]), 16)
            q.wait_ge(C.yq.h, C.yq.n)
    es2.close()


def emit_l3_attn(C, aof):
    nc, blk = C.nc, C.blk
    es2 = contextlib.ExitStack()
    EPS = 1e-6
    aot = es2.enter_context(nc.sbuf_tensor("l3aot", [128, NT * D], BF16))
    wp = es2.enter_context(nc.sbuf_tensor("l3wp", [128, NC * 128], BF16))
    qf = es2.enter_context(nc.sbuf_tensor("l3qf", [128, T], BF16))
    kf = es2.enter_context(nc.sbuf_tensor("l3kf", [128, T], BF16))
    vf = es2.enter_context(nc.sbuf_tensor("l3vf", [128, T], BF16))
    qt = es2.enter_context(nc.sbuf_tensor("l3qt", [128, NT * 128], BF16))
    kt = es2.enter_context(nc.sbuf_tensor("l3kt", [128, NT * 128], BF16))
    vt = es2.enter_context(nc.sbuf_tensor("l3vt", [128, NT * 128], BF16))
    bcol = es2.enter_context(nc.sbuf_tensor("l3bcol", [128, 4], F32))
    sums = es2.enter_context(nc.sbuf_tensor("l3sums", [128, 4], F32))
    sumb = es2.enter_context(nc.sbuf_tensor("l3sumb", [128, 4], BF16))
    tcol = es2.enter_context(nc.sbuf_tensor("l3tcol", [128, 4], F32))
    tcolb = es2.enter_context(nc.sbuf_tensor("l3tcolb", [128, 4], BF16))
    esc = es2.enter_context(nc.sbuf_tensor("l3esc", [1, 8], F32))
    escb = es2.enter_context(nc.sbuf_tensor("l3escb", [1, 8], BF16))
    stat = es2.enter_context(nc.sbuf_tensor("l3stat", [128, 4 * NT], F32))
    stb = es2.enter_context(nc.sbuf_tensor("l3stb", [128, 2 * NT], BF16))
    csr = es2.enter_context(nc.sbuf_tensor("l3csr", [1, T], F32))
    csb = vf
    rr = es2.enter_context(nc.sbuf_tensor("l3rr", [1, 4], F32))
    sct = es2.enter_context(nc.sbuf_tensor("l3sct", [128, NT], F32))
    vsb = es2.enter_context(nc.sbuf_tensor("l3vsb", [128, 2 * 64], BF16))
    kvb = es2.enter_context(nc.sbuf_tensor("l3kvb", [128, 64], BF16))
    pA = [es2.enter_context(nc.psum_tensor(f"l3pA{j}", [128, 1024], F32))
          for j in range(2)]
    pB = es2.enter_context(nc.psum_tensor("l3pB", [128, 512], F32))
    pS = es2.enter_context(nc.psum_tensor("l3pS", [128, 512], BF16))
    pD = es2.enter_context(nc.psum_tensor("l3pD", [128, 512], F32))
    pE = es2.enter_context(nc.psum_tensor("l3pE", [128, 512], F32))

    wnames = ("l3_Wq", "l3_Wk", "l3_Wv")
    bnames = ("l3_bq", "l3_bk", "l3_bv")

    for p in range(H // 2):
        wv3 = wp[:].rearrange("p (c o) -> p c o", c=NC)
        sqld = C.sq.n

        @on(blk, 'gpsimd')
        def _(q, p=p, sqn=C.sq.n):
            if p >= 1:
                q.wait_ge(C.sq.h, sqn)
            for j in range(3):
                if j >= 1:
                    q.wait_ge(C.sq.h, sqld + 2 * j)
                C.cast_load(q, wv3,
                            C.win[wnames[j]][:, p * 128:(p + 1) * 128]
                            .rearrange("(c q) o -> q c o", q=128))

        @on(blk, 'sync')
        def _(q, p=p, acn=C.ac.n):
            if p >= 1:
                q.wait_ge(C.ac.h, acn)
            for j in range(3):
                C.xq.inc(q.dma_start(out=bcol[:, j:j + 1],
                                     in_=C.win[bnames[j]][p * 128:(p + 1) * 128]
                                     .unsqueeze(1)), 16)
        wq_p, xq_p = C.wq.n, C.xq.n
        sq0, ac0 = C.sq.n, C.ac.n

        @on(blk, 'tensor')
        def _(q, p=p, sen=C.se.n, svn=C.sv.n, dvn=C.dv.n):
            if p >= 1:
                q.wait_ge(C.sv.h, svn)
                q.wait_ge(C.dv.h, dvn)
            for u in range(6):
                j, hv = u // 2, u % 2
                q.wait_ge(C.wq.h, wq_p - 16 * (2 - j))
                if u >= 2:
                    q.wait_ge(C.ac.h, ac0 + u - 1)
                dst = pA[u % 2]
                last = None
                for half in range(2):
                    for c in range(NC):
                        last = q.matmul(
                            dst[:, half * 512:(half + 1) * 512],
                            wv3[:, c, :],
                            C.xnf3()[:, c, hv * 1024 + half * 512:
                                     hv * 1024 + (half + 1) * 512],
                            start=(c == 0), stop=(c == NC - 1))
                C.sq.inc(last)

        @on(blk, 'scalar')
        def _(q, p=p):
            q.wait_ge(C.xq.h, xq_p)
            for u in range(6):
                j, hv = u // 2, u % 2
                tgt = (qf, kf, vf)[j]
                q.wait_ge(C.sq.h, sq0 + u + 1)
                fn = AF.Sigmoid if j < 2 else AF.Identity
                C.ac.inc(q.activation(out=tgt[:, hv * 1024:(hv + 1) * 1024],
                                      in_=pA[u % 2][:], func=fn,
                                      bias=bcol[:, j:j + 1]))
        ac_pr = C.ac.n

        sq1, sv1 = C.sq.n, C.sv.n

        @on(blk, 'tensor')
        def _(q, p=p):
            q.wait_ge(C.ac.h, ac_pr)
            g = 0
            for (srcT, dstT) in ((qf, qt), (kf, kt), (vf, vt)):
                for grp in range(4):
                    if g >= 1:
                        q.wait_ge(C.sv.h, sv1 + g)
                    last = None
                    for s in range(4):
                        i = grp * 4 + s
                        last = q.transpose(pS[:, s * 128:(s + 1) * 128],
                                           srcT[:, i * 128:(i + 1) * 128],
                                           C.c_ident[:])
                    C.sq.inc(last)
                    g += 1

        @on(blk, 'vector')
        def _(q, p=p):
            g = 0
            for (srcT, dstT) in ((qf, qt), (kf, kt), (vf, vt)):
                for grp in range(4):
                    q.wait_ge(C.sq.h, sq1 + g + 1)
                    C.sv.inc(q.tensor_copy(
                        dstT[:, grp * 512:(grp + 1) * 512], pS[:]))
                    g += 1
        sv_tr = C.sv.n

        @on(blk, 'vector')
        def _(q, p=p):
            q.wait_ge(C.ac.h, ac_pr)
            q.wait_ge(C.dv.h, C.dv.inc(
                q.reduce_sum(sums[:, 0:1], kf[:], axis=AX.X)))
            q.wait_ge(C.dv.h, C.dv.inc(
                q.tensor_scalar_add(sums[:, 0:1], sums[:, 0:1], EPS)))
            q.wait_ge(C.dv.h, C.dv.inc(
                q.reduce_sum(sums[:, 1:2], qf[:], axis=AX.X)))
            q.wait_ge(C.dv.h, C.dv.inc(
                q.tensor_scalar_add(sums[:, 1:2], sums[:, 1:2], EPS)))
            C.dv.inc(q.tensor_copy(sumb[:, 0:2], sums[:, 0:2]))
        dv_sums = C.dv.n

        for hh in range(2):
            h = 2 * p + hh
            base = hh * 64

            sq2 = C.sq.n

            @on(blk, 'tensor')
            def _(q, hh=hh, base=base):
                q.wait_ge(C.dv.h, dv_sums)
                q.wait_ge(C.sv.h, sv_tr)
                C.sq.inc(q.matmul(pD[0:1, 20:21], sumb[base:base + 64, 0:1],
                                  C.c_oc[base:base + 64, 0:1],
                                  start=True, stop=True))
                C.sq.inc(q.matmul(pD[0:1, 21:22], sumb[base:base + 64, 1:2],
                                  C.c_oc[base:base + 64, 0:1],
                                  start=True, stop=True))

            @on(blk, 'vector')
            def _(q, hh=hh):
                q.wait_ge(C.sq.h, sq2 + 2)
                q.wait_ge(C.dv.h, C.dv.inc(
                    q.tensor_scalar_mul(esc[0:1, 0:2], pD[0:1, 20:22], EPS)))
                C.dv.inc(q.tensor_copy(escb[0:1, 0:2], esc[0:1, 0:2]))
            dv_es = C.dv.n

            sq3 = C.sq.n

            @on(blk, 'tensor')
            def _(q, hh=hh, base=base):
                q.wait_ge(C.dv.h, dv_es)
                last = None
                for i in range(NT):
                    q.matmul(pD[:, i:i + 1], qf[base:base + 64, i * 128:(i + 1) * 128],
                             sumb[base:base + 64, 0:1], start=True, stop=False)
                    last = q.matmul(pD[:, i:i + 1], C.c_or[:1, :],
                                    escb[0:1, 0:1], start=False, stop=True)
                C.sq.inc(last)
                last = None
                for i in range(NT):
                    q.matmul(pE[:, i:i + 1], kf[base:base + 64, i * 128:(i + 1) * 128],
                             sumb[base:base + 64, 1:2], start=True, stop=False)
                    last = q.matmul(pE[:, i:i + 1], C.c_or[:1, :],
                                    escb[0:1, 1:2], start=False, stop=True)
                C.sq.inc(last)

            @on(blk, 'vector')
            def _(q, hh=hh):
                q.wait_ge(C.sq.h, sq3 + 1)
                q.wait_ge(C.dv.h, C.dv.inc(
                    q.reciprocal(stat[:, 0:NT], pD[:, 0:NT])))
                q.wait_ge(C.dv.h, C.dv.inc(
                    q.tensor_copy(stb[:, 0:NT], stat[:, 0:NT])))
                q.wait_ge(C.sq.h, sq3 + 2)
                q.wait_ge(C.dv.h, C.dv.inc(
                    q.reciprocal(stat[:, 3 * NT:4 * NT], pE[:, 0:NT])))
                C.dv.inc(q.tensor_copy(stb[:, NT:2 * NT], stat[:, 3 * NT:4 * NT]))
            dv_sio = C.dv.n

            sq4 = C.sq.n

            @on(blk, 'tensor')
            def _(q, hh=hh, base=base):
                q.wait_ge(C.dv.h, dv_sio)
                last = None
                for i in range(NT):
                    last = q.matmul(pD[base:base + 64, 16:17],
                                    kt[:, i * 128 + base:i * 128 + base + 64],
                                    stb[:, NT + i:NT + i + 1],
                                    start=(i == 0), stop=(i == NT - 1))
                C.sq.inc(last)
                last = None
                for i in range(NT):
                    last = q.matmul(pD[base:base + 64, 17:18],
                                    qt[:, i * 128 + base:i * 128 + base + 64],
                                    stb[:, i:i + 1],
                                    start=(i == 0), stop=(i == NT - 1))
                C.sq.inc(last)

            @on(blk, 'vector')
            def _(q, hh=hh, base=base):
                q.wait_ge(C.sq.h, sq4 + 2)
                q.wait_ge(C.dv.h, C.dv.inc(
                    q.tensor_scalar_add(tcol[base:base + 64, 0:2],
                                        pD[base:base + 64, 16:18], EPS)))
                C.dv.inc(q.tensor_copy(tcolb[base:base + 64, 0:2],
                                       tcol[base:base + 64, 0:2]))
            dv_t = C.dv.n
            sq5 = C.sq.n

            @on(blk, 'tensor')
            def _(q, hh=hh, base=base):
                q.wait_ge(C.dv.h, dv_t)
                C.sq.inc(q.matmul(pE[0:1, 20:21], tcolb[base:base + 64, 0:1],
                                  C.c_oc[base:base + 64, 0:1],
                                  start=True, stop=True))
                C.sq.inc(q.matmul(pE[0:1, 21:22], tcolb[base:base + 64, 1:2],
                                  C.c_oc[base:base + 64, 0:1],
                                  start=True, stop=True))

            @on(blk, 'vector')
            def _(q, hh=hh):
                q.wait_ge(C.sq.h, sq5 + 2)
                q.wait_ge(C.dv.h, C.dv.inc(
                    q.tensor_scalar_mul(esc[0:1, 2:4], pE[0:1, 20:22], EPS)))
                C.dv.inc(q.tensor_copy(escb[0:1, 2:4], esc[0:1, 2:4]))
            dv_es2 = C.dv.n

            sq6 = C.sq.n

            @on(blk, 'tensor')
            def _(q, hh=hh, base=base):
                q.wait_ge(C.dv.h, dv_es2)
                last = None
                for i in range(NT):
                    q.matmul(pB[:, i:i + 1], qf[base:base + 64, i * 128:(i + 1) * 128],
                             tcolb[base:base + 64, 0:1], start=True, stop=False)
                    last = q.matmul(pB[:, i:i + 1], C.c_or[:1, :],
                                    escb[0:1, 2:3], start=False, stop=True)
                C.sq.inc(last)
            ac1 = C.ac.n

            @on(blk, 'scalar')
            def _(q, hh=hh):
                q.wait_ge(C.sq.h, sq6 + 1)
                C.ac.inc(q.activation(out=stat[:, NT:2 * NT], in_=pB[:, 0:NT],
                                      func=AF.Exp, scale=C.fv(-1.0)))

            @on(blk, 'vector')
            def _(q, hh=hh):
                q.wait_ge(C.ac.h, ac1 + 1)
                q.wait_ge(C.dv.h, C.dv.inc(
                    q.tensor_scalar_add(stat[:, NT:2 * NT],
                                        stat[:, NT:2 * NT], 1.0)))
                q.wait_ge(C.dv.h, C.dv.inc(
                    q.reciprocal(stat[:, NT:2 * NT], stat[:, NT:2 * NT])))
                C.dv.inc(q.tensor_mul(stat[:, 2 * NT:3 * NT], stat[:, 0:NT],
                                      stat[:, NT:2 * NT]))
            dv_sa = C.dv.n

            sq7 = C.sq.n

            @on(blk, 'tensor')
            def _(q, hh=hh, base=base, dvn=C.dv.n):
                q.wait_ge(C.dv.h, dvn)
                for ch in range(4):
                    if ch >= 2:
                        q.wait_ge(C.dv.h, dvn + ch - 1)
                    last = None
                    q.matmul(pA[ch % 2][0:1, 0:512],
                             tcolb[base:base + 64, 1:2],
                             kf[base:base + 64, ch * 512:(ch + 1) * 512],
                             start=True, stop=False)
                    for r4 in range(4):
                        last = q.matmul(pA[ch % 2][0:1, r4 * 128:(r4 + 1) * 128],
                                        escb[0:1, 3:4], C.c_or[0:1, :],
                                        start=False, stop=(r4 == 3))
                    C.sq.inc(last)

            @on(blk, 'vector')
            def _(q, hh=hh, dvn=C.dv.n):
                for ch in range(4):
                    q.wait_ge(C.sq.h, sq7 + ch + 1)
                    C.dv.inc(q.tensor_copy(csr[0:1, ch * 512:(ch + 1) * 512],
                                           pA[ch % 2][0:1, 0:512]))
                q.wait_ge(C.dv.h, C.dv.inc(
                    q.tensor_scalar(csr[0:1, :], csr[0:1, :], 1.0, -1.0,
                                    ALU.min, ALU.max)))
                C.dv.inc(q.reduce_max(rr[0:1, 0:1], csr[0:1, :], axis=AX.X,
                                      negate=True))
            dv_cs = C.dv.n
            ac2 = C.ac.n

            @on(blk, 'scalar')
            def _(q, hh=hh):
                q.wait_ge(C.dv.h, dv_cs)
                C.ac.inc(q.activation(out=csr[0:1, :], in_=csr[0:1, :],
                                      func=AF.Exp, bias=rr[0:1, 0:1],
                                      accum_out=rr[0:1, 1:2]))

            @on(blk, 'vector')
            def _(q, hh=hh):
                q.wait_ge(C.ac.h, ac2 + 1)
                q.wait_ge(C.dv.h, C.dv.inc(
                    q.reciprocal(rr[0:1, 2:3], rr[0:1, 1:2])))
                C.dv.inc(q.tensor_scalar(csb[0:1, :], csr[0:1, :],
                                         rr[0:1, 2:3], float(T),
                                         ALU.mult, ALU.mult))
            dv_sc = C.dv.n
            sq8 = C.sq.n

            @on(blk, 'tensor')
            def _(q, hh=hh):
                q.wait_ge(C.dv.h, dv_sc)
                last = None
                for i in range(NT):
                    last = q.transpose(pS[:, 2 * i:2 * i + 1],
                                       csb[0:1, i * 128:(i + 1) * 128],
                                       C.c_ident[0:1, 0:1])
                C.sq.inc(last)

            @on(blk, 'vector')
            def _(q, hh=hh):
                q.wait_ge(C.sq.h, sq8 + 1)
                C.dv.inc(q.tensor_copy(
                    sct[:, 0:NT],
                    bass.AP(pS.tensor if hasattr(pS, "tensor") else pS, 0,
                            [[512, 128], [2, NT]]) if False else
                    pS[:, 0:2 * NT].rearrange("p (t two) -> p t two", two=2)[:, :, 0]))
            dv_sct = C.dv.n

            sq9, dv_v0 = C.sq.n, C.dv.n

            @on(blk, 'vector')
            def _(q, hh=hh, base=base):
                q.wait_ge(C.dv.h, dv_sct)
                for i in range(NT):
                    if i >= 2:
                        q.wait_ge(C.sq.h, sq9 + i - 1)
                    C.dv.inc(q.tensor_scalar_mul(
                        vsb[:, (i % 2) * 64:(i % 2) * 64 + 64],
                        vt[:, i * 128 + base:i * 128 + base + 64],
                        sct[:, i:i + 1]))

            @on(blk, 'tensor')
            def _(q, hh=hh, base=base):
                for i in range(NT):
                    q.wait_ge(C.dv.h, dv_v0 + i + 1)
                    C.sq.inc(q.matmul(pE[base:base + 64, 32:96],
                                      kt[:, i * 128 + base:i * 128 + base + 64],
                                      vsb[:, (i % 2) * 64:(i % 2) * 64 + 64],
                                      start=(i == 0), stop=(i == NT - 1)))

            @on(blk, 'vector')
            def _(q, hh=hh, base=base):
                q.wait_ge(C.sq.h, C.sq.n)
                C.dv.inc(q.tensor_copy(kvb[base:base + 64, :],
                                       pE[base:base + 64, 32:96]))
            dv_kv = C.dv.n

            sqA = C.sq.n

            @on(blk, 'tensor')
            def _(q, hh=hh, base=base, dvn=C.dv.n):
                q.wait_ge(C.dv.h, dv_kv)
                for i in range(NT):
                    if i >= 2:
                        q.wait_ge(C.dv.h, dvn + i - 1)
                    C.sq.inc(q.matmul(pA[i % 2][:, 0:64],
                                      qf[base:base + 64, i * 128:(i + 1) * 128],
                                      kvb[base:base + 64, :],
                                      start=True, stop=True))

            @on(blk, 'vector')
            def _(q, hh=hh, h=h):
                for i in range(NT):
                    q.wait_ge(C.sq.h, sqA + i + 1)
                    C.dv.inc(q.tensor_scalar_mul(
                        aot[:, i * 1024 + h * 64:i * 1024 + h * 64 + 64],
                        pA[i % 2][:, 0:64],
                        stat[:, 2 * NT + i:2 * NT + i + 1]))

    C.barrier()
    sqT, svT = C.sq.n, C.sv.n

    @on(blk, 'tensor')
    def _(q):
        for i in range(NT):
            for gg in range(2):
                gidx = 2 * i + gg
                if gidx >= 1:
                    q.wait_ge(C.sv.h, svT + gidx)
                last = None
                for s in range(4):
                    c = gg * 4 + s
                    last = q.transpose(
                        pS[:, s * 128:(s + 1) * 128],
                        aot[:, i * 1024 + c * 128:i * 1024 + (c + 1) * 128],
                        C.c_ident[:])
                C.sq.inc(last)

    @on(blk, 'vector')
    def _(q):
        for i in range(NT):
            for gg in range(2):
                q.wait_ge(C.sq.h, sqT + 2 * i + gg + 1)
                C.sv.inc(q.tensor_copy(
                    aof[:].rearrange("p (c t) -> p c t", c=NC)
                    [:, gg * 4:(gg + 1) * 4, i * 128:(i + 1) * 128],
                    pS[:].rearrange("p (s t) -> p s t", s=4)))

    C.barrier()
    es2.close()


def emit_st_attention(C, layer, aof, npairs=H // 2, substage=None):
    nc, blk = C.nc, C.blk
    es2 = contextlib.ExitStack()
    wpair = es2.enter_context(nc.sbuf_tensor("wpair", [128, NC * 3 * 128], BF16))
    qp = es2.enter_context(nc.sbuf_tensor("qpair", [128, T], BF16))
    kp = es2.enter_context(nc.sbuf_tensor("kpair", [128, T], BF16))
    vf = es2.enter_context(nc.sbuf_tensor("vfeat", [128, T], BF16))
    va = [es2.enter_context(nc.sbuf_tensor(f"vaug{j}", [128, NT * 80], BF16))
          for j in range(2)]
    ptb = [es2.enter_context(nc.sbuf_tensor(f"ptb{j}", [128, 1024], BF16))
           for j in range(2)]
    scrb = es2.enter_context(nc.sbuf_tensor("scrb", [128, T], BF16))
    nrow = es2.enter_context(nc.sbuf_tensor("nrow", [1, T], F32))
    nbrow = es2.enter_context(nc.sbuf_tensor("nbrow", [1, T], BF16))
    zrow = es2.enter_context(nc.sbuf_tensor("zrow", [65, 1024], F32))
    zbsb = es2.enter_context(nc.sbuf_tensor("zbsb", [64, 512], F32))
    mvsb = es2.enter_context(nc.sbuf_tensor("mvsb", [64, 512], BF16))
    kst = es2.enter_context(nc.sbuf_tensor("kst", [1, 8], F32))
    kcol = es2.enter_context(nc.sbuf_tensor("kcol", [128, 1], BF16))
    bcol = es2.enter_context(nc.sbuf_tensor("bcol", [128, 4], F32))
    ps = [es2.enter_context(nc.psum_tensor(f"psc{j}", [128, 1024], F32))
          for j in range(2)]
    ppv = es2.enter_context(nc.psum_tensor("ppv", [65, 1024], F32))
    pzb = es2.enter_context(nc.psum_tensor("pzb", [64, 512], F32))
    ptv = es2.enter_context(nc.psum_tensor("ptv", [128, 512], BF16))

    QCH = 1024
    NQC = T // QCH
    scale = 0.125

    if layer == 1:
        bdall = es2.enter_context(nc.sbuf_tensor("bdall", [128, 16 * 128], BF16))
        hkb = es2.enter_context(nc.sbuf_tensor("hkb", [128, 16 * 128], BF16))
        tblf = es2.enter_context(nc.sbuf_tensor("tblf", [64, 1024], F32))
        tcolb = es2.enter_context(nc.sbuf_tensor("tcolb", [64, 1], BF16))

        @on(blk, 'vector')
        def _(q):
            C.dv.inc(q.memset(tblf[:], 0.0))
        dv_ms = C.dv.n

        @on(blk, 'sync')
        def _(q):
            q.wait_ge(C.dv.h, dv_ms)
            th = C.win["l1_tableT"].tensor
            C.xq.inc(q.dma_start(out=tblf[0:63, :],
                                 in_=bass.AP(th, 0, [[1024, 63], [1, 1024]])), 16)
            C.xq.inc(q.dma_start(out=tblf[63:64, 0:1008],
                                 in_=bass.AP(th, 63 * 1024, [[1008, 1], [1, 1008]])), 16)
        xq_t = C.xq.n

        @on(blk, 'vector')
        def _(q):
            q.wait_ge(C.xq.h, xq_t)
            C.dv.inc(q.reduce_max(tcolb[:], tblf[:], axis=AX.X,
                                  apply_absolute_value=True))
        dv_t = C.dv.n

        @on(blk, 'tensor')
        def _(q):
            q.wait_ge(C.dv.h, dv_t)
            C.pe.inc(q.transpose(ptv[0:1, 0:64], tcolb[:], C.c_ident[0:64, 0:64]))
        pe_t = C.pe.n

        @on(blk, 'vector')
        def _(q):
            q.wait_ge(C.pe.h, pe_t)
            q.wait_ge(C.dv.h, C.dv.inc(
                q.reduce_max(kst[0:1, 4:5], ptv[0:1, 0:64], axis=AX.X)))
            C.dv.inc(q.tensor_scalar_add(kst[0:1, 4:5], kst[0:1, 4:5], 1.0))
    else:
        @on(blk, 'vector')
        def _(q):
            C.dv.inc(q.memset(kst[0:1, 4:5], 1.0))

    @on(blk, 'vector')
    def _(q):
        q.memset(kst[:], 0.0)
        q.memset(nbrow[:], 0.0)
        q.memset(nrow[:], 0.0)
        for j in range(2):
            q.memset(va[j][:], 0.0)
            C.dv.inc(q.memset(
                va[j][:].rearrange("p (n e) -> p n e", e=80)[:, :, 64:65], 1.0))

    C.barrier()

    wsl = {0: ("l0_wqkvT",) * 3, 1: ("l1_WQT", "l1_WKT", "l1_WVT"),
           2: ("l2_WqT", "l2_WkT", "l2_WvT")}[layer]
    woff = (0, 1024, 2048) if layer == 0 else (0, 0, 0)

    def kranges(qc):
        if layer == 0:
            return list(range(0, min(NT, qc * 8 + 10)))
        if layer == 1:
            return list(range(0, qc * 8 + 8))
        return list(range(NT))

    for p in range(npairs):
        wv3 = wpair[:].rearrange("p (c j o) -> p c j o", c=NC, j=3)
        pe_prev, sv_prev, se_prev = C.pe.n, C.sv.n, C.se.n

        @on(blk, 'gpsimd')
        def _(q, p=p):
            if p >= 1:
                q.wait_ge(C.sq.h, C.sq.n)
            for j in range(3):
                C.cast_load(q, wv3[:, :, j, :],
                            C.win[wsl[j]][:, woff[j] + p * 128:woff[j] + (p + 1) * 128]
                            .rearrange("(c q) o -> q c o", q=128))

        if layer in (0, 1):
            @on(blk, 'sync')
            def _(q, p=p, acn=C.ac.n):
                if p >= 1:
                    q.wait_ge(C.ac.h, acn)
                if layer == 0:
                    b = C.win["l0_bqkv"]
                    srcs = [b[p * 128:(p + 1) * 128],
                            b[1024 + p * 128:1024 + (p + 1) * 128],
                            b[2048 + p * 128:2048 + (p + 1) * 128]]
                else:
                    srcs = [C.win[nm][p * 128:(p + 1) * 128]
                            for nm in ("l1_bQ", "l1_bK", "l1_bV")]
                for j in range(3):
                    C.xq.inc(q.dma_start(out=bcol[:, j:j + 1],
                                         in_=srcs[j].unsqueeze(1)), 16)

            @on(blk, 'vector')
            def _(q):
                q.wait_ge(C.xq.h, C.xq.n)
                C.dv.inc(q.tensor_scalar_mul(bcol[:, 0:1], bcol[:, 0:1], scale))

        if substage == "load":
            continue
        wq_p, dv_p, xq_p = C.wq.n, C.dv.n, C.xq.n
        sq0, ac0 = C.sq.n, C.ac.n

        NU = 6 if not (substage or "").startswith("proj") or substage == "proj" \
            else int(substage[4:])
        @on(blk, 'tensor')
        def _(q, p=p):
            q.wait_ge(C.wq.h, wq_p)
            q.wait_ge(C.dv.h, dv_p)
            if p >= 1:
                q.wait_ge(C.se.h, se_prev)
                q.wait_ge(C.sv.h, sv_prev)
            for u in range(NU):
                j, hv = u // 2, u % 2
                if u >= 2:
                    q.wait_ge(C.ac.h, ac0 + u - 1)
                dst = ps[u % 2]
                for c in range(NC):
                    q.matmul(dst[:, 0:512], wv3[:, c, j, :],
                             C.xnf3()[:, c, hv * 1024:hv * 1024 + 512],
                             start=(c == 0), stop=(c == NC - 1))
                for c in range(NC):
                    last = q.matmul(dst[:, 512:1024], wv3[:, c, j, :],
                                    C.xnf3()[:, c, hv * 1024 + 512:hv * 1024 + 1024],
                                    start=(c == 0), stop=(c == NC - 1))
                C.sq.inc(last)

        @on(blk, 'scalar')
        def _(q, p=p):
            for u in range(NU):
                j, hv = u // 2, u % 2
                q.wait_ge(C.sq.h, sq0 + u + 1)
                tgt = (qp, kp, vf)[j]
                kw = {}
                if j == 0:
                    kw["scale"] = C.fv(0.125)
                if layer in (0, 1):
                    kw["bias"] = bcol[:, j:j + 1]
                C.ac.inc(q.activation(out=tgt[:, hv * 1024:(hv + 1) * 1024],
                                      in_=ps[u % 2][:], func=AF.Identity, **kw))

        if substage == "proj":
            continue
        sq1, sv1 = C.sq.n, C.sv.n

        @on(blk, 'tensor')
        def _(q, p=p):
            q.wait_ge(C.ac.h, ac0 + 6)
            for grp in range(4):
                if grp >= 1:
                    q.wait_ge(C.sv.h, sv1 + grp)
                last = None
                for t4 in range(4):
                    i = grp * 4 + t4
                    last = q.transpose(ptv[:, t4 * 128:(t4 + 1) * 128],
                                       vf[:, i * 128:(i + 1) * 128], C.c_ident[:])
                C.sq.inc(last)

        @on(blk, 'vector')
        def _(q, p=p):
            for grp in range(4):
                q.wait_ge(C.sq.h, sq1 + grp + 1)
                p3 = ptv[:].rearrange("p (n e) -> p n e", e=128)
                q.tensor_copy(
                    va[0][:].rearrange("p (n e) -> p n e", e=80)
                    [:, grp * 4:(grp + 1) * 4, 0:64], p3[:, :, 0:64])
                C.sv.inc(q.tensor_copy(
                    va[1][:].rearrange("p (n e) -> p n e", e=80)
                    [:, grp * 4:(grp + 1) * 4, 0:64], p3[:, :, 64:128]))
        sv_after_v = C.sv.n

        if substage == "vtr":
            continue
        if layer in (0, 1):
            ac1 = C.ac.n

            @on(blk, 'scalar')
            def _(q, p=p):
                if p >= 1:
                    q.wait_ge(C.sq.h, C.sq.n)
                C.ac.inc(q.activation(out=scrb[:], in_=qp[:], func=AF.Abs))
            dv1 = C.dv.n

            @on(blk, 'vector')
            def _(q, p=p):
                q.wait_ge(C.ac.h, ac1)
                C.dv.inc(q.reduce_max(kcol[:], kp[:], axis=AX.X,
                                      apply_absolute_value=True))
            sq2 = C.sq.n

            @on(blk, 'tensor')
            def _(q, p=p):
                q.wait_ge(C.dv.h, dv1 + 1)
                q.wait_ge(C.sv.h, sv_after_v)
                C.sq.inc(q.transpose(ptv[0:1, 0:128], kcol[:], C.c_ident[:]))

            @on(blk, 'vector')
            def _(q, p=p):
                q.wait_ge(C.sq.h, sq2 + 1)
                for hh in range(2):
                    q.wait_ge(C.dv.h, C.dv.inc(
                        q.reduce_max(kst[0:1, hh:hh + 1],
                                     ptv[0:1, hh * 64:(hh + 1) * 64], axis=AX.X)))
                    C.dv.inc(q.tensor_scalar_mul(kst[0:1, hh:hh + 1],
                                                 kst[0:1, hh:hh + 1], -1.02))

        if substage == "bound":
            continue
        for hh in range(2):
            h = 2 * p + hh
            base = hh * 64

            if layer == 1:
                wqh = C.wq.n

                @on(blk, 'gpsimd')
                def _(q, h=h):
                    if True:
                        q.wait_ge(C.sq.h, C.sq.n)
                    src = bass.AP(C.win["l1_tableT"].tensor, h * 4095 + 1920,
                                  [[1, 128], [128, 16], [1, 128]])
                    C.cast_load(q, hkb[:].rearrange("r (d c) -> r d c", d=16), src)
                sq3, sv3 = C.sq.n, C.sv.n

                @on(blk, 'tensor')
                def _(q, h=h):
                    q.wait_ge(C.wq.h, C.wq.n)
                    q.wait_ge(C.dv.h, C.dv.n)
                    q.wait_ge(C.sv.h, sv3)
                    for g in range(4):
                        if g >= 1:
                            q.wait_ge(C.sv.h, sv3 + g)
                        last = None
                        for s in range(4):
                            d = g * 4 + s
                            last = q.transpose(ptv[:, s * 128:(s + 1) * 128],
                                               hkb[:, d * 128:(d + 1) * 128],
                                               C.c_jex[:])
                        C.sq.inc(last)

                @on(blk, 'vector')
                def _(q, h=h):
                    for g in range(4):
                        q.wait_ge(C.sq.h, sq3 + g + 1)
                        C.sv.inc(q.tensor_copy(
                            bdall[:, g * 512:(g + 1) * 512], ptv[:]))

            if layer in (0, 1):
                sq4, sv4 = C.sq.n, C.sv.n

                @on(blk, 'tensor')
                def _(q, base=base):
                    q.wait_ge(C.ac.h, C.ac.n)
                    for qc4 in range(4):
                        if qc4 >= 1:
                            q.wait_ge(C.sv.h, sv4 + qc4)
                        C.sq.inc(q.matmul(pzb[0:1, 0:512],
                                          C.c_oc[base:base + 64, 0:1],
                                          scrb[base:base + 64,
                                               qc4 * 512:(qc4 + 1) * 512],
                                          start=True, stop=True))

                @on(blk, 'vector')
                def _(q, base=base, hh=hh, sqn=C.sq.n):
                    for qc4 in range(4):
                        q.wait_ge(C.sq.h, sq4 + qc4 + 1)
                        C.sv.inc(q.tensor_copy(nrow[0:1, qc4 * 512:(qc4 + 1) * 512],
                                               pzb[0:1, 0:512]))
                    q.wait_ge(C.sq.h, sqn)
                    q.wait_ge(C.sv.h, C.sv.n)
                    q.wait_ge(C.dv.h, C.dv.n)
                    C.dv.inc(q.tensor_scalar(nbrow[0:1, :], nrow[0:1, :],
                                             kst[0:1, hh:hh + 1],
                                             kst[0:1, 4:5],
                                             ALU.mult, ALU.subtract))

            for qc in range(NQC):
                ks = kranges(qc)
                s0, e0, v0 = C.sq.n, C.se.n, C.sv.n
                dvq = C.dv.n

                @on(blk, 'tensor')
                def _(q, h=h, base=base, qc=qc, ks=ks, s0=s0, e0=e0, v0=v0):
                    nk = len(ks)

                    def emit_pv(idx):
                        tjp = ks[idx]
                        last = None
                        for half in range(2):
                            last = q.matmul(
                                ppv[:, half * 512:(half + 1) * 512],
                                va[hh][:, tjp * 80:tjp * 80 + 65],
                                ptb[idx % 2][:, half * 512:(half + 1) * 512],
                                start=(idx == 0), stop=(idx == nk - 1))
                        C.sv.inc(last)

                    for idx, tj in enumerate(ks):
                        if idx >= 2:
                            q.wait_ge(C.se.h, e0 + idx - 1)
                        elif idx == 0:
                            q.wait_ge(C.se.h, e0)
                            q.wait_ge(C.dv.h, dvq)
                            q.wait_ge(C.sv.h, v0)
                        dst = ps[idx % 2]
                        mms = []
                        for half in range(2):
                            mms.append((dst[:, half * 512:(half + 1) * 512],
                                        kp[base:base + 64, tj * 128:(tj + 1) * 128],
                                        qp[base:base + 64,
                                           qc * QCH + half * 512:
                                           qc * QCH + (half + 1) * 512]))
                        if layer in (0, 1):
                            for half in range(2):
                                mms.append((dst[:, half * 512:(half + 1) * 512],
                                            C.c_or[:1, :],
                                            nbrow[0:1, qc * QCH + half * 512:
                                                  qc * QCH + (half + 1) * 512]))
                        if layer == 0:
                            pw = (tj - 2) * 128 - qc * QCH
                        elif layer == 1:
                            pw = tj * 128 - qc * QCH
                        else:
                            pw = -1
                        if layer == 1:
                            lo = max(qc * 8, tj)
                            hi = qc * 8 + 8
                            for ti in range(lo, hi):
                                d = ti - tj
                                off = (ti - qc * 8) * 128
                                mms.append((dst[:, off:off + 128],
                                            bdall[:, d * 128:(d + 1) * 128],
                                            C.c_ident[:]))
                        if layer in (0, 1) and pw > 0:
                            wv = min(pw, QCH)
                            off = 0
                            while off < wv:
                                ws = min(512, wv - off)
                                mms.append((dst[:, off:off + ws],
                                            C.c_or[:1, :], C.c_n9[0:1, 0:ws]))
                                off += ws
                        if layer in (0, 1) and 0 <= pw < QCH:
                            mms.append((dst[:, pw:pw + 128],
                                        C.c_ident[:], C.c_tmask[:]))
                        bank_ids = [int(o_.offset) // 512 for (o_, l_, r_) in mms]
                        first_of, last_of = {}, {}
                        for mi, b in enumerate(bank_ids):
                            first_of.setdefault(b, mi)
                            last_of[b] = mi
                        for mi, (o_, l_, r_) in enumerate(mms):
                            b = bank_ids[mi]
                            last = q.matmul(o_, l_, r_,
                                            start=(first_of[b] == mi),
                                            stop=(last_of[b] == mi))
                        C.sq.inc(last)
                        if idx >= 1:
                            q.wait_ge(C.se.h, e0 + idx)
                            emit_pv(idx - 1)
                    q.wait_ge(C.se.h, e0 + nk)
                    emit_pv(nk - 1)

                @on(blk, 'scalar')
                def _(q, qc=qc, ks=ks, s0=s0, e0=e0, v0=v0):
                    func = AF.Sigmoid if layer == 2 else AF.Exp
                    for idx in range(len(ks)):
                        q.wait_ge(C.sq.h, s0 + idx + 1)
                        if idx >= 2:
                            q.wait_ge(C.sv.h, v0 + idx - 1)
                        elif idx == 0:
                            q.wait_ge(C.sv.h, v0)
                        C.se.inc(q.activation(out=ptb[idx % 2][:],
                                              in_=ps[idx % 2][:], func=func))

                sq5 = C.sq.n

                @on(blk, 'vector')
                def _(q, qc=qc, ks=ks, v0=v0):
                    q.wait_ge(C.sv.h, v0 + len(ks))
                    if layer == 2:
                        q.wait_ge(C.dv.h, C.dv.inc(
                            q.tensor_scalar_add(zrow[64:65, 0:QCH],
                                                ppv[64:65, 0:QCH], 1e-6)))
                        C.dv.inc(q.reciprocal(zrow[64:65, 0:QCH],
                                              zrow[64:65, 0:QCH]))
                    else:
                        C.dv.inc(q.reciprocal(zrow[64:65, 0:QCH],
                                              ppv[64:65, 0:QCH]))
                dv2 = C.dv.n

                for half in range(2):
                    sq6 = C.sq.n

                    @on(blk, 'tensor')
                    def _(q, half=half, dv2=dv2):
                        q.wait_ge(C.dv.h, dv2)
                        if half == 1:
                            q.wait_ge(C.dv.h, C.dv.n)
                        C.sq.inc(q.matmul(
                            pzb[:, 0:512], C.c_o64[64:65, 0:64],
                            zrow[64:65, half * 512:half * 512 + 512],
                            start=True, stop=True))

                    @on(blk, 'vector')
                    def _(q, h=h, hh=hh, p=p, qc=qc, half=half, sq6=sq6):
                        q.wait_ge(C.sq.h, sq6 + 1)
                        q.wait_ge(C.dv.h, C.dv.inc(
                            q.tensor_copy(zbsb[:], pzb[:, 0:512])))
                        col = p * T + qc * QCH + half * 512
                        if hh == 0:
                            C.dv.inc(q.tensor_mul(
                                aof[0:64, col:col + 512],
                                ppv[0:64, half * 512:(half + 1) * 512], zbsb[:]))
                        else:
                            C.dv.inc(q.tensor_mul(
                                mvsb[:],
                                ppv[0:64, half * 512:(half + 1) * 512], zbsb[:]))
                    if hh == 1:
                        dv3, sq7 = C.dv.n, C.sq.n

                        @on(blk, 'tensor')
                        def _(q, dv3=dv3):
                            q.wait_ge(C.dv.h, dv3)
                            C.sq.inc(q.matmul(ps[0][64:128, 0:512],
                                              C.c_ident[0:64, 0:64], mvsb[:],
                                              start=True, stop=True))

                        @on(blk, 'vector')
                        def _(q, p=p, qc=qc, half=half, sq7=sq7):
                            q.wait_ge(C.sq.h, sq7 + 1)
                            col = p * T + qc * QCH + half * 512
                            C.dv.inc(q.tensor_copy(aof[64:128, col:col + 512],
                                                   ps[0][64:128, 0:512]))
                dv4 = C.dv.n

                @on(blk, 'tensor')
                def _(q, dv4=dv4):
                    q.wait_ge(C.dv.h, dv4)

    C.barrier()
    C._attn_dbg = dict(qp=qp, kp=kp, vf=vf, va=va, nbrow=nbrow, nrow=nrow,
                       kst=kst, scrb=scrb, ptb=ptb)
    if getattr(C, "_keep_attn_scope", False):
        C._attn_es = es2
    else:
        es2.close()


def emit_combine(C, aof, wkey, bkey, g_dram=None):
    nc, blk = C.nc, C.blk
    es2 = contextlib.ExitStack()
    wout = es2.enter_context(nc.sbuf_tensor("wout", [128, NC * 1024], BF16))
    r_bo = es2.enter_context(nc.sbuf_tensor("r_bo", [1, 1024], BF16))
    xrb = [es2.enter_context(nc.sbuf_tensor(f"xrb{j}", [128, 1024], F32))
           for j in range(2)]
    ysb = [es2.enter_context(nc.sbuf_tensor(f"ysb{j}", [128, 1024], F32))
           for j in range(2)]
    grb = [es2.enter_context(nc.sbuf_tensor(f"grb{j}", [128, 1024], F32))
           for j in range(2)] if g_dram is not None else None
    tmp = es2.enter_context(nc.sbuf_tensor("cmb_tmp", [128, 1024], F32))
    pcmb = [es2.enter_context(nc.psum_tensor(f"pcmb{j}", [128, 512], F32))
            for j in range(4)]

    @on(blk, 'gpsimd')
    def _(q):
        C.cast_load(q, wout[:].rearrange("p (c o) -> p c o", c=NC),
                    C.win[wkey].rearrange("(c p) o -> p c o", p=128))
        if bkey is not None:
            C.cast_load(q, r_bo[:], C.win[bkey].unsqueeze(0))
    wq_c = C.wq.n

    xq0, pe0, dv0, yq0 = C.xq.n, C.pe.n, C.dv.n, C.yq.n
    aof3 = aof[:].rearrange("p (c t) -> p c t", c=NC)

    @on(blk, 'sync')
    def _(q):
        for i in range(NT):
            if i >= 1:
                q.wait_ge(C.dv.h, dv0 + 2 * (i - 1) + 2)
                C.yq.inc(q.dma_start(out=C.y[(i - 1) * 128:i * 128, :],
                                     in_=ysb[(i - 1) % 2][:]), 16)
            C.xq.inc(q.dma_start(out=xrb[i % 2][:],
                                 in_=C.x[i * 128:(i + 1) * 128, :]), 16)
            if g_dram is not None:
                C.xq.inc(q.dma_start(out=grb[i % 2][:],
                                     in_=g_dram[i * 128:(i + 1) * 128, :]), 16)
        q.wait_ge(C.dv.h, dv0 + 2 * (NT - 1) + 2)
        C.yq.inc(q.dma_start(out=C.y[(NT - 1) * 128:NT * 128, :],
                             in_=ysb[(NT - 1) % 2][:]), 16)
        q.wait_ge(C.yq.h, C.yq.n)

    @on(blk, 'tensor')
    def _(q):
        q.wait_ge(C.wq.h, wq_c)
        for i in range(NT):
            if i >= 2:
                q.wait_ge(C.dv.h, dv0 + 2 * (i - 1))
            last = None
            for oc in range(2):
                dst = pcmb[2 * (i % 2) + oc]
                for c in range(NC):
                    last = q.matmul(dst[:], aof3[:, c, i * 128:(i + 1) * 128],
                                    wout[:, c * 1024 + oc * 512:c * 1024 + (oc + 1) * 512],
                                    start=(c == 0), stop=(bkey is None and c == NC - 1))
                if bkey is not None:
                    last = q.matmul(dst[:], C.c_or[:1, :],
                                    r_bo[:1, oc * 512:(oc + 1) * 512],
                                    start=False, stop=True)
            C.pe.inc(last)

    @on(blk, 'vector')
    def _(q):
        NL = 16 if g_dram is None else 32
        for i in range(NT):
            q.wait_ge(C.pe.h, pe0 + i + 1)
            q.wait_ge(C.xq.h, xq0 + NL * (i + 1))
            if i >= 2:
                q.wait_ge(C.yq.h, yq0 + 16 * (i - 1))
            for oc in range(2):
                gsl = (C.g[:, i * 1024 + oc * 512:i * 1024 + (oc + 1) * 512]
                       if g_dram is None else
                       grb[i % 2][:, oc * 512:(oc + 1) * 512])
                q.tensor_mul(tmp[:, oc * 512:(oc + 1) * 512],
                             pcmb[2 * (i % 2) + oc][:], gsl)
            C.dv.inc(q.tensor_add(ysb[i % 2][:], tmp[:], xrb[i % 2][:]), 2)

    es2.close()


def emit_debug_dumps(C, aof):
    nc, blk = C.nc, C.blk
    dxnf = nc.dram_tensor("dbg_xnf", [128, NC * T], BF16, kind="ExternalOutput").ap()
    dg = nc.dram_tensor("dbg_g", [128, NT * D], F32, kind="ExternalOutput").ap()
    daof = nc.dram_tensor("dbg_aof", [128, NC * T], BF16, kind="ExternalOutput").ap()
    C.barrier()

    A = getattr(C, "_attn_dbg", None)
    dqp = nc.dram_tensor("dbg_qp", [128, T], BF16, kind="ExternalOutput").ap()
    dkp = nc.dram_tensor("dbg_kp", [128, T], BF16, kind="ExternalOutput").ap()
    dnb = nc.dram_tensor("dbg_nb", [1, T], BF16, kind="ExternalOutput").ap()
    dva0 = nc.dram_tensor("dbg_va0", [128, NT * 80], BF16, kind="ExternalOutput").ap()
    dkst = nc.dram_tensor("dbg_kst", [1, 5], F32, kind="ExternalOutput").ap()

    @on(blk, 'sync')
    def _(q):
        C.yq.inc(q.dma_start(out=dxnf, in_=C.xnf[:]), 16)
        if C.g is not None:
            C.yq.inc(q.dma_start(out=dg, in_=C.g[:]), 16)
        C.yq.inc(q.dma_start(out=daof, in_=aof[:]), 16)
        if A is not None:
            C.yq.inc(q.dma_start(out=dqp, in_=A["qp"][:]), 16)
            C.yq.inc(q.dma_start(out=dkp, in_=A["kp"][:]), 16)
            C.yq.inc(q.dma_start(out=dnb, in_=A["nbrow"][:]), 16)
            C.yq.inc(q.dma_start(out=dva0, in_=A["va"][0][:]), 16)
            C.yq.inc(q.dma_start(out=dkst, in_=A["kst"][0:1, 0:5]), 16)
        q.wait_ge(C.yq.h, C.yq.n)


def emit_trivial_y(C):
    nc, blk = C.nc, C.blk
    es2 = contextlib.ExitStack()
    tb = es2.enter_context(nc.sbuf_tensor("triv", [128, 1024], F32))
    C.barrier()

    @on(blk, 'sync')
    def _(q):
        for i in range(NT):
            C.xq.inc(q.dma_start(out=tb[:], in_=C.x[i * 128:(i + 1) * 128, :]), 16)
            q.wait_ge(C.xq.h, C.xq.n)
            C.yq.inc(q.dma_start(out=C.y[i * 128:(i + 1) * 128, :], in_=tb[:]), 16)
            q.wait_ge(C.yq.h, C.yq.n)
    es2.close()


def body_layer0(C):
    esl = contextlib.ExitStack()
    C.ensure_consts(CONST_NEED[0], esl)
    C.g = esl.enter_context(C.nc.sbuf_tensor("g0", [128, NT * D], F32))
    with C.nc.named_scope("l0_ln"):
        emit_ln_transpose(C)
    with C.nc.named_scope("l0_ffn"):
        emit_l0_ffn(C)
    aof = esl.enter_context(C.nc.sbuf_tensor("aof0", [128, NC * T], BF16))
    with C.nc.named_scope("l0_attn"):
        emit_st_attention(C, 0, aof)
    with C.nc.named_scope("l0_comb"):
        emit_combine(C, aof, "l0_woT", "l0_bo")
        C.barrier()
    esl.close()


def build_layer0(W, dbg=False, stop_after=None, npairs=H // 2, substage=None):
    C = LayerProg(W, LAYER_KEYS[0], no_g=True, need=CONST_NEED[0])
    C._keep_attn_scope = dbg
    C.emit_const_loads()
    body_layer0(C)
    C.close()
    return C


def body_layer2(C):
    esl = contextlib.ExitStack()
    C.ensure_consts(CONST_NEED[2], esl)
    C.g = None
    g_dram = C.nc.dram_tensor("g_scratch", [T, D], F32).ap()
    with C.nc.named_scope("l2_ln"):
        emit_ln_transpose(C)
    with C.nc.named_scope("l2_ffn"):
        emit_l2_ffn(C, g_dram)
    aof = esl.enter_context(C.nc.sbuf_tensor("aof2", [128, NC * T], BF16))
    with C.nc.named_scope("l2_attn"):
        emit_st_attention(C, 2, aof)
    with C.nc.named_scope("l2_comb"):
        emit_combine(C, aof, "l2_WoT", None, g_dram=g_dram)
        C.barrier()
    esl.close()


def build_layer2(W, dbg=False, stop_after=None, npairs=H // 2, substage=None):
    C = LayerProg(W, LAYER_KEYS[2], no_g=True, need=CONST_NEED[2])
    C._keep_attn_scope = dbg
    C.emit_const_loads()
    body_layer2(C)
    C.close()
    return C


def body_layer3(C):
    esl = contextlib.ExitStack()
    C.ensure_consts(CONST_NEED[3], esl)
    C.g = None
    msktm = esl.enter_context(C.nc.sbuf_tensor("l3msk", [128, NT * 16], BF16))
    scol = esl.enter_context(C.nc.sbuf_tensor("l3scol", [128, NT], F32))
    rtm = esl.enter_context(C.nc.sbuf_tensor("l3rtm", [128, NT * 16], F32))
    es_lo = contextlib.ExitStack()
    xnlo = es_lo.enter_context(C.nc.sbuf_tensor("xnlo", [128, NC * T], BF16))
    with C.nc.named_scope("l3_ln"):
        emit_ln_transpose(C, lo_dst=xnlo)
    with C.nc.named_scope("l3_ffn"):
        emit_l3_ffn(C, xnlo=xnlo, es_lo=es_lo, g_es=esl,
                    msktm=msktm, scol=scol, rtm=rtm)
    aof = esl.enter_context(C.nc.sbuf_tensor("aof3", [128, NC * T], BF16))
    with C.nc.named_scope("l3_attn"):
        emit_l3_attn(C, aof)
    with C.nc.named_scope("l3_comb"):
        emit_combine(C, aof, "l3_Wo", "l3_bo")
        C.barrier()
    esl.close()


def build_layer3(W, dbg=False, stop_after=None, npairs=H // 2, substage=None):
    C = LayerProg(W, LAYER_KEYS[3], no_g=True, need=CONST_NEED[3])
    C._keep_attn_scope = False
    C.emit_const_loads()
    body_layer3(C)
    C.close()
    return C


def build_layer(l, W, dbg=False, stop_after=None, npairs=H // 2, substage=None):
    if l == 0:
        return build_layer0(W, dbg, stop_after, npairs, substage)
    if l == 1:
        return build_layer1(W, dbg, stop_after, npairs, substage)
    if l == 2:
        return build_layer2(W, dbg, stop_after, npairs, substage)
    if l == 3:
        return build_layer3(W, dbg, stop_after, npairs, substage)
    raise NotImplementedError(f"layer {l}")


BODIES = {0: body_layer0, 1: body_layer1, 2: body_layer2, 3: body_layer3}


def build_fused(W):
    keys = sum((LAYER_KEYS[l] for l in range(4)), [])
    C = LayerProg(W, keys, no_g=True, need=[])
    C._keep_attn_scope = False
    yout = C.y
    scr = [C.nc.dram_tensor(f"xs{l}", [T, D], F32).ap() for l in (1, 2, 3)]
    for l in range(4):
        if l > 0:
            C.x = scr[l - 1]
        C.y = scr[l] if l < 3 else yout
        BODIES[l](C)
    C.close()
    return C


def kernel(**inputs):
    W, xs = host_prep(inputs)
    C = build_fused(W)
    shared = {k: W[k] for l in range(4) for k in LAYER_KEYS[l]}
    shared.update({c[1]: W[c[1]] for c in CONSTS})
    in_maps = [dict(shared, x=xs[i]) for i in range(8)]
    res = run_bass_kernel_spmd(C.nc, in_maps, core_ids=list(range(8)))
    cur = [np.asarray(r["y"], dtype=np.float32) for r in res.results]
    return np.stack(cur, axis=0)
